# version 08
# baseline (speedup 1.0000x reference)
import numpy as np
from contextlib import ExitStack
import concourse.bass as bass
import concourse.mybir as mybir
from concourse.bass_utils import run_bass_kernel_spmd

F32 = mybir.dt.float32
BF16 = mybir.dt.bfloat16
U8 = mybir.dt.uint8
AF = mybir.ActivationFunctionType
ALU = mybir.AluOpType
AX = mybir.AxisListType

H = 4096; HC = 32
S = 1024; SCN = 8
FF = 9984; FCN = 78
G = 6
BOT = 64; RD = 48; VB = 256; UCN = 8
NSLOT = 1024
TCORE = 2048; TILE = 512; NT = 4; NSTT = 4
NSTG = 16
RSQ48 = 0.14433756729740643
RSQ64 = 0.125
NEGBIG = -30000.0
GELU_C = 0.7978845608028654
GELU_A = 0.044715
KB = 1024
NCORES = 8


def _blk_lhsT(w, oc=128):
    O, I = w.shape
    nO, nI = O // oc, I // 128
    b = w.reshape(nO, oc, nI, 128).transpose(0, 3, 2, 1)
    return np.ascontiguousarray(b).reshape(nO, 128, nI * oc)


class Sem:
    __slots__ = ("h", "n")

    def __init__(self, ctx, nc, name):
        self.h = ctx.enter_context(nc.semaphore(name))
        self.n = 0

    def inc(self, instr, k=1):
        instr.then_inc(self.h, k)
        self.n += k
        return self.n


class PsumStack:
    def __init__(self, nc):
        self.nc = nc
        self.stack = []
        self.k = 0

    def enter(self, shape, dt):
        self.k += 1
        cm = self.nc.psum_tensor(f"ps{self.k}", shape, dt)
        t = cm.__enter__()
        self.stack.append(cm)
        return t

    def exit(self, k=1):
        for _ in range(k):
            self.stack.pop().__exit__(None, None, None)


def build_program(upto=3, debug_xs=False):
    ctx = ExitStack()
    nc = bass.Bass("TRN2", target_bir_lowering=False)
    d = {}

    def din(name, shape, dt=F32):
        d[name] = nc.dram_tensor(name, list(shape), dt, kind="ExternalInput").ap()

    din("x0", (TCORE, H))
    din("gt", (FCN, 128, H)); din("ut", (FCN, 128, H)); din("dtw", (FCN, 128, H))
    din("upw", (SCN, 128, H)); din("gw", (SCN, 128, H))
    din("lao", (SCN, 128, H)); din("fuse", (G, SCN, 128, H))
    din("wq", (128, HC * 64)); din("wu1", (UCN, 128, H))
    din("wvd", (2, 128, H)); din("wvu", (2, 128, H)); din("auxv", (SCN, 128, H))
    din("akt", (BOT, NSLOT)); din("wrt", (BOT, RD))
    din("lp", (128, SCN)); din("bu1", (128, UCN)); din("wu2", (128, UCN))
    din("df", (128, G)); din("scal", (128, 3)); din("logrel", (128, NSLOT))

    okind = "ExternalOutput" if debug_xs else "Internal"
    d["x1"] = nc.dram_tensor("x1", [TCORE, H], F32, kind=okind).ap()
    d["x2"] = nc.dram_tensor("x2", [TCORE, H], F32, kind=okind).ap()
    d["y"] = nc.dram_tensor("y", [TCORE, H], F32, kind="ExternalOutput").ap()
    ymix = [nc.dram_tensor(f"ymix{i}", [TILE, H], F32, kind="Internal").ap()
            for i in range(2)]
    CW = {}
    for name, shape in [("gt", (FCN, 128, H)), ("ut", (FCN, 128, H)),
                        ("dtw", (FCN, 128, H)), ("upw", (SCN, 128, H)),
                        ("gw", (SCN, 128, H)), ("lao", (SCN, 128, H)),
                        ("fuse", (G, SCN, 128, H)), ("wu1", (UCN, 128, H)),
                        ("auxv", (SCN, 128, H)), ("wvd", (2, 128, H)),
                        ("wvu", (2, 128, H))]:
        CW[name] = nc.dram_tensor(name + "_c", list(shape), BF16, kind="Internal").ap()
    cc_in = [nc.dram_tensor(f"cc{i}_in", [1, 8], F32, kind="Internal") for i in (1, 2)]
    cc_out = [nc.dram_tensor(f"cc{i}_out", [1, 8], F32, kind="Internal") for i in (1, 2)]

    def sbt(name, shape, dt):
        return ctx.enter_context(nc.sbuf_tensor(name, list(shape), dt))

    ident = sbt("ident", [128, 128], BF16)
    ones_col = sbt("ones_col", [128, 1], F32)
    ones_row = sbt("ones_row", [1, 128], F32)
    lp_sb = sbt("lp_sb", [128, SCN], F32)
    bu1_sb = sbt("bu1_sb", [128, UCN], F32)
    wu2_sb = sbt("wu2_sb", [128, UCN], F32)
    df_sb = sbt("df_sb", [128, G], F32)
    ndf_sb = sbt("ndf_sb", [128, G], F32)
    scal_sb = sbt("scal_sb", [128, 3], F32)
    logrel_sb = sbt("logrel_sb", [128, NSLOT], F32)
    negbig_sb = sbt("negbig_sb", [128, NSLOT], F32)
    akt_sb = sbt("akt_sb", [BOT, NSLOT], F32)
    wrt_sb = sbt("wrt_sb", [BOT, RD], F32)
    rkt_sb = sbt("rkt_sb", [RD, NSLOT], F32)
    wq_sb = sbt("wq_sb", [128, HC * 64], BF16)
    stM = sbt("stM", [128, NSTG], F32)
    stR = sbt("stR", [128, NSTG], F32)
    stRh = sbt("stRh", [128, NSTG], F32)
    stA2 = sbt("stA2", [128, NSTG], F32)
    stB2 = sbt("stB2", [128, NSTG], F32)
    stL = sbt("stL", [128, NSTG], F32)
    ssA = sbt("ssA", [128, NSTG], F32)
    ssQ = sbt("ssQ", [128, NSTG], F32)
    rm = [sbt("rm1", [128, 1], F32), sbt("rm2", [128, 1], F32)]
    t8_sb = sbt("t8_sb", [128, 8], F32)
    sm1 = sbt("sm1", [128, 8], F32)
    sm2 = sbt("sm2", [128, 8], F32)
    sm3 = sbt("sm3", [128, 8], F32)
    learned_sb = sbt("learned_sb", [128, NSTT], F32)
    cc_sb = sbt("cc_sb", [1, 8], F32)
    w_stage = sbt("w_stage", [128, 4096], BF16)

    AB = 172 * KB
    arena = ctx.enter_context(nc.sbuf_tensor("arena", [128, AB], U8))
    a_f32 = arena.bitcast(F32).ap()
    a_bf = arena.bitcast(BF16).ap()
    a_u8 = arena.ap()

    def cf32(off_b, n):
        assert off_b % 4 == 0 and off_b + 4 * n <= AB, (off_b, n)
        return a_f32[:, off_b // 4: off_b // 4 + n]

    def cbf(off_b, n):
        assert off_b % 2 == 0 and off_b + 2 * n <= AB, (off_b, n)
        return a_bf[:, off_b // 2: off_b // 2 + n]

    def cu8(off_b, n):
        assert off_b + n <= AB
        return a_u8[:, off_b: off_b + n]

    hT = cbf(0, HC * TILE)
    hT3 = a_bf[:, 0:HC * TILE].rearrange("p (hc t) -> p hc t", t=TILE)
    R1 = 32 * KB
    tmpT = cbf(R1, 7 * SCN * TILE)
    aT = cbf(R1, FCN * TILE)
    aux1T = cbf(R1, HC * TILE)
    wT = cbf(R1 + 32 * KB, SCN * TILE)
    wT3 = a_bf[:, (R1 + 32 * KB) // 2:(R1 + 32 * KB) // 2 + SCN * TILE] \
        .rearrange("p (sc t) -> p sc t", t=TILE)
    v1T = cbf(R1 + 40 * KB, 2 * TILE)
    q_sb = cf32(R1 + 42 * KB, TILE)
    rq_sb = cf32(R1 + 44 * KB, TILE)
    m1g = cf32(R1 + 46 * KB, UCN * TILE)
    R_sb = cf32(R1 + 62 * KB, NSLOT)
    Sp_sb = cf32(R1 + 66 * KB, NSLOT)
    Sm_sb = cf32(R1 + 70 * KB, NSLOT)
    ex_sb = cf32(R1 + 74 * KB, NSLOT)
    mask_sb = cu8(R1 + 78 * KB, NSLOT)
    scr1k = cf32(R1 + 62 * KB, NSLOT)
    sxld = [cf32(R1, H), cf32(R1 + 16 * KB, H)]

    R2 = 112 * KB
    xld = [cf32(R2, H), cf32(R2 + 16 * KB, H)]
    h_bf = cbf(R2 + 32 * KB, H)
    wring = [cbf(R2 + i * 8 * KB, H) for i in range(4)]
    mx = [cf32(R2 + 32 * KB + i * 2 * KB, TILE) for i in range(6)]
    dring = [cbf(R2 + i * 2 * KB, 1024) for i in range(6)]
    evb = [cf32(R2 + 12 * KB + i * 4 * KB, 1024) for i in range(4)]
    xq = [cf32(R2 + 28 * KB + i * 4 * KB, 1024) for i in range(4)]
    yq = [cf32(R2 + 44 * KB + i * 4 * KB, 1024) for i in range(4)]
    ffring = [cbf(R2 + 16 * KB + i * 8 * KB, H) for i in range(3)]
    avring = [cbf(R2 + i * 2 * KB, 1024) for i in range(4)]
    wvd_sb = [cbf(R2, H), cbf(R2 + 8 * KB, H)]
    mx7 = cf32(R2 + 56 * KB, TILE)
    wvu7 = [cbf(R2, H), cbf(R2 + 8 * KB, H)]
    xq7 = [cf32(R2 + 16 * KB + i * 4 * KB, 1024) for i in range(4)]
    evb7 = [cf32(R2 + 32 * KB + i * 4 * KB, 1024) for i in range(4)]
    gz = cf32(R2 + 48 * KB, TILE)
    gt1 = cf32(R2 + 50 * KB, TILE)
    gt2 = cf32(R2 + 52 * KB, TILE)

    sx = Sem(ctx, nc, "sx")
    sst = Sem(ctx, nc, "sst")
    si = Sem(ctx, nc, "si")
    swq = Sem(ctx, nc, "swq")
    scx = Sem(ctx, nc, "scx")
    sx2 = [Sem(ctx, nc, f"sx2_{i}") for i in range(2)]
    swr = [Sem(ctx, nc, f"swr{i}") for i in range(4)]
    sdr = [Sem(ctx, nc, f"sdr{i}") for i in range(6)]
    sfr = [Sem(ctx, nc, f"sfr{i}") for i in range(3)]
    sqy = [Sem(ctx, nc, f"sqy{i}") for i in range(4)]
    swb = [Sem(ctx, nc, f"swb{i}") for i in range(4)]
    sdwb = [Sem(ctx, nc, f"sdwb{i}") for i in range(6)]
    sfwb = [Sem(ctx, nc, f"sfwb{i}") for i in range(3)]
    sw = Sem(ctx, nc, "sw")
    sw2 = Sem(ctx, nc, "sw2")
    sp = Sem(ctx, nc, "sp")
    sv = Sem(ctx, nc, "sv")
    sa = Sem(ctx, nc, "sa")
    sg = Sem(ctx, nc, "sg")
    scc = Sem(ctx, nc, "scc")

    block = ctx.enter_context(nc.Block())
    pst = PsumStack(nc)
    gd = {"pe": 0, "sv": 0, "sa": 0, "st": 0}

    def init_sync(s):
        for name, dst in [("lp", lp_sb), ("bu1", bu1_sb), ("wu2", wu2_sb),
                          ("df", df_sb), ("scal", scal_sb), ("logrel", logrel_sb),
                          ("akt", akt_sb), ("wrt", wrt_sb)]:
            si.inc(s.dma_start(dst[:, :], d[name]), 16)
    block.sync(init_sync)
    n_init_loads = si.n

    def init_g(g):
        sg.inc(g.memset(negbig_sb[:, :], NEGBIG))
        sg.inc(g.memset(ident[:, :], 0.0))
        g.wait_ge(sg.h, sg.n)
        sg.inc(g.affine_select(out=ident[:, :], in_=ident[:, :],
                               compare_op=ALU.not_equal, fill=1.0, base=0,
                               pattern=[[-1, 128]], channel_multiplier=1))
        sg.inc(g.memset(ones_col[:, :], 1.0))
        sg.inc(g.memset(ones_row[:, :], 1.0))
        swq.inc(g.dma_start(wq_sb[:, :], d["wq"]), 16)
    block.gpsimd(init_g)

    def init_v(v):
        v.wait_ge(si.h, n_init_loads)
        sv.inc(v.tensor_scalar(ndf_sb[:, :], df_sb[:, :], -1.0, None, op0=ALU.mult))
    block.vector(init_v)

    psrk_t = pst.enter([128, NSLOT], F32)

    def init_t(t):
        t.wait_ge(si.h, n_init_loads)
        t.matmul(psrk_t[0:RD, 0:512], wrt_sb[:, :], akt_sb[:, 0:512], start=True, stop=True)
        sp.inc(t.matmul(psrk_t[0:RD, 512:1024], wrt_sb[:, :], akt_sb[:, 512:1024],
                        start=True, stop=True))
    block.tensor(init_t)
    pe_rk = sp.n

    def init_v2(v):
        v.wait_ge(sp.h, pe_rk)
        sv.inc(v.tensor_copy(rkt_sb[:, :], psrk_t[0:RD, :]))
    block.vector(init_v2)
    pst.exit(1)
    gd.update(pe=sp.n, sv=sv.n, sa=sa.n, st=sst.n)

    def emit_stats(src, cc_idx):
        sxb = [s.n for s in sx2]
        svb = sv.n
        sab = sa.n
        st_guard = sst.n
        g_pe, g_sv, g_sa = gd["pe"], gd["sv"], gd["sa"]
        grp_tok = []

        def floads(s):
            s.wait_ge(sst.h, st_guard)
            s.wait_ge(sp.h, g_pe)
            s.wait_ge(sv.h, g_sv)
            s.wait_ge(sa.h, g_sa)
            for st in range(NSTG):
                if st >= 2:
                    s.wait_ge(sa.h, sab + (st - 1))
                    s.wait_ge(sv.h, svb + 2 * (st - 1))
                sx2[st % 2].inc(s.dma_start(sxld[st % 2],
                                            src[st * 128:(st + 1) * 128, :]), 16)
        block.sync(floads)

        def fsq(sc):
            sc.wait_ge(sp.h, g_pe)
            sc.wait_ge(sv.h, g_sv)
            for st in range(NSTG):
                sc.wait_ge(sx2[st % 2].h, sxb[st % 2] + 16 * (st // 2 + 1))
                if st >= 1:
                    sc.wait_ge(sv.h, svb + 2 * st)
                for k in range(4):
                    i = sc.activation(scr1k, sxld[st % 2][:, k * 1024:(k + 1) * 1024],
                                      AF.Square, accum_out=sm1[:, k:k + 1])
                sa.inc(i)
        block.scalar(fsq)
        assert sa.n == sab + NSTG

        def fred(v):
            for st in range(NSTG):
                v.wait_ge(sx2[st % 2].h, sxb[st % 2] + 16 * (st // 2 + 1))
                v.wait_ge(sv.h, sv.n)
                sv.inc(v.reduce_sum(ssA[:, st:st + 1], sxld[st % 2][:, :], axis=AX.X))
                v.wait_ge(sa.h, sab + st + 1)
                v.wait_ge(sv.h, sv.n)
                sv.inc(v.reduce_sum(ssQ[:, st:st + 1], sm1[:, 0:4], axis=AX.X))
        block.vector(fred)
        assert sv.n == svb + 2 * NSTG

        for g_ in range(4):
            c = slice(4 * g_, 4 * g_ + 4)
            svg = svb + 2 * (4 * g_ + 4)

            def ffin(v, c=c, svg=svg):
                def w():
                    v.wait_ge(sv.h, sv.n)
                v.wait_ge(sv.h, svg)
                w(); sv.inc(v.tensor_scalar(stM[:, c], ssA[:, c], 1.0 / H, None, op0=ALU.mult))
                sv.inc(v.tensor_scalar(ssQ[:, c], ssQ[:, c], 1.0 / H, None, op0=ALU.mult))
                w(); sv.inc(v.tensor_mul(ssA[:, c], stM[:, c], stM[:, c]))
                w(); sv.inc(v.tensor_sub(ssQ[:, c], ssQ[:, c], ssA[:, c]))
                w(); sv.inc(v.tensor_scalar(ssA[:, c], ssQ[:, c], 1e-5, None, op0=ALU.add))
                w(); sv.inc(v.reciprocal(stR[:, c], ssA[:, c]))
                w(); sv.inc(v.tensor_mul(stL[:, c], ssQ[:, c], stR[:, c]))
            block.vector(ffin)
            svn = sv.n

            def ffin_a(sc, c=c, svn=svn):
                sc.wait_ge(sv.h, svn)
                sc.wait_ge(sa.h, sa.n)
                sa.inc(sc.activation(ssQ[:, c], ssA[:, c], AF.Ln))
                sc.wait_ge(sa.h, sa.n)
                sa.inc(sc.activation(stR[:, c], ssQ[:, c], AF.Exp, scale=-0.5))
                sc.wait_ge(sa.h, sa.n)
                sa.inc(sc.activation(stL[:, c], stL[:, c], AF.Ln, bias=1.0))
            block.scalar(ffin_a)
            san = sa.n

            def ffin_v(v, c=c, san=san):
                v.wait_ge(sa.h, san)
                v.wait_ge(sv.h, sv.n)
                sv.inc(v.tensor_scalar(stRh[:, c], stR[:, c], 0.5, None, op0=ALU.mult))
                v.wait_ge(sv.h, sv.n)
                sv.inc(v.tensor_scalar(stA2[:, c], stRh[:, c], 1.0, None, op0=ALU.add))
                sv.inc(v.tensor_mul(stB2[:, c], stM[:, c], stRh[:, c]))
                v.wait_ge(sv.h, sv.n)
                sv.inc(v.tensor_scalar(stB2[:, c], stB2[:, c], -1.0, None, op0=ALU.mult))
            block.vector(ffin_v)
            grp_tok.append((sv.n, sa.n))

        def emit_cc(guard_sv=0):
            pslv_t = pst.enter([128, NSTG], F32)
            psbc = pst.enter([128, 1], F32)
            svn2, san2 = grp_tok[-1]

            def flv(t):
                t.wait_ge(sa.h, san2)
                t.wait_ge(sv.h, max(g_sv, guard_sv))
                sp.inc(t.matmul(pslv_t[0:1, :], ones_col[:, :], stL[:, :],
                                start=True, stop=True))
            block.tensor(flv)
            pen = sp.n

            def flv_v(v):
                v.wait_ge(sp.h, pen)
                v.wait_ge(sv.h, sv.n)
                sv.inc(v.reduce_sum(cc_sb[0:1, 0:1], pslv_t[0:1, :], axis=AX.X))
            block.vector(flv_v)
            svn3 = sv.n

            def fst(s):
                s.wait_ge(sv.h, svn3)
                sst.inc(s.dma_start(cc_in[cc_idx].ap()[0:1, 0:1], cc_sb[0:1, 0:1]), 16)
            block.sync(fst)
            stn = sst.n

            def fcc(g):
                g.wait_ge(sst.h, stn)
                scc.inc(g.collective_compute(
                    "AllReduce", ALU.add, replica_groups=[list(range(NCORES))],
                    ins=[cc_in[cc_idx].ap().opt()],
                    outs=[cc_out[cc_idx].ap().opt()]), 1)
            block.gpsimd(fcc)
            ccn = scc.n

            def fld(s):
                s.wait_ge(scc.h, ccn)
                scx.inc(s.dma_start(cc_sb[0:1, 1:2], cc_out[cc_idx].ap()[0:1, 0:1]), 16)
            block.sync(fld)
            sxn = scx.n

            def fmn(v):
                v.wait_ge(scx.h, sxn)
                v.wait_ge(sv.h, sv.n)
                sv.inc(v.tensor_scalar(cc_sb[0:1, 2:3], cc_sb[0:1, 1:2],
                                       1.0 / 16384.0, 1e-6, op0=ALU.mult, op1=ALU.add))
                v.wait_ge(sv.h, sv.n)
                sv.inc(v.reciprocal(cc_sb[0:1, 3:4], cc_sb[0:1, 2:3]))
            block.vector(fmn)
            svn4 = sv.n

            def fbc(t):
                t.wait_ge(sv.h, svn4)
                sp.inc(t.matmul(psbc[:, :], ones_row[:, :], cc_sb[0:1, 3:4],
                                start=True, stop=True))
            block.tensor(fbc)
            pen2 = sp.n

            def fbc_v(v):
                v.wait_ge(sp.h, pen2)
                v.wait_ge(sv.h, sv.n)
                sv.inc(v.tensor_copy(rm[cc_idx][:, :], psbc[:, :]))
            block.vector(fbc_v)
            pst.exit(2)
            gd["rm_sv"] = sv.n
        return grp_tok, (emit_cc if cc_idx is not None else None)

    def emit_prologue(src, tile, grp=None):
        sxb = [s.n for s in sx2]
        svb = sv.n
        sab = sa.n
        peb = sp.n
        g_pe, g_sv, g_sa = gd["pe"], gd["sv"], gd["sa"]

        def floads(s):
            s.wait_ge(sp.h, g_pe)
            s.wait_ge(sv.h, g_sv)
            s.wait_ge(sa.h, g_sa)
            s.wait_ge(sst.h, gd["st"])
            for st in range(NSTT):
                if st >= 2:
                    s.wait_ge(sv.h, svb + (st - 1))
                r0 = tile * TILE + st * 128
                sx2[st % 2].inc(s.dma_start(xld[st % 2], src[r0:r0 + 128, :]), 16)
        block.sync(floads)

        def fh(v):
            if grp is not None:
                v.wait_ge(sv.h, grp[0])
                v.wait_ge(sa.h, grp[1])
            for st in range(NSTT):
                gix = tile * NSTT + st
                v.wait_ge(sx2[st % 2].h, sxb[st % 2] + 16 * (st // 2 + 1))
                if st > 0:
                    v.wait_ge(sp.h, peb + st * 4)
                v.wait_ge(sv.h, sv.n)
                sv.inc(v.tensor_scalar(h_bf[:, :], xld[st % 2][:, :],
                                       stM[:, gix:gix + 1], stR[:, gix:gix + 1],
                                       op0=ALU.subtract, op1=ALU.mult))
        block.vector(fh)
        assert sv.n == svb + NSTT

        trp = [pst.enter([128, 1024], BF16), pst.enter([128, 1024], BF16)]

        def ftr(t):
            t.wait_ge(sv.h, g_sv)
            t.wait_ge(sa.h, g_sa)
            for st in range(NSTT):
                t.wait_ge(sv.h, svb + st + 1)
                for r in range(4):
                    gidx = st * 4 + r
                    pb = trp[gidx % 2]
                    if gidx >= 2:
                        t.wait_ge(sa.h, sab + gidx - 1)
                    for k in range(8):
                        hc = r * 8 + k
                        i = t.transpose(pb[:, k * 128:(k + 1) * 128],
                                        h_bf[:, hc * 128:(hc + 1) * 128], ident[:, :])
                    sp.inc(i)
        block.tensor(ftr)
        assert sp.n == peb + 16

        def fev(sc):
            sc.wait_ge(sp.h, g_pe)
            for st in range(NSTT):
                for r in range(4):
                    gidx = st * 4 + r
                    sc.wait_ge(sp.h, peb + gidx + 1)
                    pb3 = trp[gidx % 2].ap().rearrange("p (a b) -> p a b", b=128)
                    dst3 = hT3[:, r * 8:r * 8 + 8, st * 128:(st + 1) * 128]
                    sa.inc(sc.copy(dst3, pb3))
        block.scalar(fev)
        assert sa.n == sab + 16
        pst.exit(2)
        return {"hT_sa": sa.n, "xld_sv": svb + NSTT, "pe_end": peb + 16}

    def emit_attn_tile(src, dst, tile, grp=None):
        st_entry = sst.n
        pro = emit_prologue(src, tile, grp)
        hT_ready = pro["hT_sa"]

        psP = [pst.enter([128, 1024], F32), pst.enter([128, 1024], F32)]
        wrb = [s.n for s in swr]
        peb = sp.n
        svb = sv.n
        sab = sa.n
        DV = 17
        AC = 7

        build = (tile == 0)
        wbb = [s.n for s in swb]
        wrb_pre = [s.n for s in swr]

        def m1g_(g):
            g.wait_ge(sv.h, pro["xld_sv"])
            g.wait_ge(sst.h, st_entry)
            if not build:
                for k in range(4):
                    g.wait_ge(swb[k].h, wbb[k])
            for sc_i in range(SCN):
                a2 = 2 * (sc_i % 2)
                if sc_i >= 2:
                    g.wait_ge(sp.h, peb + 2 * (sc_i - 1))
                    if build:
                        g.wait_ge(swb[a2].h, wbb[a2] + 16 * (sc_i // 2))
                        g.wait_ge(swb[a2 + 1].h, wbb[a2 + 1] + 16 * (sc_i // 2))
                su, sg_ = (d["upw"], d["gw"]) if build else (CW["upw"], CW["gw"])
                swr[a2].inc(g.dma_start(wring[a2], su[sc_i]), 16)
                swr[a2 + 1].inc(g.dma_start(wring[a2 + 1], sg_[sc_i]), 16)
        block.gpsimd(m1g_)

        if build:
            def m1wb(s):
                for sc_i in range(SCN):
                    a2 = 2 * (sc_i % 2)
                    s.wait_ge(swr[a2].h, wrb_pre[a2] + 16 * (sc_i // 2 + 1))
                    swb[a2].inc(s.dma_start(CW["upw"][sc_i], wring[a2]), 16)
                    s.wait_ge(swr[a2 + 1].h, wrb_pre[a2 + 1] + 16 * (sc_i // 2 + 1))
                    swb[a2 + 1].inc(s.dma_start(CW["gw"][sc_i], wring[a2 + 1]), 16)
            block.sync(m1wb)

        def m1t(t):
            t.wait_ge(sa.h, hT_ready)
            for sc_i in range(SCN):
                a2 = 2 * (sc_i % 2)
                t.wait_ge(swr[a2].h, wrb[a2] + 16 * (sc_i // 2 + 1))
                t.wait_ge(swr[a2 + 1].h, wrb[a2 + 1] + 16 * (sc_i // 2 + 1))
                if sc_i >= 2:
                    t.wait_ge(sv.h, svb + DV * (sc_i - 2) + 3)
                pA = psP[sc_i % 2][:, 0:512]
                pB = psP[sc_i % 2][:, 512:1024]
                for hc in range(HC):
                    i = t.matmul(pA, wring[2 * (sc_i % 2)][:, hc * 128:(hc + 1) * 128],
                                 hT[:, hc * TILE:hc * TILE + TILE],
                                 start=(hc == 0), stop=(hc == HC - 1))
                sp.inc(i)
                for hc in range(HC):
                    i = t.matmul(pB, wring[2 * (sc_i % 2) + 1][:, hc * 128:(hc + 1) * 128],
                                 hT[:, hc * TILE:hc * TILE + TILE],
                                 start=(hc == 0), stop=(hc == HC - 1))
                sp.inc(i)
        block.tensor(m1t)
        assert sp.n == peb + 2 * SCN

        def m1a(sc):
            for sc_i in range(SCN):
                base_v = svb + DV * sc_i
                sc.wait_ge(sv.h, base_v + 5)
                sc.wait_ge(sa.h, sa.n)
                sa.inc(sc.activation(mx[4], mx[2], AF.Ln))
                for m in range(1, G + 1):
                    dt_slot = mx[5] if m % 2 == 1 else mx[2]
                    if m <= 2:
                        sc.wait_ge(sv.h, base_v + 9)
                    else:
                        tmp_idx = {1: 10, 2: 11, 3: 12, 4: 13}[m - 2]
                        sc.wait_ge(sv.h, base_v + tmp_idx)
                    sc.wait_ge(sa.h, sa.n)
                    sa.inc(sc.activation(dt_slot, mx[4], AF.Exp, scale=float(m),
                                         bias=ndf_sb[:, m - 1:m]))
        block.scalar(m1a)
        assert sa.n == sab + AC * SCN

        def m1v(v):
            for sc_i in range(SCN):
                base_a = sab + AC * sc_i

                def w():
                    v.wait_ge(sv.h, sv.n)
                pA = psP[sc_i % 2][:, 0:512]
                pB = psP[sc_i % 2][:, 512:1024]
                v.wait_ge(sp.h, peb + 2 * sc_i + 1)
                w()
                sv.inc(v.tensor_copy(tmpT[:, sc_i * TILE:sc_i * TILE + TILE], pA))
                sv.inc(v.tensor_copy(mx[0], pA))
                v.wait_ge(sp.h, peb + 2 * sc_i + 2)
                sv.inc(v.tensor_scalar(mx[1], pB, lp_sb[:, sc_i:sc_i + 1], None,
                                       op0=ALU.subtract))
                if sc_i >= 1:
                    v.wait_ge(sa.h, sab + AC * (sc_i - 1) + 6)
                w()
                sv.inc(v.tensor_scalar(mx[5], mx[1], -1.0, None, op0=ALU.mult))
                if sc_i >= 1:
                    v.wait_ge(sa.h, sab + AC * sc_i)
                w()
                sv.inc(v.tensor_max(mx[2], mx[1], mx[5]))
                w()
                sv.inc(v.tensor_scalar(mx[3], mx[1], 0.0, None, op0=ALU.is_gt))
                w()
                sv.inc(v.tensor_scalar(mx[5], mx[3], 2.0, -1.0, op0=ALU.mult,
                                       op1=ALU.add))
                w()
                sv.inc(v.tensor_scalar(mx[3], mx[2], 2.5, None, op0=ALU.is_le))
                w()
                sv.inc(v.tensor_mul(mx[1], mx[0], mx[5]))
                nops = 9
                for m in range(1, G + 1):
                    dt_slot = mx[5] if m % 2 == 1 else mx[2]
                    src_t = mx[1] if m % 2 == 1 else mx[0]
                    slot = tmpT[:, (m * SCN + sc_i) * TILE:(m * SCN + sc_i) * TILE + TILE]
                    v.wait_ge(sa.h, base_a + 1 + m)
                    w()
                    if m <= 4:
                        sv.inc(v.tensor_mul(slot, dt_slot, src_t)); nops += 1
                    else:
                        sv.inc(v.tensor_mul(mx7, dt_slot, src_t))
                        w()
                        sv.inc(v.tensor_mul(slot, mx7, mx[3]))
                        nops += 2
                assert nops == DV
        block.vector(m1v)
        assert sv.n == svb + DV * SCN
        pst.exit(2)
        m1_pe_end = sp.n
        m1_sv_end = sv.n

        psY = pst.enter([128, 4096], F32)
        NTL = 7 * SCN
        sdb = [s.n for s in sdr]
        peb2 = sp.n
        svb2 = sv.n
        stb2 = sst.n

        dwbb = [s.n for s in sdwb]
        sdb_pre = [s.n for s in sdr]

        def _m2src(j, hq, cached):
            term = j // SCN
            sc_i = j % SCN
            base = (CW if cached else d)
            if term == 0:
                return base["lao"][sc_i, :, hq * 1024:(hq + 1) * 1024]
            return base["fuse"][term - 1, sc_i, :, hq * 1024:(hq + 1) * 1024]

        def m2g_(g):
            g.wait_ge(sp.h, m1_pe_end)
            if not build:
                for k in range(6):
                    g.wait_ge(sdwb[k].h, dwbb[k])
            for hq in range(4):
                for j in range(NTL):
                    gi = hq * NTL + j
                    if gi >= 6:
                        g.wait_ge(sp.h, peb2 + gi - 5)
                        if build:
                            g.wait_ge(sdwb[gi % 6].h, dwbb[gi % 6] + 16 * (gi // 6))
                    sdr[gi % 6].inc(g.dma_start(dring[gi % 6],
                                                _m2src(j, hq, not build)), 16)
        block.gpsimd(m2g_)



        def m2t(t):
            t.wait_ge(sv.h, m1_sv_end)
            for hq in range(4):
                if hq > 0:
                    t.wait_ge(sv.h, svb2 + 4 * hq)
                for j in range(NTL):
                    gi = hq * NTL + j
                    t.wait_ge(sdr[gi % 6].h, sdb[gi % 6] + 16 * (gi // 6 + 1))
                    term = j // SCN
                    sc_i = j % SCN
                    for st in range(NSTT):
                        lhs = tmpT[:, (term * SCN + sc_i) * TILE + st * 128:
                                   (term * SCN + sc_i) * TILE + st * 128 + 128]
                        for half in range(2):
                            i = t.matmul(psY[:, st * 1024 + half * 512:
                                             st * 1024 + half * 512 + 512],
                                         lhs, dring[gi % 6][:, half * 512:half * 512 + 512],
                                         start=(j == 0), stop=(j == NTL - 1),
                                         skip_group_check=True)
                    sp.inc(i)
        block.tensor(m2t)
        assert sp.n == peb2 + 4 * NTL

        def m2v(v):
            v.wait_ge(sst.h, st_entry)
            for hq in range(4):
                v.wait_ge(sp.h, peb2 + NTL * (hq + 1))
                if hq >= 1:
                    v.wait_ge(sst.h, stb2 + 64 * hq)
                for st in range(NSTT):
                    v.wait_ge(sv.h, sv.n)
                    sv.inc(v.tensor_copy(evb[st], psY[:, st * 1024:(st + 1) * 1024]))
        block.vector(m2v)
        assert sv.n == svb2 + 16

        def m2s(s):
            for hq in range(4):
                if build:
                    for j in range(NTL):
                        gi = hq * NTL + j
                        s.wait_ge(sdr[gi % 6].h, sdb_pre[gi % 6] + 16 * (gi // 6 + 1))
                        sdwb[gi % 6].inc(s.dma_start(_m2src(j, hq, True),
                                                     dring[gi % 6]), 16)
                for st in range(NSTT):
                    s.wait_ge(sv.h, svb2 + 4 * hq + st + 1)
                    sst.inc(s.dma_start(
                        ymix[tile % 2][st * 128:(st + 1) * 128,
                                       hq * 1024:(hq + 1) * 1024], evb[st]), 16)
        block.sync(m2s)
        pst.exit(1)
        m2_pe_end = sp.n
        m2_sv_end = sv.n
        m2_st_end = sst.n

        psP2 = [pst.enter([128, 1024], F32), pst.enter([128, 1024], F32)]
        wrb3 = [s.n for s in swr]
        peb3 = sp.n
        svb3 = sv.n
        sab3 = sa.n

        wbb3 = [s.n for s in swb]
        wrb3_pre = [s.n for s in swr]

        def t1g(g):
            g.wait_ge(sp.h, m2_pe_end)
            g.wait_ge(sst.h, m2_st_end)
            if not build:
                for k in range(4):
                    g.wait_ge(swb[k].h, wbb3[k])
            for fc in range(FCN):
                a2 = 2 * (fc % 2)
                if fc >= 2:
                    g.wait_ge(sv.h, svb3 + (fc - 1))
                    if build:
                        g.wait_ge(swb[a2].h, wbb3[a2] + 16 * (fc // 2))
                        g.wait_ge(swb[a2 + 1].h, wbb3[a2 + 1] + 16 * (fc // 2))
                sgt, sut = (d["gt"], d["ut"]) if build else (CW["gt"], CW["ut"])
                swr[a2].inc(g.dma_start(wring[a2], sgt[fc]), 16)
                swr[a2 + 1].inc(g.dma_start(wring[a2 + 1], sut[fc]), 16)
        block.gpsimd(t1g)

        if build:
            def t1wb(s):
                for fc in range(FCN):
                    a2 = 2 * (fc % 2)
                    s.wait_ge(swr[a2].h, wrb3_pre[a2] + 16 * (fc // 2 + 1))
                    swb[a2].inc(s.dma_start(CW["gt"][fc], wring[a2]), 16)
                    s.wait_ge(swr[a2 + 1].h, wrb3_pre[a2 + 1] + 16 * (fc // 2 + 1))
                    swb[a2 + 1].inc(s.dma_start(CW["ut"][fc], wring[a2 + 1]), 16)
            block.sync(t1wb)

        def t1t(t):
            t.wait_ge(sv.h, m2_sv_end)
            for fc in range(FCN):
                a2 = 2 * (fc % 2)
                t.wait_ge(swr[a2].h, wrb3[a2] + 16 * (fc // 2 + 1))
                t.wait_ge(swr[a2 + 1].h, wrb3[a2 + 1] + 16 * (fc // 2 + 1))
                if fc >= 2:
                    t.wait_ge(sv.h, svb3 + (fc - 1))
                pA = psP2[fc % 2][:, 0:512]
                pB = psP2[fc % 2][:, 512:1024]
                for hc in range(HC):
                    i = t.matmul(pA, wring[2 * (fc % 2)][:, hc * 128:(hc + 1) * 128],
                                 hT[:, hc * TILE:hc * TILE + TILE],
                                 start=(hc == 0), stop=(hc == HC - 1))
                sp.inc(i)
                for hc in range(HC):
                    i = t.matmul(pB, wring[2 * (fc % 2) + 1][:, hc * 128:(hc + 1) * 128],
                                 hT[:, hc * TILE:hc * TILE + TILE],
                                 start=(hc == 0), stop=(hc == HC - 1))
                sp.inc(i)
        block.tensor(t1t)
        assert sp.n == peb3 + 2 * FCN

        def t1a(sc):
            for fc in range(FCN):
                sc.wait_ge(sp.h, peb3 + 2 * fc + 1)
                if fc >= 2:
                    sc.wait_ge(sv.h, svb3 + (fc - 1))
                sc.wait_ge(sa.h, sa.n)
                sa.inc(sc.activation(mx[fc % 2], psP2[fc % 2][:, 0:512], AF.Silu))
        block.scalar(t1a)
        assert sa.n == sab3 + FCN

        def t1v(v):
            for fc in range(FCN):
                v.wait_ge(sa.h, sab3 + fc + 1)
                v.wait_ge(sp.h, peb3 + 2 * fc + 2)
                v.wait_ge(sv.h, sv.n)
                sv.inc(v.tensor_mul(aT[:, fc * TILE:(fc + 1) * TILE], mx[fc % 2],
                                    psP2[fc % 2][:, 512:1024]))
        block.vector(t1v)
        assert sv.n == svb3 + FCN
        pst.exit(2)
        t1_pe_end = sp.n
        t1_sv_end = sv.n

        psY2 = pst.enter([128, 4096], F32)
        sdb4 = [s.n for s in sdr]
        peb4 = sp.n
        svb4 = sv.n
        stb4 = sst.n
        sqb4 = [s.n for s in sqy]

        dwbb4 = [s.n for s in sdwb]
        sdb4_pre = [s.n for s in sdr]

        def t2g(g):
            g.wait_ge(sp.h, t1_pe_end)
            if not build:
                for k in range(6):
                    g.wait_ge(sdwb[k].h, dwbb4[k])
            for hq in range(4):
                for fc in range(FCN):
                    gi = hq * FCN + fc
                    if gi >= 6:
                        g.wait_ge(sp.h, peb4 + gi - 5)
                        if build:
                            g.wait_ge(sdwb[gi % 6].h, dwbb4[gi % 6] + 16 * (gi // 6))
                    sdt = d["dtw"] if build else CW["dtw"]
                    sdr[gi % 6].inc(g.dma_start(
                        dring[gi % 6], sdt[fc, :, hq * 1024:(hq + 1) * 1024]), 16)
        block.gpsimd(t2g)



        def t2s(s):
            s.wait_ge(sv.h, t1_sv_end)
            s.wait_ge(sst.h, m2_st_end)
            for hq in range(4):
                if hq >= 1:
                    s.wait_ge(sst.h, stb4 + 64 * hq)
                for st in range(NSTT):
                    if hq >= 1:
                        s.wait_ge(sv.h, svb4 + 8 * (hq - 1) + 2 * st + 2)
                    r0 = tile * TILE + st * 128
                    sqy[st].inc(s.dma_start(xq[st], src[r0:r0 + 128,
                                                        hq * 1024:(hq + 1) * 1024]), 16)
                    sqy[st].inc(s.dma_start(yq[st],
                                            ymix[tile % 2][st * 128:(st + 1) * 128,
                                                           hq * 1024:(hq + 1) * 1024]), 16)
                if build:
                    for fc in range(FCN):
                        gi = hq * FCN + fc
                        s.wait_ge(sdr[gi % 6].h, sdb4_pre[gi % 6] + 16 * (gi // 6 + 1))
                        sdwb[gi % 6].inc(s.dma_start(
                            CW["dtw"][fc, :, hq * 1024:(hq + 1) * 1024],
                            dring[gi % 6]), 16)
                for st in range(NSTT):
                    s.wait_ge(sv.h, svb4 + 8 * hq + 2 * st + 2)
                    r0 = tile * TILE + st * 128
                    sst.inc(s.dma_start(dst[r0:r0 + 128, hq * 1024:(hq + 1) * 1024],
                                        yq[st]), 16)
        block.sync(t2s)

        def t2t(t):
            t.wait_ge(sv.h, t1_sv_end)
            for hq in range(4):
                if hq > 0:
                    t.wait_ge(sv.h, svb4 + 8 * (hq - 1) + 7)
                for fc in range(FCN):
                    gi = hq * FCN + fc
                    t.wait_ge(sdr[gi % 6].h, sdb4[gi % 6] + 16 * (gi // 6 + 1))
                    for st in range(NSTT):
                        lhs = aT[:, fc * TILE + st * 128:fc * TILE + st * 128 + 128]
                        for half in range(2):
                            i = t.matmul(psY2[:, st * 1024 + half * 512:
                                              st * 1024 + half * 512 + 512],
                                         lhs, dring[gi % 6][:, half * 512:half * 512 + 512],
                                         start=(fc == 0), stop=(fc == FCN - 1),
                                         skip_group_check=True)
                    sp.inc(i)
        block.tensor(t2t)
        assert sp.n == peb4 + 4 * FCN

        def t2v(v):
            v.wait_ge(sst.h, m2_st_end)
            for hq in range(4):
                v.wait_ge(sp.h, peb4 + FCN * (hq + 1))
                for st in range(NSTT):
                    v.wait_ge(sqy[st].h, sqb4[st] + 32 * (hq + 1))
                    v.wait_ge(sv.h, sv.n)
                    sv.inc(v.tensor_add(evb[st], psY2[:, st * 1024:(st + 1) * 1024],
                                        yq[st]))
                    v.wait_ge(sv.h, sv.n)
                    sv.inc(v.tensor_add(yq[st], evb[st], xq[st]))
        block.vector(t2v)
        assert sv.n == svb4 + 32
        pst.exit(1)
        gd.update(pe=sp.n, sv=sv.n, sa=sa.n, st=sst.n)

    def emit_ffn_tile(src, dst, tile, rm_idx, grp=None, cc_emit=None):
        st_entry = sst.n
        build = (rm_idx == 0 and tile == 0)
        pro = emit_prologue(src, tile, grp)
        hT_ready = pro["hT_sa"]

        psQ_t = pst.enter([128, TILE], F32)
        psQ = psQ_t[0:BOT, :]
        psRQ_t = pst.enter([128, TILE], F32)
        psRQ = psRQ_t[0:RD, :]

        def f1t(t):
            t.wait_ge(sa.h, hT_ready)
            t.wait_ge(swq.h, 16)
            for hc in range(HC):
                i = t.matmul(psQ[:, :], wq_sb[:, hc * 64:(hc + 1) * 64],
                             hT[:, hc * TILE:hc * TILE + TILE],
                             start=(hc == 0), stop=(hc == HC - 1))
            sp.inc(i)
        block.tensor(f1t)
        pe_q = sp.n

        def f1v(v):
            v.wait_ge(sp.h, pe_q)
            v.wait_ge(sv.h, sv.n)
            sv.inc(v.tensor_copy(q_sb[0:BOT, :], psQ[:, :]))
        block.vector(f1v)
        sv_q = sv.n

        def f1t2(t):
            t.wait_ge(sv.h, sv_q)
            sp.inc(t.matmul(psRQ[:, :], wrt_sb[:, :], q_sb[0:BOT, :],
                            start=True, stop=True))
        block.tensor(f1t2)
        pe_rq = sp.n

        def f1v2(v):
            v.wait_ge(sp.h, pe_rq)
            v.wait_ge(sv.h, sv.n)
            sv.inc(v.tensor_copy(rq_sb[0:RD, :], psRQ[:, :]))
        block.vector(f1v2)
        sv_rq = sv.n
        pst.exit(2)

        psR = pst.enter([128, NSLOT], F32)
        psS = pst.enter([128, NSLOT], F32)
        psM = [pst.enter([128, TILE], F32), pst.enter([128, TILE], F32)]
        trpF = [pst.enter([128, 1024], BF16), pst.enter([128, 1024], BF16)]

        peb_rs = sp.n
        svb_sm = sv.n
        sab_sm = sa.n
        SMV = 8

        def f2t(t):
            t.wait_ge(sv.h, sv_rq)
            for st in range(NSTT):
                if st > 0:
                    t.wait_ge(sv.h, svb_sm + SMV * (st - 1) + 4)
                for half in range(2):
                    t.matmul(psR[:, half * 512:(half + 1) * 512],
                             rq_sb[0:RD, st * 128:(st + 1) * 128],
                             rkt_sb[:, half * 512:(half + 1) * 512],
                             start=True, stop=True)
                for half in range(2):
                    i = t.matmul(psS[:, half * 512:(half + 1) * 512],
                                 q_sb[0:BOT, st * 128:(st + 1) * 128],
                                 akt_sb[:, half * 512:(half + 1) * 512],
                                 start=True, stop=True)
                sp.inc(i)
        block.tensor(f2t)
        assert sp.n == peb_rs + NSTT

        def f2a(sc):
            for st in range(NSTT):
                sc.wait_ge(sv.h, svb_sm + SMV * st + 6)
                sc.wait_ge(sa.h, sa.n)
                sa.inc(sc.activation(ex_sb, Sm_sb, AF.Exp, bias=sm1[:, 0:1],
                                     accum_out=sm1[:, 1:2]))
        block.scalar(f2a)
        assert sa.n == sab_sm + NSTT

        def f2v(v):
            for st in range(NSTT):
                def w():
                    v.wait_ge(sv.h, sv.n)
                v.wait_ge(sp.h, peb_rs + st + 1)
                w()
                sv.inc(v.scalar_tensor_tensor(R_sb, psR[:, :], RSQ48, logrel_sb[:, :],
                                              op0=ALU.mult, op1=ALU.add))
                w()
                sv.inc(v.max(out=t8_sb[:, :], in_=R_sb))
                w()
                sv.inc(v.tensor_scalar(mask_sb, R_sb, t8_sb[:, 7:8], None,
                                       op0=ALU.is_ge))
                sv.inc(v.scalar_tensor_tensor(Sp_sb, psS[:, :], RSQ64, logrel_sb[:, :],
                                              op0=ALU.mult, op1=ALU.add))
                if st >= 1:
                    v.wait_ge(sa.h, sab_sm + st)
                w()
                sv.inc(v.select(Sm_sb, mask_sb, Sp_sb, negbig_sb[:, :]))
                w()
                sv.inc(v.reduce_max(sm1[:, 0:1], Sm_sb, axis=AX.X, negate=True))
                v.wait_ge(sa.h, sab_sm + st + 1)
                w()
                sv.inc(v.reciprocal(sm1[:, 2:3], sm1[:, 1:2]))
                w()
                sv.inc(v.tensor_scalar(w_stage[:, st * 1024:(st + 1) * 1024], ex_sb,
                                       sm1[:, 2:3], None, op0=ALU.mult))
        block.vector(f2v)
        assert sv.n == svb_sm + SMV * NSTT
        sm_sv_end = sv.n

        sfb = [s.n for s in sfr]
        peb_m = sp.n
        svb_g = sv.n
        sab_g = sa.n
        GAV = 2
        GVV = 6

        fwbb = [s.n for s in sfwb]
        sfb_pre = [s.n for s in sfr]

        def f3g(g):
            g.wait_ge(sv.h, pro["xld_sv"])
            g.wait_ge(sp.h, pro["pe_end"])
            g.wait_ge(sst.h, st_entry)
            if not build:
                for k in range(3):
                    g.wait_ge(sfwb[k].h, fwbb[k])
            for uc in range(UCN):
                if uc >= 3:
                    g.wait_ge(sp.h, peb_m + uc - 2)
                    if build:
                        g.wait_ge(sfwb[uc % 3].h, fwbb[uc % 3] + 16 * (uc // 3))
                swu = d["wu1"] if build else CW["wu1"]
                sfr[uc % 3].inc(g.dma_start(ffring[uc % 3], swu[uc]), 16)
        block.gpsimd(f3g)

        if build:
            def f3wb(s):
                for uc in range(UCN):
                    s.wait_ge(sfr[uc % 3].h, sfb_pre[uc % 3] + 16 * (uc // 3 + 1))
                    sfwb[uc % 3].inc(s.dma_start(CW["wu1"][uc], ffring[uc % 3]), 16)
            block.sync(f3wb)

        def f3t(t):
            t.wait_ge(sa.h, hT_ready)
            for uc in range(UCN):
                t.wait_ge(sfr[uc % 3].h, sfb[uc % 3] + 16 * (uc // 3 + 1))
                if uc >= 2:
                    t.wait_ge(sv.h, svb_g + GVV * (uc - 2) + 1)
                pb = psM[uc % 2]
                for hc in range(HC):
                    i = t.matmul(pb[:, :], ffring[uc % 3][:, hc * 128:(hc + 1) * 128],
                                 hT[:, hc * TILE:hc * TILE + TILE],
                                 start=(hc == 0), stop=(hc == HC - 1))
                sp.inc(i)
        block.tensor(f3t)
        assert sp.n == peb_m + UCN

        def f3a(sc):
            sc.wait_ge(sst.h, st_entry)
            for uc in range(UCN):
                sc.wait_ge(sv.h, svb_g + GVV * uc + 1)
                sc.wait_ge(sa.h, sa.n)
                sa.inc(sc.activation(gt1, gz, AF.Square))
                sc.wait_ge(sv.h, svb_g + GVV * uc + 3)
                sc.wait_ge(sa.h, sa.n)
                sa.inc(sc.activation(gt2, gt1, AF.Tanh, scale=GELU_C))
        block.scalar(f3a)
        assert sa.n == sab_g + GAV * UCN

        def f3v(v):
            v.wait_ge(sst.h, st_entry)
            for uc in range(UCN):
                def w():
                    v.wait_ge(sv.h, sv.n)
                v.wait_ge(sp.h, peb_m + uc + 1)
                if uc >= 1:
                    v.wait_ge(sa.h, sab_g + GAV * (uc - 1) + 1)
                w()
                sv.inc(v.tensor_scalar(gz, psM[uc % 2][:, :], bu1_sb[:, uc:uc + 1],
                                       None, op0=ALU.add))
                v.wait_ge(sa.h, sab_g + GAV * uc + 1)
                w()
                sv.inc(v.tensor_mul(gt2, gt1, gz))
                w()
                sv.inc(v.scalar_tensor_tensor(gt1, gt2, GELU_A, gz, op0=ALU.mult,
                                              op1=ALU.add))
                v.wait_ge(sa.h, sab_g + GAV * uc + 2)
                w()
                sv.inc(v.tensor_scalar(gt1, gt2, 1.0, None, op0=ALU.add))
                w()
                sv.inc(v.tensor_mul(gt2, gt1, gz))
                w()
                sv.inc(v.tensor_scalar(m1g[:, uc * TILE:(uc + 1) * TILE], gt2, 0.5,
                                       None, op0=ALU.mult))
        block.vector(f3v)
        assert sv.n == svb_g + GVV * UCN
        gelu_sv_end = sv.n

        peb_tr = sp.n
        sab_tr = sa.n

        def f2bt(t):
            for st in range(NSTT):
                t.wait_ge(sv.h, svb_sm + SMV * (st + 1))
                pb = trpF[st % 2]
                if st >= 2:
                    t.wait_ge(sa.h, sab_tr + st - 1)
                for k in range(8):
                    i = t.transpose(pb[:, k * 128:(k + 1) * 128],
                                    w_stage[:, st * 1024 + k * 128:st * 1024 + (k + 1) * 128],
                                    ident[:, :])
                sp.inc(i)
        block.tensor(f2bt)
        assert sp.n == peb_tr + NSTT

        def f2ba(sc):
            for st in range(NSTT):
                sc.wait_ge(sp.h, peb_tr + st + 1)
                pb3 = trpF[st % 2].ap().rearrange("p (a b) -> p a b", b=128)
                sa.inc(sc.copy(wT3[:, :, st * 128:(st + 1) * 128], pb3))
        block.scalar(f2ba)
        assert sa.n == sab_tr + NSTT
        wT_ready = sa.n
        pst.exit(6)

        psL = pst.enter([128, 8], F32)
        peb_l = sp.n
        svb_l = sv.n

        def f4t(t):
            t.wait_ge(sv.h, gelu_sv_end)
            t.wait_ge(sa.h, wT_ready)
            for st in range(NSTT):
                if st > 0:
                    t.wait_ge(sv.h, svb_l + st)
                for uc in range(UCN):
                    i = t.matmul(psL[:, 0:1],
                                 m1g[:, uc * TILE + st * 128:uc * TILE + st * 128 + 128],
                                 wu2_sb[:, uc:uc + 1],
                                 start=(uc == 0), stop=(uc == UCN - 1))
                sp.inc(i)
        block.tensor(f4t)
        assert sp.n == peb_l + NSTT

        def f4v(v):
            for st in range(NSTT):
                v.wait_ge(sp.h, peb_l + st + 1)
                v.wait_ge(sv.h, sv.n)
                sv.inc(v.tensor_copy(learned_sb[:, st:st + 1], psL[:, 0:1]))
        block.vector(f4v)
        assert sv.n == svb_l + NSTT
        learned_sv = sv.n
        pst.exit(1)
        if cc_emit is not None:
            cc_emit(learned_sv)

        g_idx = tile * NSTT
        sab_gt = sa.n

        rm_tok = gd.get("rm_sv", 0)

        def fgates_a(sc):
            sc.wait_ge(sv.h, learned_sv)
            sc.wait_ge(sa.h, sa.n)
            sa.inc(sc.activation(sm1[:, 0:4], learned_sb[:, 0:4], AF.Sigmoid,
                                 bias=scal_sb[:, 2:3]))
            sc.wait_ge(sv.h, svb_gt_pred[0])
            sc.wait_ge(sa.h, sa.n)
            sa.inc(sc.activation(sm2[:, 0:4], sm2[:, 4:8], AF.Sigmoid))
        svb_gt_pred = [None]

        def fgates_v(v):
            def w():
                v.wait_ge(sv.h, sv.n)
            v.wait_ge(sv.h, rm_tok)
            v.wait_ge(sa.h, sab_gt + 1)
            w()
            sv.inc(v.tensor_scalar(sm1[:, 4:8], stL[:, g_idx:g_idx + 4],
                                   rm[rm_idx][:, 0:1], None, op0=ALU.mult))
            sv.inc(v.tensor_scalar(sm2[:, 0:4], sm1[:, 0:4], 2.5, None,
                                   op0=ALU.mult))
            w()
            sv.inc(v.scalar_tensor_tensor(sm1[:, 0:4], sm1[:, 4:8], 0.5,
                                          sm2[:, 0:4], op0=ALU.mult,
                                          op1=ALU.add))
            w()
            sv.inc(v.tensor_scalar(sm1[:, 4:8], sm1[:, 0:4], 0.0, 5.0,
                                   op0=ALU.max, op1=ALU.min))
            w()
            sv.inc(v.tensor_scalar(sm1[:, 0:4], sm1[:, 4:8], 0.5, 1.0 / 1.5,
                                   op0=ALU.subtract, op1=ALU.mult))
            w()
            sv.inc(v.tensor_scalar(sm1[:, 4:8], sm1[:, 0:4], 0.0, 1.0,
                                   op0=ALU.max, op1=ALU.min))
            w()
            sv.inc(v.tensor_scalar(sm2[:, 4:8], sm1[:, 4:8], scal_sb[:, 0:1],
                                   scal_sb[:, 1:2], op0=ALU.mult, op1=ALU.add))
            svb_gt_pred[0] = sv.n
            v.wait_ge(sa.h, sab_gt + 2)
            w()
            sv.inc(v.tensor_scalar(sm3[:, 4:8], sm2[:, 0:4], 0.05, None,
                                   op0=ALU.is_ge))
            w()
            sv.inc(v.scalar_tensor_tensor(sm3[:, 0:4], sm2[:, 0:4], 0.5,
                                          sm3[:, 4:8], op0=ALU.mult,
                                          op1=ALU.mult))
        block.vector(fgates_v)
        block.scalar(fgates_a)
        gates_sv = sv.n

        psA1 = pst.enter([128, 4096], F32)
        sdb5 = [s.n for s in sdr]
        peb5 = sp.n
        sab5 = sa.n

        dwbb5 = [s.n for s in sdwb]
        sdb5_pre = [s.n for s in sdr]

        def f5g(g):
            g.wait_ge(sv.h, pro["xld_sv"])
            if not build:
                for k in range(4):
                    g.wait_ge(sdwb[k].h, dwbb5[k])
            for rnd in range(4):
                for sc_i in range(SCN):
                    gi = rnd * SCN + sc_i
                    if gi >= 4:
                        g.wait_ge(sp.h, peb5 + gi - 3)
                        if build:
                            g.wait_ge(sdwb[gi % 4].h, dwbb5[gi % 4] + 16 * (gi // 4))
                    sav = d["auxv"] if build else CW["auxv"]
                    sdr[gi % 4].inc(g.dma_start(
                        avring[gi % 4], sav[sc_i, :, rnd * 1024:(rnd + 1) * 1024]), 16)
        block.gpsimd(f5g)

        if build:
            def f5wb(s):
                for rnd in range(4):
                    for sc_i in range(SCN):
                        gi = rnd * SCN + sc_i
                        s.wait_ge(sdr[gi % 4].h, sdb5_pre[gi % 4] + 16 * (gi // 4 + 1))
                        sdwb[gi % 4].inc(s.dma_start(
                            CW["auxv"][sc_i, :, rnd * 1024:(rnd + 1) * 1024],
                            avring[gi % 4]), 16)
            block.sync(f5wb)

        def f5t(t):
            t.wait_ge(sa.h, wT_ready)
            t.wait_ge(sv.h, learned_sv)
            for rnd in range(4):
                if rnd > 0:
                    t.wait_ge(sa.h, sab5 + 8 * rnd)
                for sc_i in range(SCN):
                    gi = rnd * SCN + sc_i
                    t.wait_ge(sdr[gi % 4].h, sdb5[gi % 4] + 16 * (gi // 4 + 1))
                    for k in range(8):
                        i = t.matmul(psA1[:, k * 512:(k + 1) * 512],
                                     avring[gi % 4][:, k * 128:(k + 1) * 128],
                                     wT[:, sc_i * TILE:(sc_i + 1) * TILE],
                                     start=(sc_i == 0), stop=(sc_i == SCN - 1),
                                     skip_group_check=True)
                    sp.inc(i)
        block.tensor(f5t)
        assert sp.n == peb5 + 32

        def f5a(sc):
            for rnd in range(4):
                sc.wait_ge(sp.h, peb5 + 8 * (rnd + 1))
                for k in range(8):
                    hc = rnd * 8 + k
                    sc.wait_ge(sa.h, sa.n)
                    sa.inc(sc.copy(aux1T[:, hc * TILE:(hc + 1) * TILE],
                                   psA1[:, k * 512:(k + 1) * 512]))
        block.scalar(f5a)
        assert sa.n == sab5 + 32
        aux1_sa = sa.n
        pst.exit(1)

        psV = [pst.enter([128, TILE], F32), pst.enter([128, TILE], F32)]
        wrb6 = [s.n for s in swr]
        peb6 = sp.n
        svb6 = sv.n

        wbb6 = [s.n for s in swb]
        wrb6_pre = [s.n for s in swr]

        def f6g(g):
            g.wait_ge(sp.h, peb5 + 32)
            if not build:
                for k in range(2):
                    g.wait_ge(swb[k].h, wbb6[k])
            for vc in range(2):
                svd = d["wvd"] if build else CW["wvd"]
                swr[vc].inc(g.dma_start(wvd_sb[vc], svd[vc]), 16)
        block.gpsimd(f6g)

        if build:
            def f6wb(s):
                for vc in range(2):
                    s.wait_ge(swr[vc].h, wrb6_pre[vc] + 16)
                    swb[vc].inc(s.dma_start(CW["wvd"][vc], wvd_sb[vc]), 16)
            block.sync(f6wb)

        def f6t(t):
            t.wait_ge(sa.h, aux1_sa)
            for vc in range(2):
                t.wait_ge(swr[vc].h, wrb6[vc] + 16)
                for hc in range(HC):
                    i = t.matmul(psV[vc][:, :], wvd_sb[vc][:, hc * 128:(hc + 1) * 128],
                                 aux1T[:, hc * TILE:hc * TILE + TILE],
                                 start=(hc == 0), stop=(hc == HC - 1))
                sp.inc(i)
        block.tensor(f6t)
        assert sp.n == peb6 + 2

        def f6v(v):
            for vc in range(2):
                v.wait_ge(sp.h, peb6 + vc + 1)
                v.wait_ge(sv.h, sv.n)
                sv.inc(v.tensor_copy(v1T[:, vc * TILE:(vc + 1) * TILE], psV[vc][:, :]))
        block.vector(f6v)
        assert sv.n == svb6 + 2
        v1_sv = sv.n
        pst.exit(2)

        wbb6b = [swb[2].n, swb[3].n]
        wrb6b_pre = [swr[2].n, swr[3].n]

        def f6g2(g):
            g.wait_ge(sp.h, peb6 + 2)
            if not build:
                for k in range(2):
                    g.wait_ge(swb[2 + k].h, wbb6b[k])
            for vc in range(2):
                svu = d["wvu"] if build else CW["wvu"]
                swr[2 + vc].inc(g.dma_start(wvu7[vc], svu[vc]), 16)
        block.gpsimd(f6g2)
        wvu_tok = [swr[2].n, swr[3].n]

        if build:
            def f6wb2(s):
                for vc in range(2):
                    s.wait_ge(swr[2 + vc].h, wrb6b_pre[vc] + 16)
                    swb[2 + vc].inc(s.dma_start(CW["wvu"][vc], wvu7[vc]), 16)
            block.sync(f6wb2)

        psYh = [pst.enter([128, 2048], F32), pst.enter([128, 2048], F32)]
        peb7 = sp.n
        svb7 = sv.n
        sqb7 = [s.n for s in sqy]
        stb7 = sst.n
        FEV = 8

        def f7s(s):
            s.wait_ge(sp.h, peb7)
            for st in range(NSTT):
                if st >= 1:
                    s.wait_ge(sv.h, svb7 + FEV * st)
                    s.wait_ge(sst.h, stb7 + 16 * 4 * st)
                r0 = tile * TILE + st * 128
                for hq in range(4):
                    sqy[hq].inc(s.dma_start(xq7[hq], src[r0:r0 + 128,
                                                         hq * 1024:(hq + 1) * 1024]), 16)
                for hq in range(4):
                    s.wait_ge(sv.h, svb7 + FEV * st + 2 * (hq + 1))
                    sst.inc(s.dma_start(dst[r0:r0 + 128, hq * 1024:(hq + 1) * 1024],
                                        xq7[hq]), 16)
        block.sync(f7s)

        def f7t(t):
            t.wait_ge(sv.h, v1_sv)
            t.wait_ge(swr[2].h, wvu_tok[0])
            t.wait_ge(swr[3].h, wvu_tok[1])
            for st in range(NSTT):
                for half in range(2):
                    if st > 0:
                        t.wait_ge(sv.h, svb7 + FEV * (st - 1) + 4 * (half + 1))
                    for vc in range(2):
                        for k in range(4):
                            kk = half * 4 + k
                            i = t.matmul(psYh[half][:, k * 512:(k + 1) * 512],
                                         v1T[:, vc * TILE + st * 128:vc * TILE + st * 128 + 128],
                                         wvu7[vc][:, kk * 512:(kk + 1) * 512],
                                         start=(vc == 0), stop=(vc == 1),
                                         skip_group_check=True)
                    sp.inc(i)
        block.tensor(f7t)
        assert sp.n == peb7 + 2 * NSTT


        def f7v(v):
            v.wait_ge(sst.h, st_entry)
            v.wait_ge(sv.h, gates_sv)
            for st in range(NSTT):
                def w():
                    v.wait_ge(sv.h, sv.n)
                gst = g_idx + st
                for hq in range(4):
                    half = hq // 2
                    v.wait_ge(sp.h, peb7 + 2 * st + half + 1)
                    v.wait_ge(sqy[hq].h, sqb7[hq] + 16 * (st + 1))
                    w()
                    sv.inc(v.tensor_scalar(evb7[hq % 2], xq7[hq],
                                           stA2[:, gst:gst + 1], stB2[:, gst:gst + 1],
                                           op0=ALU.mult, op1=ALU.add))
                    w()
                    sv.inc(v.scalar_tensor_tensor(
                        xq7[hq], psYh[half][:, (hq % 2) * 1024:(hq % 2) * 1024 + 1024],
                        sm3[:, st:st + 1], evb7[hq % 2], op0=ALU.mult,
                        op1=ALU.add))
        block.vector(f7v)
        assert sv.n == svb7 + FEV * NSTT
        pst.exit(2)
        gd.update(pe=sp.n, sv=sv.n, sa=sa.n, st=sst.n)

    gt0, cc0 = emit_stats(d["x0"], 0)
    last = "x0"
    if upto >= 1:
        for tile in range(NT):
            emit_ffn_tile(d["x0"], d["x1"], tile, 0, grp=gt0[tile],
                          cc_emit=(cc0 if tile == 0 else None))
        last = "x1"
    if upto >= 2:
        gt1_, _ = emit_stats(d["x1"], None)
        for tile in range(NT):
            emit_attn_tile(d["x1"], d["x2"], tile, grp=gt1_[tile])
        last = "x2"
    if upto >= 3:
        gt2_, cc2 = emit_stats(d["x2"], 1)
        for tile in range(NT):
            emit_ffn_tile(d["x2"], d["y"], tile, 1, grp=gt2_[tile],
                          cc_emit=(cc2 if tile == 0 else None))
        last = "y"

    if last != "y":
        stn0 = sst.n
        g_pe, g_sv, g_sa = gd["pe"], gd["sv"], gd["sa"]

        def fcp(s):
            s.wait_ge(sp.h, g_pe)
            s.wait_ge(sv.h, g_sv)
            s.wait_ge(sa.h, g_sa)
            s.wait_ge(sst.h, stn0)
            for st in range(NSTG):
                if st >= 2:
                    s.wait_ge(sst.h, stn0 + 16 * (st - 1))
                sx2[st % 2].inc(s.dma_start(xld[st % 2],
                                            d[last][st * 128:(st + 1) * 128, :]), 16)
                s.wait_ge(sx2[st % 2].h, sx2[st % 2].n)
                sst.inc(s.dma_start(d["y"][st * 128:(st + 1) * 128, :], xld[st % 2]), 16)
        block.sync(fcp)

    fins = [(sst, sst.n)] + [(s, s.n) for s in
                              ([si, swq, scx] + sx2 + swr + sdr + sfr + sqy
                               + swb + sdwb + sfwb)]

    def fend(s):
        for sem, n in fins:
            s.wait_ge(sem.h, n)
    block.sync(fend)

    ctx.close()
    return nc


def prep_inputs(inp):
    f32 = np.float32

    def A(x):
        return np.ascontiguousarray(np.asarray(x, dtype=f32))

    w = {}
    w["gt"] = _blk_lhsT(A(inp["L0_attn_ffn_gate_w"]))
    w["ut"] = _blk_lhsT(A(inp["L0_attn_ffn_up_w"]))
    w["dtw"] = A(np.asarray(inp["L0_attn_ffn_down_w"], f32).T).reshape(FCN, 128, H)
    w["upw"] = _blk_lhsT(A(inp["L0_attn_up_w"]))
    w["gw"] = _blk_lhsT(A(inp["L0_attn_gate_w"]))
    w["lao"] = A(np.asarray(inp["L0_attn_local_approx_output"], f32).T).reshape(SCN, 128, H)
    w["fuse"] = A(np.transpose(np.asarray(inp["L0_attn_fuse_weight"], f32),
                               (2, 1, 0))).reshape(G, SCN, 128, H)
    w["wq"] = _blk_lhsT(A(inp["L0_ffn_Wq"]), oc=64).reshape(128, HC * 64)
    w["wu1"] = _blk_lhsT(A(inp["L0_ffn_Wu1"]))
    w["wvd"] = _blk_lhsT(A(inp["L0_ffn_Wvd"]))
    w["wvu"] = A(np.asarray(inp["L0_ffn_Wvu"], f32).T).reshape(2, 128, H)
    w["auxv"] = A(inp["L0_ffn_aux_values"]).reshape(SCN, 128, H)
    w["akt"] = A(np.asarray(inp["L0_ffn_aux_keys"], f32).T)
    w["wrt"] = A(np.asarray(inp["L0_ffn_Wr"], f32).T)
    w["lp"] = A(np.asarray(inp["L0_attn_local_point"], f32).reshape(SCN, 128).T)
    w["bu1"] = A(np.asarray(inp["L0_ffn_bu1"], f32).reshape(UCN, 128).T)
    w["wu2"] = A(np.asarray(inp["L0_ffn_Wu2"], f32).reshape(UCN, 128).T)
    w["df"] = A(np.broadcast_to(np.asarray(inp["L0_attn_discount_factor"], f32),
                                (128, G)))
    w["scal"] = A(np.broadcast_to(np.stack([
        np.float32(inp["L0_ffn_gate_w1"]), np.float32(inp["L0_ffn_gate_bias"]),
        np.asarray(inp["L0_ffn_bu2"], f32).reshape(())]).astype(f32), (128, 3)))
    w["logrel"] = A(np.broadcast_to(np.asarray(inp["L0_ffn_log_rel"], f32),
                                    (128, NSLOT)))
    x = np.asarray(inp["L0_attn_hidden_states"], f32).reshape(TCORE * NCORES, H)
    return w, x


def kernel(upto=3, debug_xs=False, **inputs):
    w, x = prep_inputs(inputs)
    nc = build_program(upto=upto, debug_xs=debug_xs)
    in_maps = []
    for c in range(NCORES):
        m = dict(w)
        m["x0"] = np.ascontiguousarray(x[c * TCORE:(c + 1) * TCORE])
        in_maps.append(m)
    res = run_bass_kernel_spmd(nc, in_maps, core_ids=list(range(NCORES)))
    out = np.concatenate([res.results[c]["y"] for c in range(NCORES)], axis=0)
    extras = None
    if debug_xs:
        extras = {k: np.concatenate([res.results[c][k] for c in range(NCORES)], axis=0)
                  for k in ("x1", "x2")}
    out = out.reshape(2, TCORE * NCORES // 2, H)
    return (out, extras) if debug_xs else out


# version 09
# speedup vs baseline: 1.0017x; 1.0017x over previous
import numpy as np
from contextlib import ExitStack
import concourse.bass as bass
import concourse.mybir as mybir
from concourse.bass_utils import run_bass_kernel_spmd

F32 = mybir.dt.float32
BF16 = mybir.dt.bfloat16
U8 = mybir.dt.uint8
AF = mybir.ActivationFunctionType
ALU = mybir.AluOpType
AX = mybir.AxisListType

H = 4096; HC = 32
S = 1024; SCN = 8
FF = 9984; FCN = 78
G = 6
BOT = 64; RD = 48; VB = 256; UCN = 8
NSLOT = 1024
TCORE = 2048; TILE = 512; NT = 4; NSTT = 4
NSTG = 16
RSQ48 = 0.14433756729740643
RSQ64 = 0.125
NEGBIG = -30000.0
GELU_C = 0.7978845608028654
GELU_A = 0.044715
KB = 1024
NCORES = 8


def _blk_lhsT(w, oc=128):
    O, I = w.shape
    nO, nI = O // oc, I // 128
    b = w.reshape(nO, oc, nI, 128).transpose(0, 3, 2, 1)
    return np.ascontiguousarray(b).reshape(nO, 128, nI * oc)


class Sem:
    __slots__ = ("h", "n")

    def __init__(self, ctx, nc, name):
        self.h = ctx.enter_context(nc.semaphore(name))
        self.n = 0

    def inc(self, instr, k=1):
        instr.then_inc(self.h, k)
        self.n += k
        return self.n


class PsumStack:
    def __init__(self, nc):
        self.nc = nc
        self.stack = []
        self.k = 0

    def enter(self, shape, dt):
        self.k += 1
        cm = self.nc.psum_tensor(f"ps{self.k}", shape, dt)
        t = cm.__enter__()
        self.stack.append(cm)
        return t

    def exit(self, k=1):
        for _ in range(k):
            self.stack.pop().__exit__(None, None, None)


def build_program(upto=3, debug_xs=False):
    ctx = ExitStack()
    nc = bass.Bass("TRN2", target_bir_lowering=False)
    d = {}

    def din(name, shape, dt=F32):
        d[name] = nc.dram_tensor(name, list(shape), dt, kind="ExternalInput").ap()

    din("x0", (TCORE, H))
    din("gt", (FCN, 128, H)); din("ut", (FCN, 128, H)); din("dtw", (FCN, 128, H))
    din("upw", (SCN, 128, H)); din("gw", (SCN, 128, H))
    din("lao", (SCN, 128, H)); din("fuse", (G, SCN, 128, H))
    din("wq", (128, HC * 64)); din("wu1", (UCN, 128, H))
    din("wvd", (2, 128, H)); din("wvu", (2, 128, H)); din("auxv", (SCN, 128, H))
    din("akt", (BOT, NSLOT)); din("wrt", (BOT, RD))
    din("lp", (128, SCN)); din("bu1", (128, UCN)); din("wu2", (128, UCN))
    din("df", (128, G)); din("scal", (128, 3)); din("logrel", (128, NSLOT))

    okind = "ExternalOutput" if debug_xs else "Internal"
    d["x1"] = nc.dram_tensor("x1", [TCORE, H], F32, kind=okind).ap()
    d["x2"] = nc.dram_tensor("x2", [TCORE, H], F32, kind=okind).ap()
    d["y"] = nc.dram_tensor("y", [TCORE, H], F32, kind="ExternalOutput").ap()
    ymix = [nc.dram_tensor(f"ymix{i}", [TILE, H], F32, kind="Internal").ap()
            for i in range(2)]
    CW = {}
    for name, shape in [("gt", (FCN, 128, H)), ("ut", (FCN, 128, H)),
                        ("dtw", (FCN, 128, H)), ("upw", (SCN, 128, H)),
                        ("gw", (SCN, 128, H)), ("lao", (SCN, 128, H)),
                        ("fuse", (G, SCN, 128, H)), ("wu1", (UCN, 128, H)),
                        ("auxv", (SCN, 128, H)), ("wvd", (2, 128, H)),
                        ("wvu", (2, 128, H))]:
        CW[name] = nc.dram_tensor(name + "_c", list(shape), BF16, kind="Internal").ap()
    cc_in = [nc.dram_tensor(f"cc{i}_in", [1, 8], F32, kind="Internal") for i in (1, 2)]
    cc_out = [nc.dram_tensor(f"cc{i}_out", [1, 8], F32, kind="Internal") for i in (1, 2)]

    def sbt(name, shape, dt):
        return ctx.enter_context(nc.sbuf_tensor(name, list(shape), dt))

    ident = sbt("ident", [128, 128], BF16)
    ones_col = sbt("ones_col", [128, 1], F32)
    ones_row = sbt("ones_row", [1, 128], F32)
    lp_sb = sbt("lp_sb", [128, SCN], F32)
    bu1_sb = sbt("bu1_sb", [128, UCN], F32)
    wu2_sb = sbt("wu2_sb", [128, UCN], F32)
    df_sb = sbt("df_sb", [128, G], F32)
    ndf_sb = sbt("ndf_sb", [128, G], F32)
    scal_sb = sbt("scal_sb", [128, 3], F32)
    logrel_sb = sbt("logrel_sb", [128, NSLOT], F32)
    negbig_sb = sbt("negbig_sb", [128, NSLOT], F32)
    akt_sb = sbt("akt_sb", [BOT, NSLOT], F32)
    wrt_sb = sbt("wrt_sb", [BOT, RD], F32)
    rkt_sb = sbt("rkt_sb", [RD, NSLOT], F32)
    wq_sb = sbt("wq_sb", [128, HC * 64], BF16)
    stM = sbt("stM", [128, NSTG], F32)
    stR = sbt("stR", [128, NSTG], F32)
    stRh = sbt("stRh", [128, NSTG], F32)
    stA2 = sbt("stA2", [128, NSTG], F32)
    stB2 = sbt("stB2", [128, NSTG], F32)
    stL = sbt("stL", [128, NSTG], F32)
    ssA = sbt("ssA", [128, NSTG], F32)
    ssQ = sbt("ssQ", [128, NSTG], F32)
    rm = [sbt("rm1", [128, 1], F32), sbt("rm2", [128, 1], F32)]
    t8_sb = sbt("t8_sb", [128, 8], F32)
    sm1 = sbt("sm1", [128, 8], F32)
    sm2 = sbt("sm2", [128, 8], F32)
    sm3 = sbt("sm3", [128, 8], F32)
    learned_sb = sbt("learned_sb", [128, NSTT], F32)
    cc_sb = sbt("cc_sb", [1, 8], F32)
    w_stage = sbt("w_stage", [128, 4096], BF16)

    AB = 172 * KB
    arena = ctx.enter_context(nc.sbuf_tensor("arena", [128, AB], U8))
    a_f32 = arena.bitcast(F32).ap()
    a_bf = arena.bitcast(BF16).ap()
    a_u8 = arena.ap()

    def cf32(off_b, n):
        assert off_b % 4 == 0 and off_b + 4 * n <= AB, (off_b, n)
        return a_f32[:, off_b // 4: off_b // 4 + n]

    def cbf(off_b, n):
        assert off_b % 2 == 0 and off_b + 2 * n <= AB, (off_b, n)
        return a_bf[:, off_b // 2: off_b // 2 + n]

    def cu8(off_b, n):
        assert off_b + n <= AB
        return a_u8[:, off_b: off_b + n]

    hT = cbf(0, HC * TILE)
    hT3 = a_bf[:, 0:HC * TILE].rearrange("p (hc t) -> p hc t", t=TILE)
    R1 = 32 * KB
    tmpT = cbf(R1, 7 * SCN * TILE)
    aT = cbf(R1, FCN * TILE)
    aux1T = cbf(R1, HC * TILE)
    wT = cbf(R1 + 32 * KB, SCN * TILE)
    wT3 = a_bf[:, (R1 + 32 * KB) // 2:(R1 + 32 * KB) // 2 + SCN * TILE] \
        .rearrange("p (sc t) -> p sc t", t=TILE)
    v1T = cbf(R1 + 40 * KB, 2 * TILE)
    q_sb = cf32(R1 + 42 * KB, TILE)
    rq_sb = cf32(R1 + 44 * KB, TILE)
    m1g = cf32(R1 + 46 * KB, UCN * TILE)
    R_sb = cf32(R1 + 62 * KB, NSLOT)
    Sp_sb = cf32(R1 + 66 * KB, NSLOT)
    Sm_sb = cf32(R1 + 70 * KB, NSLOT)
    ex_sb = cf32(R1 + 74 * KB, NSLOT)
    mask_sb = cu8(R1 + 78 * KB, NSLOT)
    scr1k = cf32(R1 + 62 * KB, NSLOT)
    sxld = [cf32(R1, H), cf32(R1 + 16 * KB, H)]

    R2 = 112 * KB
    xld = [cf32(R2, H), cf32(R2 + 16 * KB, H)]
    h_bf = cbf(R2 + 32 * KB, H)
    wring = [cbf(R2 + i * 8 * KB, H) for i in range(4)]
    mx = [cf32(R2 + 32 * KB + i * 2 * KB, TILE) for i in range(6)]
    dring = [cbf(R2 + i * 2 * KB, 1024) for i in range(6)]
    evb = [cf32(R2 + 12 * KB + i * 4 * KB, 1024) for i in range(4)]
    xq = [cf32(R2 + 28 * KB + i * 4 * KB, 1024) for i in range(4)]
    yq = [cf32(R2 + 44 * KB + i * 4 * KB, 1024) for i in range(4)]
    ffring = [cbf(R2 + 16 * KB + i * 8 * KB, H) for i in range(3)]
    avring = [cbf(R2 + i * 2 * KB, 1024) for i in range(4)]
    wvd_sb = [cbf(R2, H), cbf(R2 + 8 * KB, H)]
    mx7 = cf32(R2 + 56 * KB, TILE)
    wvu7 = [cbf(R2, H), cbf(R2 + 8 * KB, H)]
    xq7 = [cf32(R2 + 16 * KB + i * 4 * KB, 1024) for i in range(4)]
    evb7 = [cf32(R2 + 32 * KB + i * 4 * KB, 1024) for i in range(4)]
    xq7w = [cf32(R2 + 16 * KB, 2048), cf32(R2 + 24 * KB, 2048)]
    evb7w = [cf32(R2 + 32 * KB, 2048), cf32(R2 + 40 * KB, 2048)]
    evbw = cf32(R2 + 12 * KB, 4096)
    xqw = cf32(R2 + 28 * KB, 4096)
    yqw = cf32(R2 + 44 * KB, 4096)
    gz = cf32(R2 + 48 * KB, TILE)
    gt1 = cf32(R2 + 50 * KB, TILE)
    gt2 = cf32(R2 + 52 * KB, TILE)

    sx = Sem(ctx, nc, "sx")
    sst = Sem(ctx, nc, "sst")
    si = Sem(ctx, nc, "si")
    swq = Sem(ctx, nc, "swq")
    scx = Sem(ctx, nc, "scx")
    sx2 = [Sem(ctx, nc, f"sx2_{i}") for i in range(2)]
    swr = [Sem(ctx, nc, f"swr{i}") for i in range(4)]
    sdr = [Sem(ctx, nc, f"sdr{i}") for i in range(6)]
    sfr = [Sem(ctx, nc, f"sfr{i}") for i in range(3)]
    sqy = [Sem(ctx, nc, f"sqy{i}") for i in range(4)]
    swb = [Sem(ctx, nc, f"swb{i}") for i in range(4)]
    sdwb = [Sem(ctx, nc, f"sdwb{i}") for i in range(6)]
    sfwb = [Sem(ctx, nc, f"sfwb{i}") for i in range(3)]
    sw = Sem(ctx, nc, "sw")
    sw2 = Sem(ctx, nc, "sw2")
    sp = Sem(ctx, nc, "sp")
    sv = Sem(ctx, nc, "sv")
    sa = Sem(ctx, nc, "sa")
    sg = Sem(ctx, nc, "sg")
    scc = Sem(ctx, nc, "scc")

    block = ctx.enter_context(nc.Block())
    pst = PsumStack(nc)
    gd = {"pe": 0, "sv": 0, "sa": 0, "st": 0}

    def init_sync(s):
        for name, dst in [("lp", lp_sb), ("bu1", bu1_sb), ("wu2", wu2_sb),
                          ("df", df_sb), ("scal", scal_sb), ("logrel", logrel_sb),
                          ("akt", akt_sb), ("wrt", wrt_sb)]:
            si.inc(s.dma_start(dst[:, :], d[name]), 16)
    block.sync(init_sync)
    n_init_loads = si.n

    def init_g(g):
        sg.inc(g.memset(negbig_sb[:, :], NEGBIG))
        sg.inc(g.memset(ident[:, :], 0.0))
        g.wait_ge(sg.h, sg.n)
        sg.inc(g.affine_select(out=ident[:, :], in_=ident[:, :],
                               compare_op=ALU.not_equal, fill=1.0, base=0,
                               pattern=[[-1, 128]], channel_multiplier=1))
        sg.inc(g.memset(ones_col[:, :], 1.0))
        sg.inc(g.memset(ones_row[:, :], 1.0))
        swq.inc(g.dma_start(wq_sb[:, :], d["wq"]), 16)
    block.gpsimd(init_g)

    def init_v(v):
        v.wait_ge(si.h, n_init_loads)
        sv.inc(v.tensor_scalar(ndf_sb[:, :], df_sb[:, :], -1.0, None, op0=ALU.mult))
    block.vector(init_v)

    psrk_t = pst.enter([128, NSLOT], F32)

    def init_t(t):
        t.wait_ge(si.h, n_init_loads)
        t.matmul(psrk_t[0:RD, 0:512], wrt_sb[:, :], akt_sb[:, 0:512], start=True, stop=True)
        sp.inc(t.matmul(psrk_t[0:RD, 512:1024], wrt_sb[:, :], akt_sb[:, 512:1024],
                        start=True, stop=True))
    block.tensor(init_t)
    pe_rk = sp.n

    def init_v2(v):
        v.wait_ge(sp.h, pe_rk)
        sv.inc(v.tensor_copy(rkt_sb[:, :], psrk_t[0:RD, :]))
    block.vector(init_v2)
    pst.exit(1)
    gd.update(pe=sp.n, sv=sv.n, sa=sa.n, st=sst.n)

    def emit_stats(src, cc_idx):
        sxb = [s.n for s in sx2]
        svb = sv.n
        sab = sa.n
        st_guard = sst.n
        g_pe, g_sv, g_sa = gd["pe"], gd["sv"], gd["sa"]
        grp_tok = []

        def floads(s):
            s.wait_ge(sst.h, st_guard)
            s.wait_ge(sp.h, g_pe)
            s.wait_ge(sv.h, g_sv)
            s.wait_ge(sa.h, g_sa)
            for st in range(NSTG):
                if st >= 2:
                    s.wait_ge(sa.h, sab + (st - 1))
                    s.wait_ge(sv.h, svb + 2 * (st - 1))
                sx2[st % 2].inc(s.dma_start(sxld[st % 2],
                                            src[st * 128:(st + 1) * 128, :]), 16)
        block.sync(floads)

        def fsq(sc):
            sc.wait_ge(sp.h, g_pe)
            sc.wait_ge(sv.h, g_sv)
            for st in range(NSTG):
                sc.wait_ge(sx2[st % 2].h, sxb[st % 2] + 16 * (st // 2 + 1))
                if st >= 1:
                    sc.wait_ge(sv.h, svb + 2 * st)
                for k in range(4):
                    i = sc.activation(scr1k, sxld[st % 2][:, k * 1024:(k + 1) * 1024],
                                      AF.Square, accum_out=sm1[:, k:k + 1])
                sa.inc(i)
        block.scalar(fsq)
        assert sa.n == sab + NSTG

        def fred(v):
            for st in range(NSTG):
                v.wait_ge(sx2[st % 2].h, sxb[st % 2] + 16 * (st // 2 + 1))
                v.wait_ge(sv.h, sv.n)
                sv.inc(v.reduce_sum(ssA[:, st:st + 1], sxld[st % 2][:, :], axis=AX.X))
                v.wait_ge(sa.h, sab + st + 1)
                v.wait_ge(sv.h, sv.n)
                sv.inc(v.reduce_sum(ssQ[:, st:st + 1], sm1[:, 0:4], axis=AX.X))
        block.vector(fred)
        assert sv.n == svb + 2 * NSTG

        for g_ in range(4):
            c = slice(4 * g_, 4 * g_ + 4)
            svg = svb + 2 * (4 * g_ + 4)

            def ffin(v, c=c, svg=svg):
                def w():
                    v.wait_ge(sv.h, sv.n)
                v.wait_ge(sv.h, svg)
                w(); sv.inc(v.tensor_scalar(stM[:, c], ssA[:, c], 1.0 / H, None, op0=ALU.mult))
                sv.inc(v.tensor_scalar(ssQ[:, c], ssQ[:, c], 1.0 / H, None, op0=ALU.mult))
                w(); sv.inc(v.tensor_mul(ssA[:, c], stM[:, c], stM[:, c]))
                w(); sv.inc(v.tensor_sub(ssQ[:, c], ssQ[:, c], ssA[:, c]))
                w(); sv.inc(v.tensor_scalar(ssA[:, c], ssQ[:, c], 1e-5, None, op0=ALU.add))
                w(); sv.inc(v.reciprocal(stR[:, c], ssA[:, c]))
                w(); sv.inc(v.tensor_mul(stL[:, c], ssQ[:, c], stR[:, c]))
            block.vector(ffin)
            svn = sv.n

            def ffin_a(sc, c=c, svn=svn):
                sc.wait_ge(sv.h, svn)
                sc.wait_ge(sa.h, sa.n)
                sa.inc(sc.activation(ssQ[:, c], ssA[:, c], AF.Ln))
                sc.wait_ge(sa.h, sa.n)
                sa.inc(sc.activation(stR[:, c], ssQ[:, c], AF.Exp, scale=-0.5))
                sc.wait_ge(sa.h, sa.n)
                sa.inc(sc.activation(stL[:, c], stL[:, c], AF.Ln, bias=1.0))
            block.scalar(ffin_a)
            san = sa.n

            def ffin_v(v, c=c, san=san):
                v.wait_ge(sa.h, san)
                v.wait_ge(sv.h, sv.n)
                sv.inc(v.tensor_scalar(stRh[:, c], stR[:, c], 0.5, None, op0=ALU.mult))
                v.wait_ge(sv.h, sv.n)
                sv.inc(v.tensor_scalar(stA2[:, c], stRh[:, c], 1.0, None, op0=ALU.add))
                sv.inc(v.tensor_mul(stB2[:, c], stM[:, c], stRh[:, c]))
                v.wait_ge(sv.h, sv.n)
                sv.inc(v.tensor_scalar(stB2[:, c], stB2[:, c], -1.0, None, op0=ALU.mult))
            block.vector(ffin_v)
            grp_tok.append((sv.n, sa.n))

        def emit_cc(guard_sv=0):
            pslv_t = pst.enter([128, NSTG], F32)
            psbc = pst.enter([128, 1], F32)
            svn2, san2 = grp_tok[-1]

            def flv(t):
                t.wait_ge(sa.h, san2)
                t.wait_ge(sv.h, max(g_sv, guard_sv))
                sp.inc(t.matmul(pslv_t[0:1, :], ones_col[:, :], stL[:, :],
                                start=True, stop=True))
            block.tensor(flv)
            pen = sp.n

            def flv_v(v):
                v.wait_ge(sp.h, pen)
                v.wait_ge(sv.h, sv.n)
                sv.inc(v.reduce_sum(cc_sb[0:1, 0:1], pslv_t[0:1, :], axis=AX.X))
            block.vector(flv_v)
            svn3 = sv.n

            def fst(s):
                s.wait_ge(sv.h, svn3)
                sst.inc(s.dma_start(cc_in[cc_idx].ap()[0:1, 0:1], cc_sb[0:1, 0:1]), 16)
            block.sync(fst)
            stn = sst.n

            def fcc(g):
                g.wait_ge(sst.h, stn)
                scc.inc(g.collective_compute(
                    "AllReduce", ALU.add, replica_groups=[list(range(NCORES))],
                    ins=[cc_in[cc_idx].ap().opt()],
                    outs=[cc_out[cc_idx].ap().opt()]), 1)
            block.gpsimd(fcc)
            ccn = scc.n

            def fld(s):
                s.wait_ge(scc.h, ccn)
                scx.inc(s.dma_start(cc_sb[0:1, 1:2], cc_out[cc_idx].ap()[0:1, 0:1]), 16)
            block.sync(fld)
            sxn = scx.n

            def fmn(v):
                v.wait_ge(scx.h, sxn)
                v.wait_ge(sv.h, sv.n)
                sv.inc(v.tensor_scalar(cc_sb[0:1, 2:3], cc_sb[0:1, 1:2],
                                       1.0 / 16384.0, 1e-6, op0=ALU.mult, op1=ALU.add))
                v.wait_ge(sv.h, sv.n)
                sv.inc(v.reciprocal(cc_sb[0:1, 3:4], cc_sb[0:1, 2:3]))
            block.vector(fmn)
            svn4 = sv.n

            def fbc(t):
                t.wait_ge(sv.h, svn4)
                sp.inc(t.matmul(psbc[:, :], ones_row[:, :], cc_sb[0:1, 3:4],
                                start=True, stop=True))
            block.tensor(fbc)
            pen2 = sp.n

            def fbc_v(v):
                v.wait_ge(sp.h, pen2)
                v.wait_ge(sv.h, sv.n)
                sv.inc(v.tensor_copy(rm[cc_idx][:, :], psbc[:, :]))
            block.vector(fbc_v)
            pst.exit(2)
            gd["rm_sv"] = sv.n
        return grp_tok, (emit_cc if cc_idx is not None else None)

    def emit_prologue(src, tile, grp=None):
        sxb = [s.n for s in sx2]
        svb = sv.n
        sab = sa.n
        peb = sp.n
        g_pe, g_sv, g_sa = gd["pe"], gd["sv"], gd["sa"]

        def floads(s):
            s.wait_ge(sp.h, g_pe)
            s.wait_ge(sv.h, g_sv)
            s.wait_ge(sa.h, g_sa)
            s.wait_ge(sst.h, gd["st"])
            for st in range(NSTT):
                if st >= 2:
                    s.wait_ge(sv.h, svb + (st - 1))
                r0 = tile * TILE + st * 128
                sx2[st % 2].inc(s.dma_start(xld[st % 2], src[r0:r0 + 128, :]), 16)
        block.sync(floads)

        def fh(v):
            if grp is not None:
                v.wait_ge(sv.h, grp[0])
                v.wait_ge(sa.h, grp[1])
            for st in range(NSTT):
                gix = tile * NSTT + st
                v.wait_ge(sx2[st % 2].h, sxb[st % 2] + 16 * (st // 2 + 1))
                if st > 0:
                    v.wait_ge(sp.h, peb + st * 4)
                v.wait_ge(sv.h, sv.n)
                sv.inc(v.tensor_scalar(h_bf[:, :], xld[st % 2][:, :],
                                       stM[:, gix:gix + 1], stR[:, gix:gix + 1],
                                       op0=ALU.subtract, op1=ALU.mult))
        block.vector(fh)
        assert sv.n == svb + NSTT

        trp = [pst.enter([128, 1024], BF16), pst.enter([128, 1024], BF16)]

        def ftr(t):
            t.wait_ge(sv.h, g_sv)
            t.wait_ge(sa.h, g_sa)
            for st in range(NSTT):
                t.wait_ge(sv.h, svb + st + 1)
                for r in range(4):
                    gidx = st * 4 + r
                    pb = trp[gidx % 2]
                    if gidx >= 2:
                        t.wait_ge(sa.h, sab + gidx - 1)
                    for k in range(8):
                        hc = r * 8 + k
                        i = t.transpose(pb[:, k * 128:(k + 1) * 128],
                                        h_bf[:, hc * 128:(hc + 1) * 128], ident[:, :])
                    sp.inc(i)
        block.tensor(ftr)
        assert sp.n == peb + 16

        def fev(sc):
            sc.wait_ge(sp.h, g_pe)
            for st in range(NSTT):
                for r in range(4):
                    gidx = st * 4 + r
                    sc.wait_ge(sp.h, peb + gidx + 1)
                    pb3 = trp[gidx % 2].ap().rearrange("p (a b) -> p a b", b=128)
                    dst3 = hT3[:, r * 8:r * 8 + 8, st * 128:(st + 1) * 128]
                    sa.inc(sc.copy(dst3, pb3))
        block.scalar(fev)
        assert sa.n == sab + 16
        pst.exit(2)
        return {"hT_sa": sa.n, "xld_sv": svb + NSTT, "pe_end": peb + 16}

    def emit_attn_tile(src, dst, tile, grp=None):
        st_entry = sst.n
        pro = emit_prologue(src, tile, grp)
        hT_ready = pro["hT_sa"]

        psP = [pst.enter([128, 1024], F32), pst.enter([128, 1024], F32)]
        wrb = [s.n for s in swr]
        peb = sp.n
        svb = sv.n
        sab = sa.n
        DV = 17
        AC = 7

        build = (tile == 0)
        wbb = [s.n for s in swb]
        wrb_pre = [s.n for s in swr]

        def m1g_(g):
            g.wait_ge(sv.h, pro["xld_sv"])
            g.wait_ge(sst.h, st_entry)
            if not build:
                for k in range(4):
                    g.wait_ge(swb[k].h, wbb[k])
            for sc_i in range(SCN):
                a2 = 2 * (sc_i % 2)
                if sc_i >= 2:
                    g.wait_ge(sp.h, peb + 2 * (sc_i - 1))
                    if build:
                        g.wait_ge(swb[a2].h, wbb[a2] + 16 * (sc_i // 2))
                        g.wait_ge(swb[a2 + 1].h, wbb[a2 + 1] + 16 * (sc_i // 2))
                su, sg_ = (d["upw"], d["gw"]) if build else (CW["upw"], CW["gw"])
                swr[a2].inc(g.dma_start(wring[a2], su[sc_i]), 16)
                swr[a2 + 1].inc(g.dma_start(wring[a2 + 1], sg_[sc_i]), 16)
        block.gpsimd(m1g_)

        if build:
            def m1wb(s):
                for sc_i in range(SCN):
                    a2 = 2 * (sc_i % 2)
                    s.wait_ge(swr[a2].h, wrb_pre[a2] + 16 * (sc_i // 2 + 1))
                    swb[a2].inc(s.dma_start(CW["upw"][sc_i], wring[a2]), 16)
                    s.wait_ge(swr[a2 + 1].h, wrb_pre[a2 + 1] + 16 * (sc_i // 2 + 1))
                    swb[a2 + 1].inc(s.dma_start(CW["gw"][sc_i], wring[a2 + 1]), 16)
            block.sync(m1wb)

        def m1t(t):
            t.wait_ge(sa.h, hT_ready)
            for sc_i in range(SCN):
                a2 = 2 * (sc_i % 2)
                t.wait_ge(swr[a2].h, wrb[a2] + 16 * (sc_i // 2 + 1))
                t.wait_ge(swr[a2 + 1].h, wrb[a2 + 1] + 16 * (sc_i // 2 + 1))
                if sc_i >= 2:
                    t.wait_ge(sv.h, svb + DV * (sc_i - 2) + 3)
                pA = psP[sc_i % 2][:, 0:512]
                pB = psP[sc_i % 2][:, 512:1024]
                for hc in range(HC):
                    i = t.matmul(pA, wring[2 * (sc_i % 2)][:, hc * 128:(hc + 1) * 128],
                                 hT[:, hc * TILE:hc * TILE + TILE],
                                 start=(hc == 0), stop=(hc == HC - 1))
                sp.inc(i)
                for hc in range(HC):
                    i = t.matmul(pB, wring[2 * (sc_i % 2) + 1][:, hc * 128:(hc + 1) * 128],
                                 hT[:, hc * TILE:hc * TILE + TILE],
                                 start=(hc == 0), stop=(hc == HC - 1))
                sp.inc(i)
        block.tensor(m1t)
        assert sp.n == peb + 2 * SCN

        def m1a(sc):
            for sc_i in range(SCN):
                base_v = svb + DV * sc_i
                sc.wait_ge(sv.h, base_v + 5)
                sc.wait_ge(sa.h, sa.n)
                sa.inc(sc.activation(mx[4], mx[2], AF.Ln))
                for m in range(1, G + 1):
                    dt_slot = mx[5] if m % 2 == 1 else mx[2]
                    if m <= 2:
                        sc.wait_ge(sv.h, base_v + 9)
                    else:
                        tmp_idx = {1: 10, 2: 11, 3: 12, 4: 13}[m - 2]
                        sc.wait_ge(sv.h, base_v + tmp_idx)
                    sc.wait_ge(sa.h, sa.n)
                    sa.inc(sc.activation(dt_slot, mx[4], AF.Exp, scale=float(m),
                                         bias=ndf_sb[:, m - 1:m]))
        block.scalar(m1a)
        assert sa.n == sab + AC * SCN

        def m1v(v):
            for sc_i in range(SCN):
                base_a = sab + AC * sc_i

                def w():
                    v.wait_ge(sv.h, sv.n)
                pA = psP[sc_i % 2][:, 0:512]
                pB = psP[sc_i % 2][:, 512:1024]
                v.wait_ge(sp.h, peb + 2 * sc_i + 1)
                w()
                sv.inc(v.tensor_copy(tmpT[:, sc_i * TILE:sc_i * TILE + TILE], pA))
                sv.inc(v.tensor_copy(mx[0], pA))
                v.wait_ge(sp.h, peb + 2 * sc_i + 2)
                sv.inc(v.tensor_scalar(mx[1], pB, lp_sb[:, sc_i:sc_i + 1], None,
                                       op0=ALU.subtract))
                if sc_i >= 1:
                    v.wait_ge(sa.h, sab + AC * (sc_i - 1) + 6)
                w()
                sv.inc(v.tensor_scalar(mx[5], mx[1], -1.0, None, op0=ALU.mult))
                if sc_i >= 1:
                    v.wait_ge(sa.h, sab + AC * sc_i)
                w()
                sv.inc(v.tensor_max(mx[2], mx[1], mx[5]))
                w()
                sv.inc(v.tensor_scalar(mx[3], mx[1], 0.0, None, op0=ALU.is_gt))
                w()
                sv.inc(v.tensor_scalar(mx[5], mx[3], 2.0, -1.0, op0=ALU.mult,
                                       op1=ALU.add))
                w()
                sv.inc(v.tensor_scalar(mx[3], mx[2], 2.5, None, op0=ALU.is_le))
                w()
                sv.inc(v.tensor_mul(mx[1], mx[0], mx[5]))
                nops = 9
                for m in range(1, G + 1):
                    dt_slot = mx[5] if m % 2 == 1 else mx[2]
                    src_t = mx[1] if m % 2 == 1 else mx[0]
                    slot = tmpT[:, (m * SCN + sc_i) * TILE:(m * SCN + sc_i) * TILE + TILE]
                    v.wait_ge(sa.h, base_a + 1 + m)
                    w()
                    if m <= 4:
                        sv.inc(v.tensor_mul(slot, dt_slot, src_t)); nops += 1
                    else:
                        sv.inc(v.tensor_mul(mx7, dt_slot, src_t))
                        w()
                        sv.inc(v.tensor_mul(slot, mx7, mx[3]))
                        nops += 2
                assert nops == DV
        block.vector(m1v)
        assert sv.n == svb + DV * SCN
        pst.exit(2)
        m1_pe_end = sp.n
        m1_sv_end = sv.n

        psY = pst.enter([128, 4096], F32)
        NTL = 7 * SCN
        sdb = [s.n for s in sdr]
        peb2 = sp.n
        svb2 = sv.n
        stb2 = sst.n

        dwbb = [s.n for s in sdwb]
        sdb_pre = [s.n for s in sdr]

        def _m2src(j, hq, cached):
            term = j // SCN
            sc_i = j % SCN
            base = (CW if cached else d)
            if term == 0:
                return base["lao"][sc_i, :, hq * 1024:(hq + 1) * 1024]
            return base["fuse"][term - 1, sc_i, :, hq * 1024:(hq + 1) * 1024]

        def m2g_(g):
            g.wait_ge(sp.h, m1_pe_end)
            if not build:
                for k in range(6):
                    g.wait_ge(sdwb[k].h, dwbb[k])
            for hq in range(4):
                for j in range(NTL):
                    gi = hq * NTL + j
                    if gi >= 6:
                        g.wait_ge(sp.h, peb2 + gi - 5)
                        if build:
                            g.wait_ge(sdwb[gi % 6].h, dwbb[gi % 6] + 16 * (gi // 6))
                    sdr[gi % 6].inc(g.dma_start(dring[gi % 6],
                                                _m2src(j, hq, not build)), 16)
        block.gpsimd(m2g_)



        def m2t(t):
            t.wait_ge(sv.h, m1_sv_end)
            for hq in range(4):
                if hq > 0:
                    t.wait_ge(sv.h, svb2 + hq)
                for j in range(NTL):
                    gi = hq * NTL + j
                    t.wait_ge(sdr[gi % 6].h, sdb[gi % 6] + 16 * (gi // 6 + 1))
                    term = j // SCN
                    sc_i = j % SCN
                    for st in range(NSTT):
                        lhs = tmpT[:, (term * SCN + sc_i) * TILE + st * 128:
                                   (term * SCN + sc_i) * TILE + st * 128 + 128]
                        for half in range(2):
                            i = t.matmul(psY[:, st * 1024 + half * 512:
                                             st * 1024 + half * 512 + 512],
                                         lhs, dring[gi % 6][:, half * 512:half * 512 + 512],
                                         start=(j == 0), stop=(j == NTL - 1),
                                         skip_group_check=True)
                    sp.inc(i)
        block.tensor(m2t)
        assert sp.n == peb2 + 4 * NTL

        def m2v(v):
            v.wait_ge(sst.h, st_entry)
            for hq in range(4):
                v.wait_ge(sp.h, peb2 + NTL * (hq + 1))
                if hq >= 1:
                    v.wait_ge(sst.h, stb2 + 64 * hq)
                v.wait_ge(sv.h, sv.n)
                sv.inc(v.tensor_copy(evbw, psY[:, :]))
        block.vector(m2v)
        assert sv.n == svb2 + 4

        def m2s(s):
            for hq in range(4):
                if build:
                    for j in range(NTL):
                        gi = hq * NTL + j
                        s.wait_ge(sdr[gi % 6].h, sdb_pre[gi % 6] + 16 * (gi // 6 + 1))
                        sdwb[gi % 6].inc(s.dma_start(_m2src(j, hq, True),
                                                     dring[gi % 6]), 16)
                s.wait_ge(sv.h, svb2 + hq + 1)
                for st in range(NSTT):
                    sst.inc(s.dma_start(
                        ymix[tile % 2][st * 128:(st + 1) * 128,
                                       hq * 1024:(hq + 1) * 1024], evb[st]), 16)
        block.sync(m2s)
        pst.exit(1)
        m2_pe_end = sp.n
        m2_sv_end = sv.n
        m2_st_end = sst.n

        psP2 = [pst.enter([128, 1024], F32), pst.enter([128, 1024], F32)]
        wrb3 = [s.n for s in swr]
        peb3 = sp.n
        svb3 = sv.n
        sab3 = sa.n

        wbb3 = [s.n for s in swb]
        wrb3_pre = [s.n for s in swr]

        def t1g(g):
            g.wait_ge(sp.h, m2_pe_end)
            g.wait_ge(sst.h, m2_st_end)
            if not build:
                for k in range(4):
                    g.wait_ge(swb[k].h, wbb3[k])
            for fc in range(FCN):
                a2 = 2 * (fc % 2)
                if fc >= 2:
                    g.wait_ge(sv.h, svb3 + (fc - 1))
                    if build:
                        g.wait_ge(swb[a2].h, wbb3[a2] + 16 * (fc // 2))
                        g.wait_ge(swb[a2 + 1].h, wbb3[a2 + 1] + 16 * (fc // 2))
                sgt, sut = (d["gt"], d["ut"]) if build else (CW["gt"], CW["ut"])
                swr[a2].inc(g.dma_start(wring[a2], sgt[fc]), 16)
                swr[a2 + 1].inc(g.dma_start(wring[a2 + 1], sut[fc]), 16)
        block.gpsimd(t1g)

        if build:
            def t1wb(s):
                for fc in range(FCN):
                    a2 = 2 * (fc % 2)
                    s.wait_ge(swr[a2].h, wrb3_pre[a2] + 16 * (fc // 2 + 1))
                    swb[a2].inc(s.dma_start(CW["gt"][fc], wring[a2]), 16)
                    s.wait_ge(swr[a2 + 1].h, wrb3_pre[a2 + 1] + 16 * (fc // 2 + 1))
                    swb[a2 + 1].inc(s.dma_start(CW["ut"][fc], wring[a2 + 1]), 16)
            block.sync(t1wb)

        def t1t(t):
            t.wait_ge(sv.h, m2_sv_end)
            for fc in range(FCN):
                a2 = 2 * (fc % 2)
                t.wait_ge(swr[a2].h, wrb3[a2] + 16 * (fc // 2 + 1))
                t.wait_ge(swr[a2 + 1].h, wrb3[a2 + 1] + 16 * (fc // 2 + 1))
                if fc >= 2:
                    t.wait_ge(sv.h, svb3 + (fc - 1))
                pA = psP2[fc % 2][:, 0:512]
                pB = psP2[fc % 2][:, 512:1024]
                for hc in range(HC):
                    i = t.matmul(pA, wring[2 * (fc % 2)][:, hc * 128:(hc + 1) * 128],
                                 hT[:, hc * TILE:hc * TILE + TILE],
                                 start=(hc == 0), stop=(hc == HC - 1))
                sp.inc(i)
                for hc in range(HC):
                    i = t.matmul(pB, wring[2 * (fc % 2) + 1][:, hc * 128:(hc + 1) * 128],
                                 hT[:, hc * TILE:hc * TILE + TILE],
                                 start=(hc == 0), stop=(hc == HC - 1))
                sp.inc(i)
        block.tensor(t1t)
        assert sp.n == peb3 + 2 * FCN

        def t1a(sc):
            for fc in range(FCN):
                sc.wait_ge(sp.h, peb3 + 2 * fc + 1)
                if fc >= 2:
                    sc.wait_ge(sv.h, svb3 + (fc - 1))
                sc.wait_ge(sa.h, sa.n)
                sa.inc(sc.activation(mx[fc % 2], psP2[fc % 2][:, 0:512], AF.Silu))
        block.scalar(t1a)
        assert sa.n == sab3 + FCN

        def t1v(v):
            for fc in range(FCN):
                v.wait_ge(sa.h, sab3 + fc + 1)
                v.wait_ge(sp.h, peb3 + 2 * fc + 2)
                v.wait_ge(sv.h, sv.n)
                sv.inc(v.tensor_mul(aT[:, fc * TILE:(fc + 1) * TILE], mx[fc % 2],
                                    psP2[fc % 2][:, 512:1024]))
        block.vector(t1v)
        assert sv.n == svb3 + FCN
        pst.exit(2)
        t1_pe_end = sp.n
        t1_sv_end = sv.n

        psY2 = pst.enter([128, 4096], F32)
        sdb4 = [s.n for s in sdr]
        peb4 = sp.n
        svb4 = sv.n
        stb4 = sst.n
        sqb4 = [s.n for s in sqy]

        dwbb4 = [s.n for s in sdwb]
        sdb4_pre = [s.n for s in sdr]

        def t2g(g):
            g.wait_ge(sp.h, t1_pe_end)
            if not build:
                for k in range(6):
                    g.wait_ge(sdwb[k].h, dwbb4[k])
            for hq in range(4):
                for fc in range(FCN):
                    gi = hq * FCN + fc
                    if gi >= 6:
                        g.wait_ge(sp.h, peb4 + gi - 5)
                        if build:
                            g.wait_ge(sdwb[gi % 6].h, dwbb4[gi % 6] + 16 * (gi // 6))
                    sdt = d["dtw"] if build else CW["dtw"]
                    sdr[gi % 6].inc(g.dma_start(
                        dring[gi % 6], sdt[fc, :, hq * 1024:(hq + 1) * 1024]), 16)
        block.gpsimd(t2g)



        def t2s(s):
            s.wait_ge(sv.h, t1_sv_end)
            s.wait_ge(sst.h, m2_st_end)
            for hq in range(4):
                if hq >= 1:
                    s.wait_ge(sst.h, stb4 + 64 * hq)
                if hq >= 1:
                    s.wait_ge(sv.h, svb4 + 2 * hq)
                for st in range(NSTT):
                    r0 = tile * TILE + st * 128
                    sqy[st].inc(s.dma_start(xq[st], src[r0:r0 + 128,
                                                        hq * 1024:(hq + 1) * 1024]), 16)
                    sqy[st].inc(s.dma_start(yq[st],
                                            ymix[tile % 2][st * 128:(st + 1) * 128,
                                                           hq * 1024:(hq + 1) * 1024]), 16)
                if build:
                    for fc in range(FCN):
                        gi = hq * FCN + fc
                        s.wait_ge(sdr[gi % 6].h, sdb4_pre[gi % 6] + 16 * (gi // 6 + 1))
                        sdwb[gi % 6].inc(s.dma_start(
                            CW["dtw"][fc, :, hq * 1024:(hq + 1) * 1024],
                            dring[gi % 6]), 16)
                s.wait_ge(sv.h, svb4 + 2 * (hq + 1))
                for st in range(NSTT):
                    r0 = tile * TILE + st * 128
                    sst.inc(s.dma_start(dst[r0:r0 + 128, hq * 1024:(hq + 1) * 1024],
                                        yq[st]), 16)
        block.sync(t2s)

        def t2t(t):
            t.wait_ge(sv.h, t1_sv_end)
            for hq in range(4):
                if hq > 0:
                    t.wait_ge(sv.h, svb4 + 2 * (hq - 1) + 1)
                for fc in range(FCN):
                    gi = hq * FCN + fc
                    t.wait_ge(sdr[gi % 6].h, sdb4[gi % 6] + 16 * (gi // 6 + 1))
                    for st in range(NSTT):
                        lhs = aT[:, fc * TILE + st * 128:fc * TILE + st * 128 + 128]
                        for half in range(2):
                            i = t.matmul(psY2[:, st * 1024 + half * 512:
                                              st * 1024 + half * 512 + 512],
                                         lhs, dring[gi % 6][:, half * 512:half * 512 + 512],
                                         start=(fc == 0), stop=(fc == FCN - 1),
                                         skip_group_check=True)
                    sp.inc(i)
        block.tensor(t2t)
        assert sp.n == peb4 + 4 * FCN

        def t2v(v):
            v.wait_ge(sst.h, m2_st_end)
            for hq in range(4):
                v.wait_ge(sp.h, peb4 + FCN * (hq + 1))
                for st in range(NSTT):
                    v.wait_ge(sqy[st].h, sqb4[st] + 32 * (hq + 1))
                v.wait_ge(sv.h, sv.n)
                sv.inc(v.tensor_add(evbw, psY2[:, :], yqw))
                v.wait_ge(sv.h, sv.n)
                sv.inc(v.tensor_add(yqw, evbw, xqw))
        block.vector(t2v)
        assert sv.n == svb4 + 8
        pst.exit(1)
        gd.update(pe=sp.n, sv=sv.n, sa=sa.n, st=sst.n)

    def emit_ffn_tile(src, dst, tile, rm_idx, grp=None, cc_emit=None):
        st_entry = sst.n
        build = (rm_idx == 0 and tile == 0)
        pro = emit_prologue(src, tile, grp)
        hT_ready = pro["hT_sa"]

        psQ_t = pst.enter([128, TILE], F32)
        psQ = psQ_t[0:BOT, :]
        psRQ_t = pst.enter([128, TILE], F32)
        psRQ = psRQ_t[0:RD, :]

        def f1t(t):
            t.wait_ge(sa.h, hT_ready)
            t.wait_ge(swq.h, 16)
            for hc in range(HC):
                i = t.matmul(psQ[:, :], wq_sb[:, hc * 64:(hc + 1) * 64],
                             hT[:, hc * TILE:hc * TILE + TILE],
                             start=(hc == 0), stop=(hc == HC - 1))
            sp.inc(i)
        block.tensor(f1t)
        pe_q = sp.n

        def f1v(v):
            v.wait_ge(sp.h, pe_q)
            v.wait_ge(sv.h, sv.n)
            sv.inc(v.tensor_copy(q_sb[0:BOT, :], psQ[:, :]))
        block.vector(f1v)
        sv_q = sv.n

        def f1t2(t):
            t.wait_ge(sv.h, sv_q)
            sp.inc(t.matmul(psRQ[:, :], wrt_sb[:, :], q_sb[0:BOT, :],
                            start=True, stop=True))
        block.tensor(f1t2)
        pe_rq = sp.n

        def f1v2(v):
            v.wait_ge(sp.h, pe_rq)
            v.wait_ge(sv.h, sv.n)
            sv.inc(v.tensor_copy(rq_sb[0:RD, :], psRQ[:, :]))
        block.vector(f1v2)
        sv_rq = sv.n
        pst.exit(2)

        psR = pst.enter([128, NSLOT], F32)
        psS = pst.enter([128, NSLOT], F32)
        psM = [pst.enter([128, TILE], F32), pst.enter([128, TILE], F32)]
        trpF = [pst.enter([128, 1024], BF16), pst.enter([128, 1024], BF16)]

        peb_rs = sp.n
        svb_sm = sv.n
        sab_sm = sa.n
        SMV = 8

        def f2t(t):
            t.wait_ge(sv.h, sv_rq)
            for st in range(NSTT):
                if st > 0:
                    t.wait_ge(sv.h, svb_sm + SMV * (st - 1) + 4)
                for half in range(2):
                    t.matmul(psR[:, half * 512:(half + 1) * 512],
                             rq_sb[0:RD, st * 128:(st + 1) * 128],
                             rkt_sb[:, half * 512:(half + 1) * 512],
                             start=True, stop=True)
                for half in range(2):
                    i = t.matmul(psS[:, half * 512:(half + 1) * 512],
                                 q_sb[0:BOT, st * 128:(st + 1) * 128],
                                 akt_sb[:, half * 512:(half + 1) * 512],
                                 start=True, stop=True)
                sp.inc(i)
        block.tensor(f2t)
        assert sp.n == peb_rs + NSTT

        def f2a(sc):
            for st in range(NSTT):
                sc.wait_ge(sv.h, svb_sm + SMV * st + 6)
                sc.wait_ge(sa.h, sa.n)
                sa.inc(sc.activation(ex_sb, Sm_sb, AF.Exp, bias=sm1[:, 0:1],
                                     accum_out=sm1[:, 1:2]))
        block.scalar(f2a)
        assert sa.n == sab_sm + NSTT

        def f2v(v):
            for st in range(NSTT):
                def w():
                    v.wait_ge(sv.h, sv.n)
                v.wait_ge(sp.h, peb_rs + st + 1)
                w()
                sv.inc(v.scalar_tensor_tensor(R_sb, psR[:, :], RSQ48, logrel_sb[:, :],
                                              op0=ALU.mult, op1=ALU.add))
                w()
                sv.inc(v.max(out=t8_sb[:, :], in_=R_sb))
                w()
                sv.inc(v.tensor_scalar(mask_sb, R_sb, t8_sb[:, 7:8], None,
                                       op0=ALU.is_ge))
                sv.inc(v.scalar_tensor_tensor(Sp_sb, psS[:, :], RSQ64, logrel_sb[:, :],
                                              op0=ALU.mult, op1=ALU.add))
                if st >= 1:
                    v.wait_ge(sa.h, sab_sm + st)
                w()
                sv.inc(v.select(Sm_sb, mask_sb, Sp_sb, negbig_sb[:, :]))
                w()
                sv.inc(v.reduce_max(sm1[:, 0:1], Sm_sb, axis=AX.X, negate=True))
                v.wait_ge(sa.h, sab_sm + st + 1)
                w()
                sv.inc(v.reciprocal(sm1[:, 2:3], sm1[:, 1:2]))
                w()
                sv.inc(v.tensor_scalar(w_stage[:, st * 1024:(st + 1) * 1024], ex_sb,
                                       sm1[:, 2:3], None, op0=ALU.mult))
        block.vector(f2v)
        assert sv.n == svb_sm + SMV * NSTT
        sm_sv_end = sv.n

        sfb = [s.n for s in sfr]
        peb_m = sp.n
        svb_g = sv.n
        sab_g = sa.n
        GAV = 2
        GVV = 6

        fwbb = [s.n for s in sfwb]
        sfb_pre = [s.n for s in sfr]

        def f3g(g):
            g.wait_ge(sv.h, pro["xld_sv"])
            g.wait_ge(sp.h, pro["pe_end"])
            g.wait_ge(sst.h, st_entry)
            if not build:
                for k in range(3):
                    g.wait_ge(sfwb[k].h, fwbb[k])
            for uc in range(UCN):
                if uc >= 3:
                    g.wait_ge(sp.h, peb_m + uc - 2)
                    if build:
                        g.wait_ge(sfwb[uc % 3].h, fwbb[uc % 3] + 16 * (uc // 3))
                swu = d["wu1"] if build else CW["wu1"]
                sfr[uc % 3].inc(g.dma_start(ffring[uc % 3], swu[uc]), 16)
        block.gpsimd(f3g)

        if build:
            def f3wb(s):
                for uc in range(UCN):
                    s.wait_ge(sfr[uc % 3].h, sfb_pre[uc % 3] + 16 * (uc // 3 + 1))
                    sfwb[uc % 3].inc(s.dma_start(CW["wu1"][uc], ffring[uc % 3]), 16)
            block.sync(f3wb)

        def f3t(t):
            t.wait_ge(sa.h, hT_ready)
            for uc in range(UCN):
                t.wait_ge(sfr[uc % 3].h, sfb[uc % 3] + 16 * (uc // 3 + 1))
                if uc >= 2:
                    t.wait_ge(sv.h, svb_g + GVV * (uc - 2) + 1)
                pb = psM[uc % 2]
                for hc in range(HC):
                    i = t.matmul(pb[:, :], ffring[uc % 3][:, hc * 128:(hc + 1) * 128],
                                 hT[:, hc * TILE:hc * TILE + TILE],
                                 start=(hc == 0), stop=(hc == HC - 1))
                sp.inc(i)
        block.tensor(f3t)
        assert sp.n == peb_m + UCN

        def f3a(sc):
            sc.wait_ge(sst.h, st_entry)
            for uc in range(UCN):
                sc.wait_ge(sv.h, svb_g + GVV * uc + 1)
                sc.wait_ge(sa.h, sa.n)
                sa.inc(sc.activation(gt1, gz, AF.Square))
                sc.wait_ge(sv.h, svb_g + GVV * uc + 3)
                sc.wait_ge(sa.h, sa.n)
                sa.inc(sc.activation(gt2, gt1, AF.Tanh, scale=GELU_C))
        block.scalar(f3a)
        assert sa.n == sab_g + GAV * UCN

        def f3v(v):
            v.wait_ge(sst.h, st_entry)
            for uc in range(UCN):
                def w():
                    v.wait_ge(sv.h, sv.n)
                v.wait_ge(sp.h, peb_m + uc + 1)
                if uc >= 1:
                    v.wait_ge(sa.h, sab_g + GAV * (uc - 1) + 1)
                w()
                sv.inc(v.tensor_scalar(gz, psM[uc % 2][:, :], bu1_sb[:, uc:uc + 1],
                                       None, op0=ALU.add))
                v.wait_ge(sa.h, sab_g + GAV * uc + 1)
                w()
                sv.inc(v.tensor_mul(gt2, gt1, gz))
                w()
                sv.inc(v.scalar_tensor_tensor(gt1, gt2, GELU_A, gz, op0=ALU.mult,
                                              op1=ALU.add))
                v.wait_ge(sa.h, sab_g + GAV * uc + 2)
                w()
                sv.inc(v.tensor_scalar(gt1, gt2, 1.0, None, op0=ALU.add))
                w()
                sv.inc(v.tensor_mul(gt2, gt1, gz))
                w()
                sv.inc(v.tensor_scalar(m1g[:, uc * TILE:(uc + 1) * TILE], gt2, 0.5,
                                       None, op0=ALU.mult))
        block.vector(f3v)
        assert sv.n == svb_g + GVV * UCN
        gelu_sv_end = sv.n

        peb_tr = sp.n
        sab_tr = sa.n

        def f2bt(t):
            for st in range(NSTT):
                t.wait_ge(sv.h, svb_sm + SMV * (st + 1))
                pb = trpF[st % 2]
                if st >= 2:
                    t.wait_ge(sa.h, sab_tr + st - 1)
                for k in range(8):
                    i = t.transpose(pb[:, k * 128:(k + 1) * 128],
                                    w_stage[:, st * 1024 + k * 128:st * 1024 + (k + 1) * 128],
                                    ident[:, :])
                sp.inc(i)
        block.tensor(f2bt)
        assert sp.n == peb_tr + NSTT

        def f2ba(sc):
            for st in range(NSTT):
                sc.wait_ge(sp.h, peb_tr + st + 1)
                pb3 = trpF[st % 2].ap().rearrange("p (a b) -> p a b", b=128)
                sa.inc(sc.copy(wT3[:, :, st * 128:(st + 1) * 128], pb3))
        block.scalar(f2ba)
        assert sa.n == sab_tr + NSTT
        wT_ready = sa.n
        pst.exit(6)

        psL = pst.enter([128, 8], F32)
        peb_l = sp.n
        svb_l = sv.n

        def f4t(t):
            t.wait_ge(sv.h, gelu_sv_end)
            t.wait_ge(sa.h, wT_ready)
            for st in range(NSTT):
                if st > 0:
                    t.wait_ge(sv.h, svb_l + st)
                for uc in range(UCN):
                    i = t.matmul(psL[:, 0:1],
                                 m1g[:, uc * TILE + st * 128:uc * TILE + st * 128 + 128],
                                 wu2_sb[:, uc:uc + 1],
                                 start=(uc == 0), stop=(uc == UCN - 1))
                sp.inc(i)
        block.tensor(f4t)
        assert sp.n == peb_l + NSTT

        def f4v(v):
            for st in range(NSTT):
                v.wait_ge(sp.h, peb_l + st + 1)
                v.wait_ge(sv.h, sv.n)
                sv.inc(v.tensor_copy(learned_sb[:, st:st + 1], psL[:, 0:1]))
        block.vector(f4v)
        assert sv.n == svb_l + NSTT
        learned_sv = sv.n
        pst.exit(1)
        if cc_emit is not None:
            cc_emit(learned_sv)

        g_idx = tile * NSTT
        sab_gt = sa.n

        rm_tok = gd.get("rm_sv", 0)

        def fgates_a(sc):
            sc.wait_ge(sv.h, learned_sv)
            sc.wait_ge(sa.h, sa.n)
            sa.inc(sc.activation(sm1[:, 0:4], learned_sb[:, 0:4], AF.Sigmoid,
                                 bias=scal_sb[:, 2:3]))
            sc.wait_ge(sv.h, svb_gt_pred[0])
            sc.wait_ge(sa.h, sa.n)
            sa.inc(sc.activation(sm2[:, 0:4], sm2[:, 4:8], AF.Sigmoid))
        svb_gt_pred = [None]

        def fgates_v(v):
            def w():
                v.wait_ge(sv.h, sv.n)
            v.wait_ge(sv.h, rm_tok)
            v.wait_ge(sa.h, sab_gt + 1)
            w()
            sv.inc(v.tensor_scalar(sm1[:, 4:8], stL[:, g_idx:g_idx + 4],
                                   rm[rm_idx][:, 0:1], None, op0=ALU.mult))
            sv.inc(v.tensor_scalar(sm2[:, 0:4], sm1[:, 0:4], 2.5, None,
                                   op0=ALU.mult))
            w()
            sv.inc(v.scalar_tensor_tensor(sm1[:, 0:4], sm1[:, 4:8], 0.5,
                                          sm2[:, 0:4], op0=ALU.mult,
                                          op1=ALU.add))
            w()
            sv.inc(v.tensor_scalar(sm1[:, 4:8], sm1[:, 0:4], 0.0, 5.0,
                                   op0=ALU.max, op1=ALU.min))
            w()
            sv.inc(v.tensor_scalar(sm1[:, 0:4], sm1[:, 4:8], 0.5, 1.0 / 1.5,
                                   op0=ALU.subtract, op1=ALU.mult))
            w()
            sv.inc(v.tensor_scalar(sm1[:, 4:8], sm1[:, 0:4], 0.0, 1.0,
                                   op0=ALU.max, op1=ALU.min))
            w()
            sv.inc(v.tensor_scalar(sm2[:, 4:8], sm1[:, 4:8], scal_sb[:, 0:1],
                                   scal_sb[:, 1:2], op0=ALU.mult, op1=ALU.add))
            svb_gt_pred[0] = sv.n
            v.wait_ge(sa.h, sab_gt + 2)
            w()
            sv.inc(v.tensor_scalar(sm3[:, 4:8], sm2[:, 0:4], 0.05, None,
                                   op0=ALU.is_ge))
            w()
            sv.inc(v.scalar_tensor_tensor(sm3[:, 0:4], sm2[:, 0:4], 0.5,
                                          sm3[:, 4:8], op0=ALU.mult,
                                          op1=ALU.mult))
        block.vector(fgates_v)
        block.scalar(fgates_a)
        gates_sv = sv.n

        psA1 = pst.enter([128, 4096], F32)
        sdb5 = [s.n for s in sdr]
        peb5 = sp.n
        sab5 = sa.n

        dwbb5 = [s.n for s in sdwb]
        sdb5_pre = [s.n for s in sdr]

        def f5g(g):
            g.wait_ge(sv.h, pro["xld_sv"])
            if not build:
                for k in range(4):
                    g.wait_ge(sdwb[k].h, dwbb5[k])
            for rnd in range(4):
                for sc_i in range(SCN):
                    gi = rnd * SCN + sc_i
                    if gi >= 4:
                        g.wait_ge(sp.h, peb5 + gi - 3)
                        if build:
                            g.wait_ge(sdwb[gi % 4].h, dwbb5[gi % 4] + 16 * (gi // 4))
                    sav = d["auxv"] if build else CW["auxv"]
                    sdr[gi % 4].inc(g.dma_start(
                        avring[gi % 4], sav[sc_i, :, rnd * 1024:(rnd + 1) * 1024]), 16)
        block.gpsimd(f5g)

        if build:
            def f5wb(s):
                for rnd in range(4):
                    for sc_i in range(SCN):
                        gi = rnd * SCN + sc_i
                        s.wait_ge(sdr[gi % 4].h, sdb5_pre[gi % 4] + 16 * (gi // 4 + 1))
                        sdwb[gi % 4].inc(s.dma_start(
                            CW["auxv"][sc_i, :, rnd * 1024:(rnd + 1) * 1024],
                            avring[gi % 4]), 16)
            block.sync(f5wb)

        def f5t(t):
            t.wait_ge(sa.h, wT_ready)
            t.wait_ge(sv.h, learned_sv)
            for rnd in range(4):
                if rnd > 0:
                    t.wait_ge(sa.h, sab5 + rnd)
                for sc_i in range(SCN):
                    gi = rnd * SCN + sc_i
                    t.wait_ge(sdr[gi % 4].h, sdb5[gi % 4] + 16 * (gi // 4 + 1))
                    for k in range(8):
                        i = t.matmul(psA1[:, k * 512:(k + 1) * 512],
                                     avring[gi % 4][:, k * 128:(k + 1) * 128],
                                     wT[:, sc_i * TILE:(sc_i + 1) * TILE],
                                     start=(sc_i == 0), stop=(sc_i == SCN - 1),
                                     skip_group_check=True)
                    sp.inc(i)
        block.tensor(f5t)
        assert sp.n == peb5 + 32

        def f5a(sc):
            for rnd in range(4):
                sc.wait_ge(sp.h, peb5 + 8 * (rnd + 1))
                sc.wait_ge(sa.h, sa.n)
                sa.inc(sc.copy(aux1T[:, rnd * 8 * TILE:(rnd + 1) * 8 * TILE],
                               psA1[:, :]))
        block.scalar(f5a)
        assert sa.n == sab5 + 4
        aux1_sa = sa.n
        pst.exit(1)

        psV = [pst.enter([128, TILE], F32), pst.enter([128, TILE], F32)]
        wrb6 = [s.n for s in swr]
        peb6 = sp.n
        svb6 = sv.n

        wbb6 = [s.n for s in swb]
        wrb6_pre = [s.n for s in swr]

        def f6g(g):
            g.wait_ge(sp.h, peb5 + 32)
            if not build:
                for k in range(2):
                    g.wait_ge(swb[k].h, wbb6[k])
            for vc in range(2):
                svd = d["wvd"] if build else CW["wvd"]
                swr[vc].inc(g.dma_start(wvd_sb[vc], svd[vc]), 16)
        block.gpsimd(f6g)

        if build:
            def f6wb(s):
                for vc in range(2):
                    s.wait_ge(swr[vc].h, wrb6_pre[vc] + 16)
                    swb[vc].inc(s.dma_start(CW["wvd"][vc], wvd_sb[vc]), 16)
            block.sync(f6wb)

        def f6t(t):
            t.wait_ge(sa.h, aux1_sa)
            for vc in range(2):
                t.wait_ge(swr[vc].h, wrb6[vc] + 16)
                for hc in range(HC):
                    i = t.matmul(psV[vc][:, :], wvd_sb[vc][:, hc * 128:(hc + 1) * 128],
                                 aux1T[:, hc * TILE:hc * TILE + TILE],
                                 start=(hc == 0), stop=(hc == HC - 1))
                sp.inc(i)
        block.tensor(f6t)
        assert sp.n == peb6 + 2

        def f6v(v):
            for vc in range(2):
                v.wait_ge(sp.h, peb6 + vc + 1)
                v.wait_ge(sv.h, sv.n)
                sv.inc(v.tensor_copy(v1T[:, vc * TILE:(vc + 1) * TILE], psV[vc][:, :]))
        block.vector(f6v)
        assert sv.n == svb6 + 2
        v1_sv = sv.n
        pst.exit(2)

        wbb6b = [swb[2].n, swb[3].n]
        wrb6b_pre = [swr[2].n, swr[3].n]

        def f6g2(g):
            g.wait_ge(sp.h, peb6 + 2)
            if not build:
                for k in range(2):
                    g.wait_ge(swb[2 + k].h, wbb6b[k])
            for vc in range(2):
                svu = d["wvu"] if build else CW["wvu"]
                swr[2 + vc].inc(g.dma_start(wvu7[vc], svu[vc]), 16)
        block.gpsimd(f6g2)
        wvu_tok = [swr[2].n, swr[3].n]

        if build:
            def f6wb2(s):
                for vc in range(2):
                    s.wait_ge(swr[2 + vc].h, wrb6b_pre[vc] + 16)
                    swb[2 + vc].inc(s.dma_start(CW["wvu"][vc], wvu7[vc]), 16)
            block.sync(f6wb2)

        psYh = [pst.enter([128, 2048], F32), pst.enter([128, 2048], F32)]
        peb7 = sp.n
        svb7 = sv.n
        sqb7 = [s.n for s in sqy]
        stb7 = sst.n
        FEV = 4

        def f7s(s):
            s.wait_ge(sp.h, peb7)
            for st in range(NSTT):
                if st >= 1:
                    s.wait_ge(sv.h, svb7 + FEV * st)
                    s.wait_ge(sst.h, stb7 + 16 * 4 * st)
                r0 = tile * TILE + st * 128
                for hq in range(4):
                    sqy[hq].inc(s.dma_start(xq7[hq], src[r0:r0 + 128,
                                                         hq * 1024:(hq + 1) * 1024]), 16)
                for hq in range(4):
                    s.wait_ge(sv.h, svb7 + FEV * st + 2 * (hq // 2 + 1))
                    sst.inc(s.dma_start(dst[r0:r0 + 128, hq * 1024:(hq + 1) * 1024],
                                        xq7[hq]), 16)
        block.sync(f7s)

        def f7t(t):
            t.wait_ge(sv.h, v1_sv)
            t.wait_ge(swr[2].h, wvu_tok[0])
            t.wait_ge(swr[3].h, wvu_tok[1])
            for st in range(NSTT):
                for half in range(2):
                    if st > 0:
                        t.wait_ge(sv.h, svb7 + FEV * (st - 1) + 2 * (half + 1))
                    for vc in range(2):
                        for k in range(4):
                            kk = half * 4 + k
                            i = t.matmul(psYh[half][:, k * 512:(k + 1) * 512],
                                         v1T[:, vc * TILE + st * 128:vc * TILE + st * 128 + 128],
                                         wvu7[vc][:, kk * 512:(kk + 1) * 512],
                                         start=(vc == 0), stop=(vc == 1),
                                         skip_group_check=True)
                    sp.inc(i)
        block.tensor(f7t)
        assert sp.n == peb7 + 2 * NSTT


        def f7v(v):
            v.wait_ge(sst.h, st_entry)
            v.wait_ge(sv.h, gates_sv)
            for st in range(NSTT):
                def w():
                    v.wait_ge(sv.h, sv.n)
                gst = g_idx + st
                for half in range(2):
                    v.wait_ge(sp.h, peb7 + 2 * st + half + 1)
                    v.wait_ge(sqy[2 * half].h, sqb7[2 * half] + 16 * (st + 1))
                    v.wait_ge(sqy[2 * half + 1].h, sqb7[2 * half + 1] + 16 * (st + 1))
                    w()
                    sv.inc(v.tensor_scalar(evb7w[half], xq7w[half],
                                           stA2[:, gst:gst + 1], stB2[:, gst:gst + 1],
                                           op0=ALU.mult, op1=ALU.add))
                    w()
                    sv.inc(v.scalar_tensor_tensor(
                        xq7w[half], psYh[half][:, :],
                        sm3[:, st:st + 1], evb7w[half], op0=ALU.mult,
                        op1=ALU.add))
        block.vector(f7v)
        assert sv.n == svb7 + FEV * NSTT
        pst.exit(2)
        gd.update(pe=sp.n, sv=sv.n, sa=sa.n, st=sst.n)

    gt0, cc0 = emit_stats(d["x0"], 0)
    last = "x0"
    if upto >= 1:
        for tile in range(NT):
            emit_ffn_tile(d["x0"], d["x1"], tile, 0, grp=gt0[tile],
                          cc_emit=(cc0 if tile == 0 else None))
        last = "x1"
    if upto >= 2:
        gt1_, _ = emit_stats(d["x1"], None)
        for tile in range(NT):
            emit_attn_tile(d["x1"], d["x2"], tile, grp=gt1_[tile])
        last = "x2"
    if upto >= 3:
        gt2_, cc2 = emit_stats(d["x2"], 1)
        for tile in range(NT):
            emit_ffn_tile(d["x2"], d["y"], tile, 1, grp=gt2_[tile],
                          cc_emit=(cc2 if tile == 0 else None))
        last = "y"

    if last != "y":
        stn0 = sst.n
        g_pe, g_sv, g_sa = gd["pe"], gd["sv"], gd["sa"]

        def fcp(s):
            s.wait_ge(sp.h, g_pe)
            s.wait_ge(sv.h, g_sv)
            s.wait_ge(sa.h, g_sa)
            s.wait_ge(sst.h, stn0)
            for st in range(NSTG):
                if st >= 2:
                    s.wait_ge(sst.h, stn0 + 16 * (st - 1))
                sx2[st % 2].inc(s.dma_start(xld[st % 2],
                                            d[last][st * 128:(st + 1) * 128, :]), 16)
                s.wait_ge(sx2[st % 2].h, sx2[st % 2].n)
                sst.inc(s.dma_start(d["y"][st * 128:(st + 1) * 128, :], xld[st % 2]), 16)
        block.sync(fcp)

    fins = [(sst, sst.n)] + [(s, s.n) for s in
                              ([si, swq, scx] + sx2 + swr + sdr + sfr + sqy
                               + swb + sdwb + sfwb)]

    def fend(s):
        for sem, n in fins:
            s.wait_ge(sem.h, n)
    block.sync(fend)

    ctx.close()
    return nc


def prep_inputs(inp):
    f32 = np.float32

    def A(x):
        return np.ascontiguousarray(np.asarray(x, dtype=f32))

    w = {}
    w["gt"] = _blk_lhsT(A(inp["L0_attn_ffn_gate_w"]))
    w["ut"] = _blk_lhsT(A(inp["L0_attn_ffn_up_w"]))
    w["dtw"] = A(np.asarray(inp["L0_attn_ffn_down_w"], f32).T).reshape(FCN, 128, H)
    w["upw"] = _blk_lhsT(A(inp["L0_attn_up_w"]))
    w["gw"] = _blk_lhsT(A(inp["L0_attn_gate_w"]))
    w["lao"] = A(np.asarray(inp["L0_attn_local_approx_output"], f32).T).reshape(SCN, 128, H)
    w["fuse"] = A(np.transpose(np.asarray(inp["L0_attn_fuse_weight"], f32),
                               (2, 1, 0))).reshape(G, SCN, 128, H)
    w["wq"] = _blk_lhsT(A(inp["L0_ffn_Wq"]), oc=64).reshape(128, HC * 64)
    w["wu1"] = _blk_lhsT(A(inp["L0_ffn_Wu1"]))
    w["wvd"] = _blk_lhsT(A(inp["L0_ffn_Wvd"]))
    w["wvu"] = A(np.asarray(inp["L0_ffn_Wvu"], f32).T).reshape(2, 128, H)
    w["auxv"] = A(inp["L0_ffn_aux_values"]).reshape(SCN, 128, H)
    w["akt"] = A(np.asarray(inp["L0_ffn_aux_keys"], f32).T)
    w["wrt"] = A(np.asarray(inp["L0_ffn_Wr"], f32).T)
    w["lp"] = A(np.asarray(inp["L0_attn_local_point"], f32).reshape(SCN, 128).T)
    w["bu1"] = A(np.asarray(inp["L0_ffn_bu1"], f32).reshape(UCN, 128).T)
    w["wu2"] = A(np.asarray(inp["L0_ffn_Wu2"], f32).reshape(UCN, 128).T)
    w["df"] = A(np.broadcast_to(np.asarray(inp["L0_attn_discount_factor"], f32),
                                (128, G)))
    w["scal"] = A(np.broadcast_to(np.stack([
        np.float32(inp["L0_ffn_gate_w1"]), np.float32(inp["L0_ffn_gate_bias"]),
        np.asarray(inp["L0_ffn_bu2"], f32).reshape(())]).astype(f32), (128, 3)))
    w["logrel"] = A(np.broadcast_to(np.asarray(inp["L0_ffn_log_rel"], f32),
                                    (128, NSLOT)))
    x = np.asarray(inp["L0_attn_hidden_states"], f32).reshape(TCORE * NCORES, H)
    return w, x


def kernel(upto=3, debug_xs=False, **inputs):
    w, x = prep_inputs(inputs)
    nc = build_program(upto=upto, debug_xs=debug_xs)
    in_maps = []
    for c in range(NCORES):
        m = dict(w)
        m["x0"] = np.ascontiguousarray(x[c * TCORE:(c + 1) * TCORE])
        in_maps.append(m)
    res = run_bass_kernel_spmd(nc, in_maps, core_ids=list(range(NCORES)))
    out = np.concatenate([res.results[c]["y"] for c in range(NCORES)], axis=0)
    extras = None
    if debug_xs:
        extras = {k: np.concatenate([res.results[c][k] for c in range(NCORES)], axis=0)
                  for k in ("x1", "x2")}
    out = out.reshape(2, TCORE * NCORES // 2, H)
    return (out, extras) if debug_xs else out


# version 10
# speedup vs baseline: 1.0083x; 1.0066x over previous
import numpy as np
from contextlib import ExitStack
import concourse.bass as bass
import concourse.mybir as mybir
from concourse.bass_utils import run_bass_kernel_spmd

F32 = mybir.dt.float32
BF16 = mybir.dt.bfloat16
U8 = mybir.dt.uint8
AF = mybir.ActivationFunctionType
ALU = mybir.AluOpType
AX = mybir.AxisListType

H = 4096; HC = 32
S = 1024; SCN = 8
FF = 9984; FCN = 78
G = 6
BOT = 64; RD = 48; VB = 256; UCN = 8
NSLOT = 1024
TCORE = 2048; TILE = 512; NT = 4; NSTT = 4
NSTG = 16
RSQ48 = 0.14433756729740643
RSQ64 = 0.125
NEGBIG = -30000.0
GELU_C = 0.7978845608028654
GELU_A = 0.044715
KB = 1024
NCORES = 8


def _blk_lhsT(w, oc=128):
    O, I = w.shape
    nO, nI = O // oc, I // 128
    b = w.reshape(nO, oc, nI, 128).transpose(0, 3, 2, 1)
    return np.ascontiguousarray(b).reshape(nO, 128, nI * oc)


class Sem:
    __slots__ = ("h", "n")

    def __init__(self, ctx, nc, name):
        self.h = ctx.enter_context(nc.semaphore(name))
        self.n = 0

    def inc(self, instr, k=1):
        instr.then_inc(self.h, k)
        self.n += k
        return self.n


class PsumStack:
    def __init__(self, nc):
        self.nc = nc
        self.stack = []
        self.k = 0

    def enter(self, shape, dt):
        self.k += 1
        cm = self.nc.psum_tensor(f"ps{self.k}", shape, dt)
        t = cm.__enter__()
        self.stack.append(cm)
        return t

    def exit(self, k=1):
        for _ in range(k):
            self.stack.pop().__exit__(None, None, None)


def build_program(upto=3, debug_xs=False):
    ctx = ExitStack()
    nc = bass.Bass("TRN2", target_bir_lowering=False)
    d = {}

    def din(name, shape, dt=F32):
        d[name] = nc.dram_tensor(name, list(shape), dt, kind="ExternalInput").ap()

    din("x0", (TCORE, H))
    din("gt", (FCN, 128, H)); din("ut", (FCN, 128, H)); din("dtw", (FCN, 128, H))
    din("upw", (SCN, 128, H)); din("gw", (SCN, 128, H))
    din("lao", (SCN, 128, H)); din("fuse", (G, SCN, 128, H))
    din("wq", (128, HC * 64)); din("wu1", (UCN, 128, H))
    din("wvd", (2, 128, H)); din("wvu", (2, 128, H)); din("auxv", (SCN, 128, H))
    din("akt", (BOT, NSLOT)); din("wrt", (BOT, RD))
    din("lp", (128, SCN)); din("bu1", (128, UCN)); din("wu2", (128, UCN))
    din("df", (128, G)); din("scal", (128, 3)); din("logrel", (128, NSLOT))

    okind = "ExternalOutput" if debug_xs else "Internal"
    d["x1"] = nc.dram_tensor("x1", [TCORE, H], F32, kind=okind).ap()
    d["x2"] = nc.dram_tensor("x2", [TCORE, H], F32, kind=okind).ap()
    d["y"] = nc.dram_tensor("y", [TCORE, H], F32, kind="ExternalOutput").ap()
    ymix = [nc.dram_tensor(f"ymix{i}", [TILE, H], F32, kind="Internal").ap()
            for i in range(2)]
    CW = {}
    for name, shape in [("gt", (FCN, 128, H)), ("ut", (FCN, 128, H)),
                        ("dtw", (FCN, 128, H)), ("upw", (SCN, 128, H)),
                        ("gw", (SCN, 128, H)), ("lao", (SCN, 128, H)),
                        ("fuse", (G, SCN, 128, H)), ("wu1", (UCN, 128, H)),
                        ("auxv", (SCN, 128, H)), ("wvd", (2, 128, H)),
                        ("wvu", (2, 128, H))]:
        CW[name] = nc.dram_tensor(name + "_c", list(shape), BF16, kind="Internal").ap()
    cc_in = [nc.dram_tensor(f"cc{i}_in", [1, 8], F32, kind="Internal") for i in (1, 2)]
    cc_out = [nc.dram_tensor(f"cc{i}_out", [1, 8], F32, kind="Internal") for i in (1, 2)]

    def sbt(name, shape, dt):
        return ctx.enter_context(nc.sbuf_tensor(name, list(shape), dt))

    ident = sbt("ident", [128, 128], BF16)
    ones_col = sbt("ones_col", [128, 1], F32)
    ones_row = sbt("ones_row", [1, 128], F32)
    lp_sb = sbt("lp_sb", [128, SCN], F32)
    bu1_sb = sbt("bu1_sb", [128, UCN], F32)
    wu2_sb = sbt("wu2_sb", [128, UCN], F32)
    df_sb = sbt("df_sb", [128, G], F32)
    ndf_sb = sbt("ndf_sb", [128, G], F32)
    scal_sb = sbt("scal_sb", [128, 3], F32)
    logrel_sb = sbt("logrel_sb", [128, NSLOT], F32)
    negbig_sb = sbt("negbig_sb", [128, NSLOT], F32)
    akt_sb = sbt("akt_sb", [BOT, NSLOT], F32)
    wrt_sb = sbt("wrt_sb", [BOT, RD], F32)
    rkt_sb = sbt("rkt_sb", [RD, NSLOT], F32)
    wq_sb = sbt("wq_sb", [128, HC * 64], BF16)
    stM = sbt("stM", [128, NSTG], F32)
    stR = sbt("stR", [128, NSTG], F32)
    stRh = sbt("stRh", [128, NSTG], F32)
    stA2 = sbt("stA2", [128, NSTG], F32)
    stB2 = sbt("stB2", [128, NSTG], F32)
    stL = sbt("stL", [128, NSTG], F32)
    ssA = sbt("ssA", [128, NSTG], F32)
    ssQ = sbt("ssQ", [128, NSTG], F32)
    rm = [sbt("rm1", [128, 1], F32), sbt("rm2", [128, 1], F32)]
    t8_sb = sbt("t8_sb", [128, 8], F32)
    sm1 = sbt("sm1", [128, 8], F32)
    sm2 = sbt("sm2", [128, 8], F32)
    sm3 = sbt("sm3", [128, 8], F32)
    learned_sb = sbt("learned_sb", [128, NSTT], F32)
    cc_sb = sbt("cc_sb", [1, 8], F32)
    w_stage = sbt("w_stage", [128, 4096], BF16)

    AB = 172 * KB
    arena = ctx.enter_context(nc.sbuf_tensor("arena", [128, AB], U8))
    a_f32 = arena.bitcast(F32).ap()
    a_bf = arena.bitcast(BF16).ap()
    a_u8 = arena.ap()

    def cf32(off_b, n):
        assert off_b % 4 == 0 and off_b + 4 * n <= AB, (off_b, n)
        return a_f32[:, off_b // 4: off_b // 4 + n]

    def cbf(off_b, n):
        assert off_b % 2 == 0 and off_b + 2 * n <= AB, (off_b, n)
        return a_bf[:, off_b // 2: off_b // 2 + n]

    def cu8(off_b, n):
        assert off_b + n <= AB
        return a_u8[:, off_b: off_b + n]

    hT = cbf(0, HC * TILE)
    hT3 = a_bf[:, 0:HC * TILE].rearrange("p (hc t) -> p hc t", t=TILE)
    R1 = 32 * KB
    tmpT = cbf(R1, 7 * SCN * TILE)
    aT = cbf(R1, FCN * TILE)
    aux1T = cbf(R1, HC * TILE)
    wT = cbf(R1 + 32 * KB, SCN * TILE)
    wT3 = a_bf[:, (R1 + 32 * KB) // 2:(R1 + 32 * KB) // 2 + SCN * TILE] \
        .rearrange("p (sc t) -> p sc t", t=TILE)
    v1T = cbf(R1 + 40 * KB, 2 * TILE)
    q_sb = cf32(R1 + 42 * KB, TILE)
    rq_sb = cf32(R1 + 44 * KB, TILE)
    m1g = cf32(R1 + 46 * KB, UCN * TILE)
    R_sb = cf32(R1 + 62 * KB, NSLOT)
    Sp_sb = cf32(R1 + 66 * KB, NSLOT)
    Sm_sb = cf32(R1 + 70 * KB, NSLOT)
    ex_sb = cf32(R1 + 74 * KB, NSLOT)
    mask_sb = cu8(R1 + 78 * KB, NSLOT)
    scr1k = cf32(R1 + 62 * KB, NSLOT)
    sxld = [cf32(R1, H), cf32(R1 + 16 * KB, H), cf32(R1 + 32 * KB, H)]

    R2 = 112 * KB
    xld = [cf32(R2, H), cf32(R2 + 16 * KB, H)]
    h_bf = cbf(R2 + 32 * KB, H)
    wring = [cbf(R2 + i * 8 * KB, H) for i in range(4)]
    mx = [cf32(R2 + 32 * KB + i * 2 * KB, TILE) for i in range(6)]
    dring = [cbf(R2 + i * 2 * KB, 1024) for i in range(6)]
    evb = [cf32(R2 + 12 * KB + i * 4 * KB, 1024) for i in range(4)]
    xq = [cf32(R2 + 28 * KB + i * 4 * KB, 1024) for i in range(4)]
    yq = [cf32(R2 + 44 * KB + i * 4 * KB, 1024) for i in range(4)]
    ffring = [cbf(R2 + 16 * KB + i * 8 * KB, H) for i in range(3)]
    avring = [cbf(R2 + i * 2 * KB, 1024) for i in range(4)]
    wvd_sb = [cbf(R2, H), cbf(R2 + 8 * KB, H)]
    mx7 = cf32(R2 + 56 * KB, TILE)
    wvu7 = [cbf(R2, H), cbf(R2 + 8 * KB, H)]
    xq7 = [cf32(R2 + 16 * KB + i * 4 * KB, 1024) for i in range(4)]
    evb7 = [cf32(R2 + 32 * KB + i * 4 * KB, 1024) for i in range(4)]
    xq7w = [cf32(R2 + 16 * KB, 2048), cf32(R2 + 24 * KB, 2048)]
    evb7w = [cf32(R2 + 32 * KB, 2048), cf32(R2 + 40 * KB, 2048)]
    evbw = cf32(R2 + 12 * KB, 4096)
    xqw = cf32(R2 + 28 * KB, 4096)
    yqw = cf32(R2 + 44 * KB, 4096)
    gz = cf32(R2 + 48 * KB, TILE)
    gt1 = cf32(R2 + 50 * KB, TILE)
    gt2 = cf32(R2 + 52 * KB, TILE)

    sx = Sem(ctx, nc, "sx")
    sst = Sem(ctx, nc, "sst")
    si = Sem(ctx, nc, "si")
    swq = Sem(ctx, nc, "swq")
    scx = Sem(ctx, nc, "scx")
    sx2 = [Sem(ctx, nc, f"sx2_{i}") for i in range(2)]
    ssl = [Sem(ctx, nc, f"ssl{i}") for i in range(3)]
    swr = [Sem(ctx, nc, f"swr{i}") for i in range(4)]
    sdr = [Sem(ctx, nc, f"sdr{i}") for i in range(6)]
    sfr = [Sem(ctx, nc, f"sfr{i}") for i in range(3)]
    sqy = [Sem(ctx, nc, f"sqy{i}") for i in range(4)]
    swb = [Sem(ctx, nc, f"swb{i}") for i in range(4)]
    sdwb = [Sem(ctx, nc, f"sdwb{i}") for i in range(6)]
    sfwb = [Sem(ctx, nc, f"sfwb{i}") for i in range(3)]
    sw = Sem(ctx, nc, "sw")
    sw2 = Sem(ctx, nc, "sw2")
    sp = Sem(ctx, nc, "sp")
    sv = Sem(ctx, nc, "sv")
    sa = Sem(ctx, nc, "sa")
    sg = Sem(ctx, nc, "sg")
    scc = Sem(ctx, nc, "scc")

    block = ctx.enter_context(nc.Block())
    pst = PsumStack(nc)
    gd = {"pe": 0, "sv": 0, "sa": 0, "st": 0}

    def init_sync(s):
        for name, dst in [("lp", lp_sb), ("bu1", bu1_sb), ("wu2", wu2_sb),
                          ("df", df_sb), ("scal", scal_sb), ("logrel", logrel_sb),
                          ("akt", akt_sb), ("wrt", wrt_sb)]:
            si.inc(s.dma_start(dst[:, :], d[name]), 16)
    block.sync(init_sync)
    n_init_loads = si.n

    def init_g(g):
        sg.inc(g.memset(negbig_sb[:, :], NEGBIG))
        sg.inc(g.memset(ident[:, :], 0.0))
        g.wait_ge(sg.h, sg.n)
        sg.inc(g.affine_select(out=ident[:, :], in_=ident[:, :],
                               compare_op=ALU.not_equal, fill=1.0, base=0,
                               pattern=[[-1, 128]], channel_multiplier=1))
        sg.inc(g.memset(ones_col[:, :], 1.0))
        sg.inc(g.memset(ones_row[:, :], 1.0))
        swq.inc(g.dma_start(wq_sb[:, :], d["wq"]), 16)
    block.gpsimd(init_g)

    def init_v(v):
        v.wait_ge(si.h, n_init_loads)
        sv.inc(v.tensor_scalar(ndf_sb[:, :], df_sb[:, :], -1.0, None, op0=ALU.mult))
    block.vector(init_v)

    psrk_t = pst.enter([128, NSLOT], F32)

    def init_t(t):
        t.wait_ge(si.h, n_init_loads)
        t.matmul(psrk_t[0:RD, 0:512], wrt_sb[:, :], akt_sb[:, 0:512], start=True, stop=True)
        sp.inc(t.matmul(psrk_t[0:RD, 512:1024], wrt_sb[:, :], akt_sb[:, 512:1024],
                        start=True, stop=True))
    block.tensor(init_t)
    pe_rk = sp.n

    def init_v2(v):
        v.wait_ge(sp.h, pe_rk)
        sv.inc(v.tensor_copy(rkt_sb[:, :], psrk_t[0:RD, :]))
    block.vector(init_v2)
    pst.exit(1)
    gd.update(pe=sp.n, sv=sv.n, sa=sa.n, st=sst.n)

    def emit_stats(src, cc_idx):
        slb = [s.n for s in ssl]
        svb = sv.n
        sab = sa.n
        st_guard = sst.n
        g_pe, g_sv, g_sa = gd["pe"], gd["sv"], gd["sa"]
        grp_tok = []

        def fsq(sc):
            sc.wait_ge(sst.h, st_guard)
            sc.wait_ge(sp.h, g_pe)
            sc.wait_ge(sv.h, g_sv)
            for st in range(3):
                ssl[st % 3].inc(sc.dma_start(sxld[st % 3],
                                             src[st * 128:(st + 1) * 128, :]), 16)
            for st in range(NSTG):
                sc.wait_ge(ssl[st % 3].h, slb[st % 3] + 16 * (st // 3 + 1))
                if st >= 1:
                    sc.wait_ge(sv.h, svb + 2 * st)
                for k in range(4):
                    i = sc.activation(scr1k, sxld[st % 3][:, k * 1024:(k + 1) * 1024],
                                      AF.Square, accum_out=sm1[:, k:k + 1])
                sa.inc(i)
                nxt = st + 3
                if nxt < NSTG:
                    sc.wait_ge(sv.h, svb + 2 * (st + 1))
                    ssl[nxt % 3].inc(sc.dma_start(sxld[nxt % 3],
                                                  src[nxt * 128:(nxt + 1) * 128, :]), 16)
        block.scalar(fsq)
        assert sa.n == sab + NSTG

        def fred(v):
            for st in range(NSTG):
                v.wait_ge(ssl[st % 3].h, slb[st % 3] + 16 * (st // 3 + 1))
                v.wait_ge(sv.h, sv.n)
                sv.inc(v.reduce_sum(ssA[:, st:st + 1], sxld[st % 3][:, :], axis=AX.X))
                v.wait_ge(sa.h, sab + st + 1)
                v.wait_ge(sv.h, sv.n)
                sv.inc(v.reduce_sum(ssQ[:, st:st + 1], sm1[:, 0:4], axis=AX.X))
        block.vector(fred)
        assert sv.n == svb + 2 * NSTG

        for g_ in range(4):
            c = slice(4 * g_, 4 * g_ + 4)
            svg = svb + 2 * (4 * g_ + 4)

            def ffin(v, c=c, svg=svg):
                def w():
                    v.wait_ge(sv.h, sv.n)
                v.wait_ge(sv.h, svg)
                w(); sv.inc(v.tensor_scalar(stM[:, c], ssA[:, c], 1.0 / H, None, op0=ALU.mult))
                sv.inc(v.tensor_scalar(ssQ[:, c], ssQ[:, c], 1.0 / H, None, op0=ALU.mult))
                w(); sv.inc(v.tensor_mul(ssA[:, c], stM[:, c], stM[:, c]))
                w(); sv.inc(v.tensor_sub(ssQ[:, c], ssQ[:, c], ssA[:, c]))
                w(); sv.inc(v.tensor_scalar(ssA[:, c], ssQ[:, c], 1e-5, None, op0=ALU.add))
                w(); sv.inc(v.reciprocal(stR[:, c], ssA[:, c]))
                w(); sv.inc(v.tensor_mul(stL[:, c], ssQ[:, c], stR[:, c]))
            block.vector(ffin)
            svn = sv.n

            def ffin_a(sc, c=c, svn=svn):
                sc.wait_ge(sv.h, svn)
                sc.wait_ge(sa.h, sa.n)
                sa.inc(sc.activation(ssQ[:, c], ssA[:, c], AF.Ln))
                sc.wait_ge(sa.h, sa.n)
                sa.inc(sc.activation(stR[:, c], ssQ[:, c], AF.Exp, scale=-0.5))
                sc.wait_ge(sa.h, sa.n)
                sa.inc(sc.activation(stL[:, c], stL[:, c], AF.Ln, bias=1.0))
            block.scalar(ffin_a)
            san = sa.n

            def ffin_v(v, c=c, san=san):
                v.wait_ge(sa.h, san)
                v.wait_ge(sv.h, sv.n)
                sv.inc(v.tensor_scalar(stRh[:, c], stR[:, c], 0.5, None, op0=ALU.mult))
                v.wait_ge(sv.h, sv.n)
                sv.inc(v.tensor_scalar(stA2[:, c], stRh[:, c], 1.0, None, op0=ALU.add))
                sv.inc(v.tensor_mul(stB2[:, c], stM[:, c], stRh[:, c]))
                v.wait_ge(sv.h, sv.n)
                sv.inc(v.tensor_scalar(stB2[:, c], stB2[:, c], -1.0, None, op0=ALU.mult))
            block.vector(ffin_v)
            grp_tok.append((sv.n, sa.n))

        def emit_cc(guard_sv=0):
            pslv_t = pst.enter([128, NSTG], F32)
            psbc = pst.enter([128, 1], F32)
            svn2, san2 = grp_tok[-1]

            def flv(t):
                t.wait_ge(sa.h, san2)
                t.wait_ge(sv.h, max(g_sv, guard_sv))
                sp.inc(t.matmul(pslv_t[0:1, :], ones_col[:, :], stL[:, :],
                                start=True, stop=True))
            block.tensor(flv)
            pen = sp.n

            def flv_v(v):
                v.wait_ge(sp.h, pen)
                v.wait_ge(sv.h, sv.n)
                sv.inc(v.reduce_sum(cc_sb[0:1, 0:1], pslv_t[0:1, :], axis=AX.X))
            block.vector(flv_v)
            svn3 = sv.n

            def fst(s):
                s.wait_ge(sv.h, svn3)
                sst.inc(s.dma_start(cc_in[cc_idx].ap()[0:1, 0:1], cc_sb[0:1, 0:1]), 16)
            block.sync(fst)
            stn = sst.n

            def fcc(g):
                g.wait_ge(sst.h, stn)
                scc.inc(g.collective_compute(
                    "AllReduce", ALU.add, replica_groups=[list(range(NCORES))],
                    ins=[cc_in[cc_idx].ap().opt()],
                    outs=[cc_out[cc_idx].ap().opt()]), 1)
            block.gpsimd(fcc)
            ccn = scc.n

            def fld(s):
                s.wait_ge(scc.h, ccn)
                scx.inc(s.dma_start(cc_sb[0:1, 1:2], cc_out[cc_idx].ap()[0:1, 0:1]), 16)
            block.sync(fld)
            sxn = scx.n

            def fmn(v):
                v.wait_ge(scx.h, sxn)
                v.wait_ge(sv.h, sv.n)
                sv.inc(v.tensor_scalar(cc_sb[0:1, 2:3], cc_sb[0:1, 1:2],
                                       1.0 / 16384.0, 1e-6, op0=ALU.mult, op1=ALU.add))
                v.wait_ge(sv.h, sv.n)
                sv.inc(v.reciprocal(cc_sb[0:1, 3:4], cc_sb[0:1, 2:3]))
            block.vector(fmn)
            svn4 = sv.n

            def fbc(t):
                t.wait_ge(sv.h, svn4)
                sp.inc(t.matmul(psbc[:, :], ones_row[:, :], cc_sb[0:1, 3:4],
                                start=True, stop=True))
            block.tensor(fbc)
            pen2 = sp.n

            def fbc_v(v):
                v.wait_ge(sp.h, pen2)
                v.wait_ge(sv.h, sv.n)
                sv.inc(v.tensor_copy(rm[cc_idx][:, :], psbc[:, :]))
            block.vector(fbc_v)
            pst.exit(2)
            gd["rm_sv"] = sv.n
        return grp_tok, (emit_cc if cc_idx is not None else None)

    def emit_prologue(src, tile, grp=None):
        sxb = [s.n for s in sx2]
        svb = sv.n
        sab = sa.n
        peb = sp.n
        g_pe, g_sv, g_sa = gd["pe"], gd["sv"], gd["sa"]

        def floads(s):
            s.wait_ge(sp.h, g_pe)
            s.wait_ge(sv.h, g_sv)
            s.wait_ge(sa.h, g_sa)
            s.wait_ge(sst.h, gd["st"])
            for st in range(NSTT):
                if st >= 2:
                    s.wait_ge(sv.h, svb + (st - 1))
                r0 = tile * TILE + st * 128
                sx2[st % 2].inc(s.dma_start(xld[st % 2], src[r0:r0 + 128, :]), 16)
        block.sync(floads)

        def fh(v):
            if grp is not None:
                v.wait_ge(sv.h, grp[0])
                v.wait_ge(sa.h, grp[1])
            for st in range(NSTT):
                gix = tile * NSTT + st
                v.wait_ge(sx2[st % 2].h, sxb[st % 2] + 16 * (st // 2 + 1))
                if st > 0:
                    v.wait_ge(sp.h, peb + st * 4)
                v.wait_ge(sv.h, sv.n)
                sv.inc(v.tensor_scalar(h_bf[:, :], xld[st % 2][:, :],
                                       stM[:, gix:gix + 1], stR[:, gix:gix + 1],
                                       op0=ALU.subtract, op1=ALU.mult))
        block.vector(fh)
        assert sv.n == svb + NSTT

        trp = [pst.enter([128, 1024], BF16), pst.enter([128, 1024], BF16)]

        def ftr(t):
            t.wait_ge(sv.h, g_sv)
            t.wait_ge(sa.h, g_sa)
            for st in range(NSTT):
                t.wait_ge(sv.h, svb + st + 1)
                for r in range(4):
                    gidx = st * 4 + r
                    pb = trp[gidx % 2]
                    if gidx >= 2:
                        t.wait_ge(sa.h, sab + gidx - 1)
                    for k in range(8):
                        hc = r * 8 + k
                        i = t.transpose(pb[:, k * 128:(k + 1) * 128],
                                        h_bf[:, hc * 128:(hc + 1) * 128], ident[:, :])
                    sp.inc(i)
        block.tensor(ftr)
        assert sp.n == peb + 16

        def fev(sc):
            sc.wait_ge(sp.h, g_pe)
            for st in range(NSTT):
                for r in range(4):
                    gidx = st * 4 + r
                    sc.wait_ge(sp.h, peb + gidx + 1)
                    pb3 = trp[gidx % 2].ap().rearrange("p (a b) -> p a b", b=128)
                    dst3 = hT3[:, r * 8:r * 8 + 8, st * 128:(st + 1) * 128]
                    sa.inc(sc.copy(dst3, pb3))
        block.scalar(fev)
        assert sa.n == sab + 16
        pst.exit(2)
        return {"hT_sa": sa.n, "xld_sv": svb + NSTT, "pe_end": peb + 16}

    def emit_attn_tile(src, dst, tile, grp=None):
        st_entry = sst.n
        pro = emit_prologue(src, tile, grp)
        hT_ready = pro["hT_sa"]

        psP = [pst.enter([128, 1024], F32), pst.enter([128, 1024], F32)]
        wrb = [s.n for s in swr]
        peb = sp.n
        svb = sv.n
        sab = sa.n
        DV = 17
        AC = 7

        build = (tile == 0)
        wbb = [s.n for s in swb]
        wrb_pre = [s.n for s in swr]

        def m1g_(g):
            g.wait_ge(sv.h, pro["xld_sv"])
            g.wait_ge(sst.h, st_entry)
            if not build:
                for k in range(4):
                    g.wait_ge(swb[k].h, wbb[k])
            for sc_i in range(SCN):
                a2 = 2 * (sc_i % 2)
                if sc_i >= 2:
                    g.wait_ge(sp.h, peb + 2 * (sc_i - 1))
                    if build:
                        g.wait_ge(swb[a2].h, wbb[a2] + 16 * (sc_i // 2))
                        g.wait_ge(swb[a2 + 1].h, wbb[a2 + 1] + 16 * (sc_i // 2))
                su, sg_ = (d["upw"], d["gw"]) if build else (CW["upw"], CW["gw"])
                swr[a2].inc(g.dma_start(wring[a2], su[sc_i]), 16)
                swr[a2 + 1].inc(g.dma_start(wring[a2 + 1], sg_[sc_i]), 16)
        block.gpsimd(m1g_)

        if build:
            def m1wb(s):
                for sc_i in range(SCN):
                    a2 = 2 * (sc_i % 2)
                    s.wait_ge(swr[a2].h, wrb_pre[a2] + 16 * (sc_i // 2 + 1))
                    swb[a2].inc(s.dma_start(CW["upw"][sc_i], wring[a2]), 16)
                    s.wait_ge(swr[a2 + 1].h, wrb_pre[a2 + 1] + 16 * (sc_i // 2 + 1))
                    swb[a2 + 1].inc(s.dma_start(CW["gw"][sc_i], wring[a2 + 1]), 16)
            block.sync(m1wb)

        def m1t(t):
            t.wait_ge(sa.h, hT_ready)
            for sc_i in range(SCN):
                a2 = 2 * (sc_i % 2)
                t.wait_ge(swr[a2].h, wrb[a2] + 16 * (sc_i // 2 + 1))
                t.wait_ge(swr[a2 + 1].h, wrb[a2 + 1] + 16 * (sc_i // 2 + 1))
                if sc_i >= 2:
                    t.wait_ge(sv.h, svb + DV * (sc_i - 2) + 3)
                pA = psP[sc_i % 2][:, 0:512]
                pB = psP[sc_i % 2][:, 512:1024]
                for hc in range(HC):
                    i = t.matmul(pA, wring[2 * (sc_i % 2)][:, hc * 128:(hc + 1) * 128],
                                 hT[:, hc * TILE:hc * TILE + TILE],
                                 start=(hc == 0), stop=(hc == HC - 1))
                sp.inc(i)
                for hc in range(HC):
                    i = t.matmul(pB, wring[2 * (sc_i % 2) + 1][:, hc * 128:(hc + 1) * 128],
                                 hT[:, hc * TILE:hc * TILE + TILE],
                                 start=(hc == 0), stop=(hc == HC - 1))
                sp.inc(i)
        block.tensor(m1t)
        assert sp.n == peb + 2 * SCN

        def m1a(sc):
            for sc_i in range(SCN):
                base_v = svb + DV * sc_i
                sc.wait_ge(sv.h, base_v + 5)
                sc.wait_ge(sa.h, sa.n)
                sa.inc(sc.activation(mx[4], mx[2], AF.Ln))
                for m in range(1, G + 1):
                    dt_slot = mx[5] if m % 2 == 1 else mx[2]
                    if m <= 2:
                        sc.wait_ge(sv.h, base_v + 9)
                    else:
                        tmp_idx = {1: 10, 2: 11, 3: 12, 4: 13}[m - 2]
                        sc.wait_ge(sv.h, base_v + tmp_idx)
                    sc.wait_ge(sa.h, sa.n)
                    sa.inc(sc.activation(dt_slot, mx[4], AF.Exp, scale=float(m),
                                         bias=ndf_sb[:, m - 1:m]))
        block.scalar(m1a)
        assert sa.n == sab + AC * SCN

        def m1v(v):
            for sc_i in range(SCN):
                base_a = sab + AC * sc_i

                def w():
                    v.wait_ge(sv.h, sv.n)
                pA = psP[sc_i % 2][:, 0:512]
                pB = psP[sc_i % 2][:, 512:1024]
                v.wait_ge(sp.h, peb + 2 * sc_i + 1)
                w()
                sv.inc(v.tensor_copy(tmpT[:, sc_i * TILE:sc_i * TILE + TILE], pA))
                sv.inc(v.tensor_copy(mx[0], pA))
                v.wait_ge(sp.h, peb + 2 * sc_i + 2)
                sv.inc(v.tensor_scalar(mx[1], pB, lp_sb[:, sc_i:sc_i + 1], None,
                                       op0=ALU.subtract))
                if sc_i >= 1:
                    v.wait_ge(sa.h, sab + AC * (sc_i - 1) + 6)
                w()
                sv.inc(v.tensor_scalar(mx[5], mx[1], -1.0, None, op0=ALU.mult))
                if sc_i >= 1:
                    v.wait_ge(sa.h, sab + AC * sc_i)
                w()
                sv.inc(v.tensor_max(mx[2], mx[1], mx[5]))
                w()
                sv.inc(v.tensor_scalar(mx[3], mx[1], 0.0, None, op0=ALU.is_gt))
                w()
                sv.inc(v.tensor_scalar(mx[5], mx[3], 2.0, -1.0, op0=ALU.mult,
                                       op1=ALU.add))
                w()
                sv.inc(v.tensor_scalar(mx[3], mx[2], 2.5, None, op0=ALU.is_le))
                w()
                sv.inc(v.tensor_mul(mx[1], mx[0], mx[5]))
                nops = 9
                for m in range(1, G + 1):
                    dt_slot = mx[5] if m % 2 == 1 else mx[2]
                    src_t = mx[1] if m % 2 == 1 else mx[0]
                    slot = tmpT[:, (m * SCN + sc_i) * TILE:(m * SCN + sc_i) * TILE + TILE]
                    v.wait_ge(sa.h, base_a + 1 + m)
                    w()
                    if m <= 4:
                        sv.inc(v.tensor_mul(slot, dt_slot, src_t)); nops += 1
                    else:
                        sv.inc(v.tensor_mul(mx7, dt_slot, src_t))
                        w()
                        sv.inc(v.tensor_mul(slot, mx7, mx[3]))
                        nops += 2
                assert nops == DV
        block.vector(m1v)
        assert sv.n == svb + DV * SCN
        pst.exit(2)
        m1_pe_end = sp.n
        m1_sv_end = sv.n

        psY = pst.enter([128, 4096], F32)
        NTL = 7 * SCN
        sdb = [s.n for s in sdr]
        peb2 = sp.n
        svb2 = sv.n
        stb2 = sst.n

        dwbb = [s.n for s in sdwb]
        sdb_pre = [s.n for s in sdr]

        def _m2src(j, hq, cached):
            term = j // SCN
            sc_i = j % SCN
            base = (CW if cached else d)
            if term == 0:
                return base["lao"][sc_i, :, hq * 1024:(hq + 1) * 1024]
            return base["fuse"][term - 1, sc_i, :, hq * 1024:(hq + 1) * 1024]

        def m2g_(g):
            g.wait_ge(sp.h, m1_pe_end)
            if not build:
                for k in range(6):
                    g.wait_ge(sdwb[k].h, dwbb[k])
            for hq in range(4):
                for j in range(NTL):
                    gi = hq * NTL + j
                    if gi >= 6:
                        g.wait_ge(sp.h, peb2 + gi - 5)
                        if build:
                            g.wait_ge(sdwb[gi % 6].h, dwbb[gi % 6] + 16 * (gi // 6))
                    sdr[gi % 6].inc(g.dma_start(dring[gi % 6],
                                                _m2src(j, hq, not build)), 16)
        block.gpsimd(m2g_)



        def m2t(t):
            t.wait_ge(sv.h, m1_sv_end)
            for hq in range(4):
                if hq > 0:
                    t.wait_ge(sv.h, svb2 + hq)
                for j in range(NTL):
                    gi = hq * NTL + j
                    t.wait_ge(sdr[gi % 6].h, sdb[gi % 6] + 16 * (gi // 6 + 1))
                    term = j // SCN
                    sc_i = j % SCN
                    for st in range(NSTT):
                        lhs = tmpT[:, (term * SCN + sc_i) * TILE + st * 128:
                                   (term * SCN + sc_i) * TILE + st * 128 + 128]
                        for half in range(2):
                            i = t.matmul(psY[:, st * 1024 + half * 512:
                                             st * 1024 + half * 512 + 512],
                                         lhs, dring[gi % 6][:, half * 512:half * 512 + 512],
                                         start=(j == 0), stop=(j == NTL - 1),
                                         skip_group_check=True)
                    sp.inc(i)
        block.tensor(m2t)
        assert sp.n == peb2 + 4 * NTL

        def m2v(v):
            v.wait_ge(sst.h, st_entry)
            for hq in range(4):
                v.wait_ge(sp.h, peb2 + NTL * (hq + 1))
                if hq >= 1:
                    v.wait_ge(sst.h, stb2 + 64 * hq)
                v.wait_ge(sv.h, sv.n)
                sv.inc(v.tensor_copy(evbw, psY[:, :]))
        block.vector(m2v)
        assert sv.n == svb2 + 4

        def m2s(s):
            for hq in range(4):
                if build:
                    for j in range(NTL):
                        gi = hq * NTL + j
                        s.wait_ge(sdr[gi % 6].h, sdb_pre[gi % 6] + 16 * (gi // 6 + 1))
                        sdwb[gi % 6].inc(s.dma_start(_m2src(j, hq, True),
                                                     dring[gi % 6]), 16)
                s.wait_ge(sv.h, svb2 + hq + 1)
                for st in range(NSTT):
                    sst.inc(s.dma_start(
                        ymix[tile % 2][st * 128:(st + 1) * 128,
                                       hq * 1024:(hq + 1) * 1024], evb[st]), 16)
        block.sync(m2s)
        pst.exit(1)
        m2_pe_end = sp.n
        m2_sv_end = sv.n
        m2_st_end = sst.n

        psP2 = [pst.enter([128, 1024], F32), pst.enter([128, 1024], F32)]
        wrb3 = [s.n for s in swr]
        peb3 = sp.n
        svb3 = sv.n
        sab3 = sa.n

        wbb3 = [s.n for s in swb]
        wrb3_pre = [s.n for s in swr]

        def t1g(g):
            g.wait_ge(sp.h, m2_pe_end)
            g.wait_ge(sst.h, m2_st_end)
            if not build:
                for k in range(4):
                    g.wait_ge(swb[k].h, wbb3[k])
            for fc in range(FCN):
                a2 = 2 * (fc % 2)
                if fc >= 2:
                    g.wait_ge(sv.h, svb3 + (fc - 1))
                    if build:
                        g.wait_ge(swb[a2].h, wbb3[a2] + 16 * (fc // 2))
                        g.wait_ge(swb[a2 + 1].h, wbb3[a2 + 1] + 16 * (fc // 2))
                sgt, sut = (d["gt"], d["ut"]) if build else (CW["gt"], CW["ut"])
                swr[a2].inc(g.dma_start(wring[a2], sgt[fc]), 16)
                swr[a2 + 1].inc(g.dma_start(wring[a2 + 1], sut[fc]), 16)
        block.gpsimd(t1g)

        if build:
            def t1wb(s):
                for fc in range(FCN):
                    a2 = 2 * (fc % 2)
                    s.wait_ge(swr[a2].h, wrb3_pre[a2] + 16 * (fc // 2 + 1))
                    swb[a2].inc(s.dma_start(CW["gt"][fc], wring[a2]), 16)
                    s.wait_ge(swr[a2 + 1].h, wrb3_pre[a2 + 1] + 16 * (fc // 2 + 1))
                    swb[a2 + 1].inc(s.dma_start(CW["ut"][fc], wring[a2 + 1]), 16)
            block.sync(t1wb)

        def t1t(t):
            t.wait_ge(sv.h, m2_sv_end)
            for fc in range(FCN):
                a2 = 2 * (fc % 2)
                t.wait_ge(swr[a2].h, wrb3[a2] + 16 * (fc // 2 + 1))
                t.wait_ge(swr[a2 + 1].h, wrb3[a2 + 1] + 16 * (fc // 2 + 1))
                if fc >= 2:
                    t.wait_ge(sv.h, svb3 + (fc - 1))
                pA = psP2[fc % 2][:, 0:512]
                pB = psP2[fc % 2][:, 512:1024]
                for hc in range(HC):
                    i = t.matmul(pA, wring[2 * (fc % 2)][:, hc * 128:(hc + 1) * 128],
                                 hT[:, hc * TILE:hc * TILE + TILE],
                                 start=(hc == 0), stop=(hc == HC - 1))
                sp.inc(i)
                for hc in range(HC):
                    i = t.matmul(pB, wring[2 * (fc % 2) + 1][:, hc * 128:(hc + 1) * 128],
                                 hT[:, hc * TILE:hc * TILE + TILE],
                                 start=(hc == 0), stop=(hc == HC - 1))
                sp.inc(i)
        block.tensor(t1t)
        assert sp.n == peb3 + 2 * FCN

        def t1a(sc):
            for fc in range(FCN):
                sc.wait_ge(sp.h, peb3 + 2 * fc + 1)
                if fc >= 2:
                    sc.wait_ge(sv.h, svb3 + (fc - 1))
                sc.wait_ge(sa.h, sa.n)
                sa.inc(sc.activation(mx[fc % 2], psP2[fc % 2][:, 0:512], AF.Silu))
        block.scalar(t1a)
        assert sa.n == sab3 + FCN

        def t1v(v):
            for fc in range(FCN):
                v.wait_ge(sa.h, sab3 + fc + 1)
                v.wait_ge(sp.h, peb3 + 2 * fc + 2)
                v.wait_ge(sv.h, sv.n)
                sv.inc(v.tensor_mul(aT[:, fc * TILE:(fc + 1) * TILE], mx[fc % 2],
                                    psP2[fc % 2][:, 512:1024]))
        block.vector(t1v)
        assert sv.n == svb3 + FCN
        pst.exit(2)
        t1_pe_end = sp.n
        t1_sv_end = sv.n

        psY2 = pst.enter([128, 4096], F32)
        sdb4 = [s.n for s in sdr]
        peb4 = sp.n
        svb4 = sv.n
        stb4 = sst.n
        sqb4 = [s.n for s in sqy]

        dwbb4 = [s.n for s in sdwb]
        sdb4_pre = [s.n for s in sdr]

        def t2g(g):
            g.wait_ge(sp.h, t1_pe_end)
            if not build:
                for k in range(6):
                    g.wait_ge(sdwb[k].h, dwbb4[k])
            for hq in range(4):
                for fc in range(FCN):
                    gi = hq * FCN + fc
                    if gi >= 6:
                        g.wait_ge(sp.h, peb4 + gi - 5)
                        if build:
                            g.wait_ge(sdwb[gi % 6].h, dwbb4[gi % 6] + 16 * (gi // 6))
                    sdt = d["dtw"] if build else CW["dtw"]
                    sdr[gi % 6].inc(g.dma_start(
                        dring[gi % 6], sdt[fc, :, hq * 1024:(hq + 1) * 1024]), 16)
        block.gpsimd(t2g)



        def t2s(s):
            s.wait_ge(sv.h, t1_sv_end)
            s.wait_ge(sst.h, m2_st_end)
            for hq in range(4):
                if hq >= 1:
                    s.wait_ge(sst.h, stb4 + 64 * hq)
                if hq >= 1:
                    s.wait_ge(sv.h, svb4 + 2 * hq)
                for st in range(NSTT):
                    r0 = tile * TILE + st * 128
                    sqy[st].inc(s.dma_start(xq[st], src[r0:r0 + 128,
                                                        hq * 1024:(hq + 1) * 1024]), 16)
                    sqy[st].inc(s.dma_start(yq[st],
                                            ymix[tile % 2][st * 128:(st + 1) * 128,
                                                           hq * 1024:(hq + 1) * 1024]), 16)
                if build:
                    for fc in range(FCN):
                        gi = hq * FCN + fc
                        s.wait_ge(sdr[gi % 6].h, sdb4_pre[gi % 6] + 16 * (gi // 6 + 1))
                        sdwb[gi % 6].inc(s.dma_start(
                            CW["dtw"][fc, :, hq * 1024:(hq + 1) * 1024],
                            dring[gi % 6]), 16)
                s.wait_ge(sv.h, svb4 + 2 * (hq + 1))
                for st in range(NSTT):
                    r0 = tile * TILE + st * 128
                    sst.inc(s.dma_start(dst[r0:r0 + 128, hq * 1024:(hq + 1) * 1024],
                                        yq[st]), 16)
        block.sync(t2s)

        def t2t(t):
            t.wait_ge(sv.h, t1_sv_end)
            for hq in range(4):
                if hq > 0:
                    t.wait_ge(sv.h, svb4 + 2 * (hq - 1) + 1)
                for fc in range(FCN):
                    gi = hq * FCN + fc
                    t.wait_ge(sdr[gi % 6].h, sdb4[gi % 6] + 16 * (gi // 6 + 1))
                    for st in range(NSTT):
                        lhs = aT[:, fc * TILE + st * 128:fc * TILE + st * 128 + 128]
                        for half in range(2):
                            i = t.matmul(psY2[:, st * 1024 + half * 512:
                                              st * 1024 + half * 512 + 512],
                                         lhs, dring[gi % 6][:, half * 512:half * 512 + 512],
                                         start=(fc == 0), stop=(fc == FCN - 1),
                                         skip_group_check=True)
                    sp.inc(i)
        block.tensor(t2t)
        assert sp.n == peb4 + 4 * FCN

        def t2v(v):
            v.wait_ge(sst.h, m2_st_end)
            for hq in range(4):
                v.wait_ge(sp.h, peb4 + FCN * (hq + 1))
                for st in range(NSTT):
                    v.wait_ge(sqy[st].h, sqb4[st] + 32 * (hq + 1))
                v.wait_ge(sv.h, sv.n)
                sv.inc(v.tensor_add(evbw, psY2[:, :], yqw))
                v.wait_ge(sv.h, sv.n)
                sv.inc(v.tensor_add(yqw, evbw, xqw))
        block.vector(t2v)
        assert sv.n == svb4 + 8
        pst.exit(1)
        gd.update(pe=sp.n, sv=sv.n, sa=sa.n, st=sst.n)

    def emit_ffn_tile(src, dst, tile, rm_idx, grp=None, cc_emit=None):
        st_entry = sst.n
        build = (rm_idx == 0 and tile == 0)
        pro = emit_prologue(src, tile, grp)
        hT_ready = pro["hT_sa"]

        psQ_t = pst.enter([128, TILE], F32)
        psQ = psQ_t[0:BOT, :]
        psRQ_t = pst.enter([128, TILE], F32)
        psRQ = psRQ_t[0:RD, :]

        def f1t(t):
            t.wait_ge(sa.h, hT_ready)
            t.wait_ge(swq.h, 16)
            for hc in range(HC):
                i = t.matmul(psQ[:, :], wq_sb[:, hc * 64:(hc + 1) * 64],
                             hT[:, hc * TILE:hc * TILE + TILE],
                             start=(hc == 0), stop=(hc == HC - 1))
            sp.inc(i)
        block.tensor(f1t)
        pe_q = sp.n

        def f1v(v):
            v.wait_ge(sp.h, pe_q)
            v.wait_ge(sv.h, sv.n)
            sv.inc(v.tensor_copy(q_sb[0:BOT, :], psQ[:, :]))
        block.vector(f1v)
        sv_q = sv.n

        def f1t2(t):
            t.wait_ge(sv.h, sv_q)
            sp.inc(t.matmul(psRQ[:, :], wrt_sb[:, :], q_sb[0:BOT, :],
                            start=True, stop=True))
        block.tensor(f1t2)
        pe_rq = sp.n

        def f1v2(v):
            v.wait_ge(sp.h, pe_rq)
            v.wait_ge(sv.h, sv.n)
            sv.inc(v.tensor_copy(rq_sb[0:RD, :], psRQ[:, :]))
        block.vector(f1v2)
        sv_rq = sv.n
        pst.exit(2)

        psR = pst.enter([128, NSLOT], F32)
        psS = pst.enter([128, NSLOT], F32)
        psM = [pst.enter([128, TILE], F32), pst.enter([128, TILE], F32)]
        trpF = [pst.enter([128, 1024], BF16), pst.enter([128, 1024], BF16)]

        peb_rs = sp.n
        svb_sm = sv.n
        sab_sm = sa.n
        SMV = 8

        def f2t(t):
            t.wait_ge(sv.h, sv_rq)
            for st in range(NSTT):
                if st > 0:
                    t.wait_ge(sv.h, svb_sm + SMV * (st - 1) + 4)
                for half in range(2):
                    t.matmul(psR[:, half * 512:(half + 1) * 512],
                             rq_sb[0:RD, st * 128:(st + 1) * 128],
                             rkt_sb[:, half * 512:(half + 1) * 512],
                             start=True, stop=True)
                for half in range(2):
                    i = t.matmul(psS[:, half * 512:(half + 1) * 512],
                                 q_sb[0:BOT, st * 128:(st + 1) * 128],
                                 akt_sb[:, half * 512:(half + 1) * 512],
                                 start=True, stop=True)
                sp.inc(i)
        block.tensor(f2t)
        assert sp.n == peb_rs + NSTT

        def f2a(sc):
            for st in range(NSTT):
                sc.wait_ge(sv.h, svb_sm + SMV * st + 6)
                sc.wait_ge(sa.h, sa.n)
                sa.inc(sc.activation(ex_sb, Sm_sb, AF.Exp, bias=sm1[:, 0:1],
                                     accum_out=sm1[:, 1:2]))
        block.scalar(f2a)
        assert sa.n == sab_sm + NSTT

        def f2v(v):
            for st in range(NSTT):
                def w():
                    v.wait_ge(sv.h, sv.n)
                v.wait_ge(sp.h, peb_rs + st + 1)
                w()
                sv.inc(v.scalar_tensor_tensor(R_sb, psR[:, :], RSQ48, logrel_sb[:, :],
                                              op0=ALU.mult, op1=ALU.add))
                w()
                sv.inc(v.max(out=t8_sb[:, :], in_=R_sb))
                w()
                sv.inc(v.tensor_scalar(mask_sb, R_sb, t8_sb[:, 7:8], None,
                                       op0=ALU.is_ge))
                sv.inc(v.scalar_tensor_tensor(Sp_sb, psS[:, :], RSQ64, logrel_sb[:, :],
                                              op0=ALU.mult, op1=ALU.add))
                if st >= 1:
                    v.wait_ge(sa.h, sab_sm + st)
                w()
                sv.inc(v.select(Sm_sb, mask_sb, Sp_sb, negbig_sb[:, :]))
                w()
                sv.inc(v.reduce_max(sm1[:, 0:1], Sm_sb, axis=AX.X, negate=True))
                v.wait_ge(sa.h, sab_sm + st + 1)
                w()
                sv.inc(v.reciprocal(sm1[:, 2:3], sm1[:, 1:2]))
                w()
                sv.inc(v.tensor_scalar(w_stage[:, st * 1024:(st + 1) * 1024], ex_sb,
                                       sm1[:, 2:3], None, op0=ALU.mult))
        block.vector(f2v)
        assert sv.n == svb_sm + SMV * NSTT
        sm_sv_end = sv.n

        sfb = [s.n for s in sfr]
        peb_m = sp.n
        svb_g = sv.n
        sab_g = sa.n
        GAV = 2
        GVV = 6

        fwbb = [s.n for s in sfwb]
        sfb_pre = [s.n for s in sfr]

        def f3g(g):
            g.wait_ge(sv.h, pro["xld_sv"])
            g.wait_ge(sp.h, pro["pe_end"])
            g.wait_ge(sst.h, st_entry)
            if not build:
                for k in range(3):
                    g.wait_ge(sfwb[k].h, fwbb[k])
            for uc in range(UCN):
                if uc >= 3:
                    g.wait_ge(sp.h, peb_m + uc - 2)
                    if build:
                        g.wait_ge(sfwb[uc % 3].h, fwbb[uc % 3] + 16 * (uc // 3))
                swu = d["wu1"] if build else CW["wu1"]
                sfr[uc % 3].inc(g.dma_start(ffring[uc % 3], swu[uc]), 16)
        block.gpsimd(f3g)

        if build:
            def f3wb(s):
                for uc in range(UCN):
                    s.wait_ge(sfr[uc % 3].h, sfb_pre[uc % 3] + 16 * (uc // 3 + 1))
                    sfwb[uc % 3].inc(s.dma_start(CW["wu1"][uc], ffring[uc % 3]), 16)
            block.sync(f3wb)

        def f3t(t):
            t.wait_ge(sa.h, hT_ready)
            for uc in range(UCN):
                t.wait_ge(sfr[uc % 3].h, sfb[uc % 3] + 16 * (uc // 3 + 1))
                if uc >= 2:
                    t.wait_ge(sv.h, svb_g + GVV * (uc - 2) + 1)
                pb = psM[uc % 2]
                for hc in range(HC):
                    i = t.matmul(pb[:, :], ffring[uc % 3][:, hc * 128:(hc + 1) * 128],
                                 hT[:, hc * TILE:hc * TILE + TILE],
                                 start=(hc == 0), stop=(hc == HC - 1))
                sp.inc(i)
        block.tensor(f3t)
        assert sp.n == peb_m + UCN

        def f3a(sc):
            sc.wait_ge(sst.h, st_entry)
            for uc in range(UCN):
                sc.wait_ge(sv.h, svb_g + GVV * uc + 1)
                sc.wait_ge(sa.h, sa.n)
                sa.inc(sc.activation(gt1, gz, AF.Square))
                sc.wait_ge(sv.h, svb_g + GVV * uc + 3)
                sc.wait_ge(sa.h, sa.n)
                sa.inc(sc.activation(gt2, gt1, AF.Tanh, scale=GELU_C))
        block.scalar(f3a)
        assert sa.n == sab_g + GAV * UCN

        def f3v(v):
            v.wait_ge(sst.h, st_entry)
            for uc in range(UCN):
                def w():
                    v.wait_ge(sv.h, sv.n)
                v.wait_ge(sp.h, peb_m + uc + 1)
                if uc >= 1:
                    v.wait_ge(sa.h, sab_g + GAV * (uc - 1) + 1)
                w()
                sv.inc(v.tensor_scalar(gz, psM[uc % 2][:, :], bu1_sb[:, uc:uc + 1],
                                       None, op0=ALU.add))
                v.wait_ge(sa.h, sab_g + GAV * uc + 1)
                w()
                sv.inc(v.tensor_mul(gt2, gt1, gz))
                w()
                sv.inc(v.scalar_tensor_tensor(gt1, gt2, GELU_A, gz, op0=ALU.mult,
                                              op1=ALU.add))
                v.wait_ge(sa.h, sab_g + GAV * uc + 2)
                w()
                sv.inc(v.tensor_scalar(gt1, gt2, 1.0, None, op0=ALU.add))
                w()
                sv.inc(v.tensor_mul(gt2, gt1, gz))
                w()
                sv.inc(v.tensor_scalar(m1g[:, uc * TILE:(uc + 1) * TILE], gt2, 0.5,
                                       None, op0=ALU.mult))
        block.vector(f3v)
        assert sv.n == svb_g + GVV * UCN
        gelu_sv_end = sv.n

        peb_tr = sp.n
        sab_tr = sa.n

        def f2bt(t):
            for st in range(NSTT):
                t.wait_ge(sv.h, svb_sm + SMV * (st + 1))
                pb = trpF[st % 2]
                if st >= 2:
                    t.wait_ge(sa.h, sab_tr + st - 1)
                for k in range(8):
                    i = t.transpose(pb[:, k * 128:(k + 1) * 128],
                                    w_stage[:, st * 1024 + k * 128:st * 1024 + (k + 1) * 128],
                                    ident[:, :])
                sp.inc(i)
        block.tensor(f2bt)
        assert sp.n == peb_tr + NSTT

        def f2ba(sc):
            for st in range(NSTT):
                sc.wait_ge(sp.h, peb_tr + st + 1)
                pb3 = trpF[st % 2].ap().rearrange("p (a b) -> p a b", b=128)
                sa.inc(sc.copy(wT3[:, :, st * 128:(st + 1) * 128], pb3))
        block.scalar(f2ba)
        assert sa.n == sab_tr + NSTT
        wT_ready = sa.n
        pst.exit(6)

        psL = pst.enter([128, 8], F32)
        peb_l = sp.n
        svb_l = sv.n

        def f4t(t):
            t.wait_ge(sv.h, gelu_sv_end)
            t.wait_ge(sa.h, wT_ready)
            for st in range(NSTT):
                if st > 0:
                    t.wait_ge(sv.h, svb_l + st)
                for uc in range(UCN):
                    i = t.matmul(psL[:, 0:1],
                                 m1g[:, uc * TILE + st * 128:uc * TILE + st * 128 + 128],
                                 wu2_sb[:, uc:uc + 1],
                                 start=(uc == 0), stop=(uc == UCN - 1))
                sp.inc(i)
        block.tensor(f4t)
        assert sp.n == peb_l + NSTT

        def f4v(v):
            for st in range(NSTT):
                v.wait_ge(sp.h, peb_l + st + 1)
                v.wait_ge(sv.h, sv.n)
                sv.inc(v.tensor_copy(learned_sb[:, st:st + 1], psL[:, 0:1]))
        block.vector(f4v)
        assert sv.n == svb_l + NSTT
        learned_sv = sv.n
        pst.exit(1)
        if cc_emit is not None:
            cc_emit(learned_sv)

        g_idx = tile * NSTT
        sab_gt = sa.n

        rm_tok = gd.get("rm_sv", 0)

        def fgates_a(sc):
            sc.wait_ge(sv.h, learned_sv)
            sc.wait_ge(sa.h, sa.n)
            sa.inc(sc.activation(sm1[:, 0:4], learned_sb[:, 0:4], AF.Sigmoid,
                                 bias=scal_sb[:, 2:3]))
            sc.wait_ge(sv.h, svb_gt_pred[0])
            sc.wait_ge(sa.h, sa.n)
            sa.inc(sc.activation(sm2[:, 0:4], sm2[:, 4:8], AF.Sigmoid))
        svb_gt_pred = [None]

        def fgates_v(v):
            def w():
                v.wait_ge(sv.h, sv.n)
            v.wait_ge(sv.h, rm_tok)
            v.wait_ge(sa.h, sab_gt + 1)
            w()
            sv.inc(v.tensor_scalar(sm1[:, 4:8], stL[:, g_idx:g_idx + 4],
                                   rm[rm_idx][:, 0:1], None, op0=ALU.mult))
            sv.inc(v.tensor_scalar(sm2[:, 0:4], sm1[:, 0:4], 2.5, None,
                                   op0=ALU.mult))
            w()
            sv.inc(v.scalar_tensor_tensor(sm1[:, 0:4], sm1[:, 4:8], 0.5,
                                          sm2[:, 0:4], op0=ALU.mult,
                                          op1=ALU.add))
            w()
            sv.inc(v.tensor_scalar(sm1[:, 4:8], sm1[:, 0:4], 0.0, 5.0,
                                   op0=ALU.max, op1=ALU.min))
            w()
            sv.inc(v.tensor_scalar(sm1[:, 0:4], sm1[:, 4:8], 0.5, 1.0 / 1.5,
                                   op0=ALU.subtract, op1=ALU.mult))
            w()
            sv.inc(v.tensor_scalar(sm1[:, 4:8], sm1[:, 0:4], 0.0, 1.0,
                                   op0=ALU.max, op1=ALU.min))
            w()
            sv.inc(v.tensor_scalar(sm2[:, 4:8], sm1[:, 4:8], scal_sb[:, 0:1],
                                   scal_sb[:, 1:2], op0=ALU.mult, op1=ALU.add))
            svb_gt_pred[0] = sv.n
            v.wait_ge(sa.h, sab_gt + 2)
            w()
            sv.inc(v.tensor_scalar(sm3[:, 4:8], sm2[:, 0:4], 0.05, None,
                                   op0=ALU.is_ge))
            w()
            sv.inc(v.scalar_tensor_tensor(sm3[:, 0:4], sm2[:, 0:4], 0.5,
                                          sm3[:, 4:8], op0=ALU.mult,
                                          op1=ALU.mult))
        block.vector(fgates_v)
        block.scalar(fgates_a)
        gates_sv = sv.n

        psA1 = pst.enter([128, 4096], F32)
        sdb5 = [s.n for s in sdr]
        peb5 = sp.n
        sab5 = sa.n

        dwbb5 = [s.n for s in sdwb]
        sdb5_pre = [s.n for s in sdr]

        def f5g(g):
            g.wait_ge(sv.h, pro["xld_sv"])
            if not build:
                for k in range(4):
                    g.wait_ge(sdwb[k].h, dwbb5[k])
            for rnd in range(4):
                for sc_i in range(SCN):
                    gi = rnd * SCN + sc_i
                    if gi >= 4:
                        g.wait_ge(sp.h, peb5 + gi - 3)
                        if build:
                            g.wait_ge(sdwb[gi % 4].h, dwbb5[gi % 4] + 16 * (gi // 4))
                    sav = d["auxv"] if build else CW["auxv"]
                    sdr[gi % 4].inc(g.dma_start(
                        avring[gi % 4], sav[sc_i, :, rnd * 1024:(rnd + 1) * 1024]), 16)
        block.gpsimd(f5g)

        if build:
            def f5wb(s):
                for rnd in range(4):
                    for sc_i in range(SCN):
                        gi = rnd * SCN + sc_i
                        s.wait_ge(sdr[gi % 4].h, sdb5_pre[gi % 4] + 16 * (gi // 4 + 1))
                        sdwb[gi % 4].inc(s.dma_start(
                            CW["auxv"][sc_i, :, rnd * 1024:(rnd + 1) * 1024],
                            avring[gi % 4]), 16)
            block.sync(f5wb)

        def f5t(t):
            t.wait_ge(sa.h, wT_ready)
            t.wait_ge(sv.h, learned_sv)
            for rnd in range(4):
                if rnd > 0:
                    t.wait_ge(sa.h, sab5 + rnd)
                for sc_i in range(SCN):
                    gi = rnd * SCN + sc_i
                    t.wait_ge(sdr[gi % 4].h, sdb5[gi % 4] + 16 * (gi // 4 + 1))
                    for k in range(8):
                        i = t.matmul(psA1[:, k * 512:(k + 1) * 512],
                                     avring[gi % 4][:, k * 128:(k + 1) * 128],
                                     wT[:, sc_i * TILE:(sc_i + 1) * TILE],
                                     start=(sc_i == 0), stop=(sc_i == SCN - 1),
                                     skip_group_check=True)
                    sp.inc(i)
        block.tensor(f5t)
        assert sp.n == peb5 + 32

        def f5a(sc):
            for rnd in range(4):
                sc.wait_ge(sp.h, peb5 + 8 * (rnd + 1))
                sc.wait_ge(sa.h, sa.n)
                sa.inc(sc.copy(aux1T[:, rnd * 8 * TILE:(rnd + 1) * 8 * TILE],
                               psA1[:, :]))
        block.scalar(f5a)
        assert sa.n == sab5 + 4
        aux1_sa = sa.n
        pst.exit(1)

        psV = [pst.enter([128, TILE], F32), pst.enter([128, TILE], F32)]
        wrb6 = [s.n for s in swr]
        peb6 = sp.n
        svb6 = sv.n

        wbb6 = [s.n for s in swb]
        wrb6_pre = [s.n for s in swr]

        def f6g(g):
            g.wait_ge(sp.h, peb5 + 32)
            if not build:
                for k in range(2):
                    g.wait_ge(swb[k].h, wbb6[k])
            for vc in range(2):
                svd = d["wvd"] if build else CW["wvd"]
                swr[vc].inc(g.dma_start(wvd_sb[vc], svd[vc]), 16)
        block.gpsimd(f6g)

        if build:
            def f6wb(s):
                for vc in range(2):
                    s.wait_ge(swr[vc].h, wrb6_pre[vc] + 16)
                    swb[vc].inc(s.dma_start(CW["wvd"][vc], wvd_sb[vc]), 16)
            block.sync(f6wb)

        def f6t(t):
            t.wait_ge(sa.h, aux1_sa)
            for vc in range(2):
                t.wait_ge(swr[vc].h, wrb6[vc] + 16)
                for hc in range(HC):
                    i = t.matmul(psV[vc][:, :], wvd_sb[vc][:, hc * 128:(hc + 1) * 128],
                                 aux1T[:, hc * TILE:hc * TILE + TILE],
                                 start=(hc == 0), stop=(hc == HC - 1))
                sp.inc(i)
        block.tensor(f6t)
        assert sp.n == peb6 + 2

        def f6v(v):
            for vc in range(2):
                v.wait_ge(sp.h, peb6 + vc + 1)
                v.wait_ge(sv.h, sv.n)
                sv.inc(v.tensor_copy(v1T[:, vc * TILE:(vc + 1) * TILE], psV[vc][:, :]))
        block.vector(f6v)
        assert sv.n == svb6 + 2
        v1_sv = sv.n
        pst.exit(2)

        wbb6b = [swb[2].n, swb[3].n]
        wrb6b_pre = [swr[2].n, swr[3].n]

        def f6g2(g):
            g.wait_ge(sp.h, peb6 + 2)
            if not build:
                for k in range(2):
                    g.wait_ge(swb[2 + k].h, wbb6b[k])
            for vc in range(2):
                svu = d["wvu"] if build else CW["wvu"]
                swr[2 + vc].inc(g.dma_start(wvu7[vc], svu[vc]), 16)
        block.gpsimd(f6g2)
        wvu_tok = [swr[2].n, swr[3].n]

        if build:
            def f6wb2(s):
                for vc in range(2):
                    s.wait_ge(swr[2 + vc].h, wrb6b_pre[vc] + 16)
                    swb[2 + vc].inc(s.dma_start(CW["wvu"][vc], wvu7[vc]), 16)
            block.sync(f6wb2)

        psYh = [pst.enter([128, 2048], F32), pst.enter([128, 2048], F32)]
        peb7 = sp.n
        svb7 = sv.n
        sqb7 = [s.n for s in sqy]
        stb7 = sst.n
        FEV = 4

        def f7s(s):
            s.wait_ge(sp.h, peb7)
            for st in range(NSTT):
                if st >= 1:
                    s.wait_ge(sv.h, svb7 + FEV * st)
                    s.wait_ge(sst.h, stb7 + 16 * 4 * st)
                r0 = tile * TILE + st * 128
                for hq in range(4):
                    sqy[hq].inc(s.dma_start(xq7[hq], src[r0:r0 + 128,
                                                         hq * 1024:(hq + 1) * 1024]), 16)
                for hq in range(4):
                    s.wait_ge(sv.h, svb7 + FEV * st + 2 * (hq // 2 + 1))
                    sst.inc(s.dma_start(dst[r0:r0 + 128, hq * 1024:(hq + 1) * 1024],
                                        xq7[hq]), 16)
        block.sync(f7s)

        def f7t(t):
            t.wait_ge(sv.h, v1_sv)
            t.wait_ge(swr[2].h, wvu_tok[0])
            t.wait_ge(swr[3].h, wvu_tok[1])
            for st in range(NSTT):
                for half in range(2):
                    if st > 0:
                        t.wait_ge(sv.h, svb7 + FEV * (st - 1) + 2 * (half + 1))
                    for vc in range(2):
                        for k in range(4):
                            kk = half * 4 + k
                            i = t.matmul(psYh[half][:, k * 512:(k + 1) * 512],
                                         v1T[:, vc * TILE + st * 128:vc * TILE + st * 128 + 128],
                                         wvu7[vc][:, kk * 512:(kk + 1) * 512],
                                         start=(vc == 0), stop=(vc == 1),
                                         skip_group_check=True)
                    sp.inc(i)
        block.tensor(f7t)
        assert sp.n == peb7 + 2 * NSTT


        def f7v(v):
            v.wait_ge(sst.h, st_entry)
            v.wait_ge(sv.h, gates_sv)
            for st in range(NSTT):
                def w():
                    v.wait_ge(sv.h, sv.n)
                gst = g_idx + st
                for half in range(2):
                    v.wait_ge(sp.h, peb7 + 2 * st + half + 1)
                    v.wait_ge(sqy[2 * half].h, sqb7[2 * half] + 16 * (st + 1))
                    v.wait_ge(sqy[2 * half + 1].h, sqb7[2 * half + 1] + 16 * (st + 1))
                    w()
                    sv.inc(v.tensor_scalar(evb7w[half], xq7w[half],
                                           stA2[:, gst:gst + 1], stB2[:, gst:gst + 1],
                                           op0=ALU.mult, op1=ALU.add))
                    w()
                    sv.inc(v.scalar_tensor_tensor(
                        xq7w[half], psYh[half][:, :],
                        sm3[:, st:st + 1], evb7w[half], op0=ALU.mult,
                        op1=ALU.add))
        block.vector(f7v)
        assert sv.n == svb7 + FEV * NSTT
        pst.exit(2)
        gd.update(pe=sp.n, sv=sv.n, sa=sa.n, st=sst.n)

    gt0, cc0 = emit_stats(d["x0"], 0)
    last = "x0"
    if upto >= 1:
        for tile in range(NT):
            emit_ffn_tile(d["x0"], d["x1"], tile, 0, grp=gt0[tile],
                          cc_emit=(cc0 if tile == 0 else None))
        last = "x1"
    if upto >= 2:
        gt1_, _ = emit_stats(d["x1"], None)
        for tile in range(NT):
            emit_attn_tile(d["x1"], d["x2"], tile, grp=gt1_[tile])
        last = "x2"
    if upto >= 3:
        gt2_, cc2 = emit_stats(d["x2"], 1)
        for tile in range(NT):
            emit_ffn_tile(d["x2"], d["y"], tile, 1, grp=gt2_[tile],
                          cc_emit=(cc2 if tile == 0 else None))
        last = "y"

    if last != "y":
        stn0 = sst.n
        g_pe, g_sv, g_sa = gd["pe"], gd["sv"], gd["sa"]

        def fcp(s):
            s.wait_ge(sp.h, g_pe)
            s.wait_ge(sv.h, g_sv)
            s.wait_ge(sa.h, g_sa)
            s.wait_ge(sst.h, stn0)
            for st in range(NSTG):
                if st >= 2:
                    s.wait_ge(sst.h, stn0 + 16 * (st - 1))
                sx2[st % 2].inc(s.dma_start(xld[st % 2],
                                            d[last][st * 128:(st + 1) * 128, :]), 16)
                s.wait_ge(sx2[st % 2].h, sx2[st % 2].n)
                sst.inc(s.dma_start(d["y"][st * 128:(st + 1) * 128, :], xld[st % 2]), 16)
        block.sync(fcp)

    fins = [(sst, sst.n)] + [(s, s.n) for s in
                              ([si, swq, scx] + sx2 + swr + sdr + sfr + sqy
                               + swb + sdwb + sfwb)]

    def fend(s):
        for sem, n in fins:
            s.wait_ge(sem.h, n)
    block.sync(fend)

    ctx.close()
    return nc


def prep_inputs(inp):
    f32 = np.float32

    def A(x):
        return np.ascontiguousarray(np.asarray(x, dtype=f32))

    w = {}
    w["gt"] = _blk_lhsT(A(inp["L0_attn_ffn_gate_w"]))
    w["ut"] = _blk_lhsT(A(inp["L0_attn_ffn_up_w"]))
    w["dtw"] = A(np.asarray(inp["L0_attn_ffn_down_w"], f32).T).reshape(FCN, 128, H)
    w["upw"] = _blk_lhsT(A(inp["L0_attn_up_w"]))
    w["gw"] = _blk_lhsT(A(inp["L0_attn_gate_w"]))
    w["lao"] = A(np.asarray(inp["L0_attn_local_approx_output"], f32).T).reshape(SCN, 128, H)
    w["fuse"] = A(np.transpose(np.asarray(inp["L0_attn_fuse_weight"], f32),
                               (2, 1, 0))).reshape(G, SCN, 128, H)
    w["wq"] = _blk_lhsT(A(inp["L0_ffn_Wq"]), oc=64).reshape(128, HC * 64)
    w["wu1"] = _blk_lhsT(A(inp["L0_ffn_Wu1"]))
    w["wvd"] = _blk_lhsT(A(inp["L0_ffn_Wvd"]))
    w["wvu"] = A(np.asarray(inp["L0_ffn_Wvu"], f32).T).reshape(2, 128, H)
    w["auxv"] = A(inp["L0_ffn_aux_values"]).reshape(SCN, 128, H)
    w["akt"] = A(np.asarray(inp["L0_ffn_aux_keys"], f32).T)
    w["wrt"] = A(np.asarray(inp["L0_ffn_Wr"], f32).T)
    w["lp"] = A(np.asarray(inp["L0_attn_local_point"], f32).reshape(SCN, 128).T)
    w["bu1"] = A(np.asarray(inp["L0_ffn_bu1"], f32).reshape(UCN, 128).T)
    w["wu2"] = A(np.asarray(inp["L0_ffn_Wu2"], f32).reshape(UCN, 128).T)
    w["df"] = A(np.broadcast_to(np.asarray(inp["L0_attn_discount_factor"], f32),
                                (128, G)))
    w["scal"] = A(np.broadcast_to(np.stack([
        np.float32(inp["L0_ffn_gate_w1"]), np.float32(inp["L0_ffn_gate_bias"]),
        np.asarray(inp["L0_ffn_bu2"], f32).reshape(())]).astype(f32), (128, 3)))
    w["logrel"] = A(np.broadcast_to(np.asarray(inp["L0_ffn_log_rel"], f32),
                                    (128, NSLOT)))
    x = np.asarray(inp["L0_attn_hidden_states"], f32).reshape(TCORE * NCORES, H)
    return w, x


def kernel(upto=3, debug_xs=False, **inputs):
    w, x = prep_inputs(inputs)
    nc = build_program(upto=upto, debug_xs=debug_xs)
    in_maps = []
    for c in range(NCORES):
        m = dict(w)
        m["x0"] = np.ascontiguousarray(x[c * TCORE:(c + 1) * TCORE])
        in_maps.append(m)
    res = run_bass_kernel_spmd(nc, in_maps, core_ids=list(range(NCORES)))
    out = np.concatenate([res.results[c]["y"] for c in range(NCORES)], axis=0)
    extras = None
    if debug_xs:
        extras = {k: np.concatenate([res.results[c][k] for c in range(NCORES)], axis=0)
                  for k in ("x1", "x2")}
    out = out.reshape(2, TCORE * NCORES // 2, H)
    return (out, extras) if debug_xs else out


# version 11
# speedup vs baseline: 1.0123x; 1.0039x over previous
import numpy as np
from contextlib import ExitStack
import concourse.bass as bass
import concourse.mybir as mybir
from concourse.bass_utils import run_bass_kernel_spmd

F32 = mybir.dt.float32
BF16 = mybir.dt.bfloat16
U8 = mybir.dt.uint8
AF = mybir.ActivationFunctionType
ALU = mybir.AluOpType
AX = mybir.AxisListType

H = 4096; HC = 32
S = 1024; SCN = 8
FF = 9984; FCN = 78
G = 6
BOT = 64; RD = 48; VB = 256; UCN = 8
NSLOT = 1024
TCORE = 2048; TILE = 512; NT = 4; NSTT = 4
NSTG = 16
RSQ48 = 0.14433756729740643
RSQ64 = 0.125
NEGBIG = -30000.0
GELU_C = 0.7978845608028654
GELU_A = 0.044715
KB = 1024
NCORES = 8


def _blk_lhsT(w, oc=128):
    O, I = w.shape
    nO, nI = O // oc, I // 128
    b = w.reshape(nO, oc, nI, 128).transpose(0, 3, 2, 1)
    return np.ascontiguousarray(b).reshape(nO, 128, nI * oc)


class Sem:
    __slots__ = ("h", "n")

    def __init__(self, ctx, nc, name):
        self.h = ctx.enter_context(nc.semaphore(name))
        self.n = 0

    def inc(self, instr, k=1):
        instr.then_inc(self.h, k)
        self.n += k
        return self.n


class PsumStack:
    def __init__(self, nc):
        self.nc = nc
        self.stack = []
        self.k = 0

    def enter(self, shape, dt):
        self.k += 1
        cm = self.nc.psum_tensor(f"ps{self.k}", shape, dt)
        t = cm.__enter__()
        self.stack.append(cm)
        return t

    def exit(self, k=1):
        for _ in range(k):
            self.stack.pop().__exit__(None, None, None)


def build_program(upto=3, debug_xs=False):
    ctx = ExitStack()
    nc = bass.Bass("TRN2", target_bir_lowering=False)
    d = {}

    def din(name, shape, dt=F32):
        d[name] = nc.dram_tensor(name, list(shape), dt, kind="ExternalInput").ap()

    din("x0", (TCORE, H))
    din("gt", (FCN, 128, H)); din("ut", (FCN, 128, H)); din("dtw", (FCN, 128, H))
    din("upw", (SCN, 128, H)); din("gw", (SCN, 128, H))
    din("lao", (SCN, 128, H)); din("fuse", (G, SCN, 128, H))
    din("wq", (128, HC * 64)); din("wu1", (UCN, 128, H))
    din("wvd", (2, 128, H)); din("wvu", (2, 128, H)); din("auxv", (SCN, 128, H))
    din("akt", (BOT, NSLOT)); din("wrt", (BOT, RD))
    din("lp", (128, SCN)); din("bu1", (128, UCN)); din("wu2", (128, UCN))
    din("df", (128, G)); din("scal", (128, 3)); din("logrel", (128, NSLOT))

    okind = "ExternalOutput" if debug_xs else "Internal"
    d["x1"] = nc.dram_tensor("x1", [TCORE, H], F32, kind=okind).ap()
    d["x2"] = nc.dram_tensor("x2", [TCORE, H], F32, kind=okind).ap()
    d["y"] = nc.dram_tensor("y", [TCORE, H], F32, kind="ExternalOutput").ap()
    ymix = [nc.dram_tensor(f"ymix{i}", [TILE, H], F32, kind="Internal").ap()
            for i in range(2)]
    CW = {}
    for name, shape in [("gt", (FCN, 128, H)), ("ut", (FCN, 128, H)),
                        ("dtw", (FCN, 128, H)), ("upw", (SCN, 128, H)),
                        ("gw", (SCN, 128, H)), ("lao", (SCN, 128, H)),
                        ("fuse", (G, SCN, 128, H)), ("wu1", (UCN, 128, H)),
                        ("auxv", (SCN, 128, H)), ("wvd", (2, 128, H)),
                        ("wvu", (2, 128, H))]:
        CW[name] = nc.dram_tensor(name + "_c", list(shape), BF16, kind="Internal").ap()
    cc_in = [nc.dram_tensor(f"cc{i}_in", [1, 8], F32, kind="Internal") for i in (1, 2)]
    cc_out = [nc.dram_tensor(f"cc{i}_out", [1, 8], F32, kind="Internal") for i in (1, 2)]

    def sbt(name, shape, dt):
        return ctx.enter_context(nc.sbuf_tensor(name, list(shape), dt))

    ident = sbt("ident", [128, 128], BF16)
    ones_col = sbt("ones_col", [128, 1], F32)
    ones_row = sbt("ones_row", [1, 128], F32)
    lp_sb = sbt("lp_sb", [128, SCN], F32)
    bu1_sb = sbt("bu1_sb", [128, UCN], F32)
    wu2_sb = sbt("wu2_sb", [128, UCN], F32)
    df_sb = sbt("df_sb", [128, G], F32)
    ndf_sb = sbt("ndf_sb", [128, G], F32)
    scal_sb = sbt("scal_sb", [128, 3], F32)
    logrel_sb = sbt("logrel_sb", [128, NSLOT], F32)
    negbig_sb = sbt("negbig_sb", [128, NSLOT], F32)
    akt_sb = sbt("akt_sb", [BOT, NSLOT], F32)
    wrt_sb = sbt("wrt_sb", [BOT, RD], F32)
    rkt_sb = sbt("rkt_sb", [RD, NSLOT], F32)
    wq_sb = sbt("wq_sb", [128, HC * 64], BF16)
    stM = sbt("stM", [128, NSTG], F32)
    stR = sbt("stR", [128, NSTG], F32)
    stRh = sbt("stRh", [128, NSTG], F32)
    stA2 = sbt("stA2", [128, NSTG], F32)
    stB2 = sbt("stB2", [128, NSTG], F32)
    stL = sbt("stL", [128, NSTG], F32)
    ssA = sbt("ssA", [128, NSTG], F32)
    ssQ = sbt("ssQ", [128, NSTG], F32)
    rm = [sbt("rm1", [128, 1], F32), sbt("rm2", [128, 1], F32)]
    t8_sb = sbt("t8_sb", [128, 8], F32)
    sm1 = sbt("sm1", [128, 8], F32)
    sm2 = sbt("sm2", [128, 8], F32)
    sm3 = sbt("sm3", [128, 8], F32)
    learned_sb = sbt("learned_sb", [128, NSTT], F32)
    cc_sb = sbt("cc_sb", [1, 8], F32)
    w_stage = sbt("w_stage", [128, 4096], BF16)

    AB = 172 * KB
    arena = ctx.enter_context(nc.sbuf_tensor("arena", [128, AB], U8))
    a_f32 = arena.bitcast(F32).ap()
    a_bf = arena.bitcast(BF16).ap()
    a_u8 = arena.ap()

    def cf32(off_b, n):
        assert off_b % 4 == 0 and off_b + 4 * n <= AB, (off_b, n)
        return a_f32[:, off_b // 4: off_b // 4 + n]

    def cbf(off_b, n):
        assert off_b % 2 == 0 and off_b + 2 * n <= AB, (off_b, n)
        return a_bf[:, off_b // 2: off_b // 2 + n]

    def cu8(off_b, n):
        assert off_b + n <= AB
        return a_u8[:, off_b: off_b + n]

    hT = cbf(0, HC * TILE)
    hT3 = a_bf[:, 0:HC * TILE].rearrange("p (hc t) -> p hc t", t=TILE)
    R1 = 32 * KB
    tmpT = cbf(R1, 7 * SCN * TILE)
    aT = cbf(R1, FCN * TILE)
    aux1T = cbf(R1, HC * TILE)
    wT = cbf(R1 + 32 * KB, SCN * TILE)
    wT3 = a_bf[:, (R1 + 32 * KB) // 2:(R1 + 32 * KB) // 2 + SCN * TILE] \
        .rearrange("p (sc t) -> p sc t", t=TILE)
    v1T = cbf(R1 + 40 * KB, 2 * TILE)
    q_sb = cf32(R1 + 42 * KB, TILE)
    rq_sb = cf32(R1 + 44 * KB, TILE)
    m1g = cf32(R1 + 46 * KB, UCN * TILE)
    R_sb = cf32(R1 + 62 * KB, NSLOT)
    Sp_sb = cf32(R1 + 66 * KB, NSLOT)
    Sm_sb = cf32(R1 + 70 * KB, NSLOT)
    ex_sb = cf32(R1 + 74 * KB, NSLOT)
    mask_sb = cu8(R1 + 78 * KB, NSLOT)
    scr1k = cf32(R1 + 62 * KB, NSLOT)
    sxld = [cf32(R1, H), cf32(R1 + 16 * KB, H), cf32(R1 + 32 * KB, H)]

    R2 = 112 * KB
    xld = [cf32(R2, H), cf32(R2 + 16 * KB, H)]
    h_bf = cbf(R2 + 32 * KB, H)
    wring = [cbf(R2 + i * 8 * KB, H) for i in range(4)]
    mx = [cf32(R2 + 32 * KB + i * 2 * KB, TILE) for i in range(6)]
    dring = [cbf(R2 + i * 2 * KB, 1024) for i in range(6)]
    evb = [cf32(R2 + 12 * KB + i * 4 * KB, 1024) for i in range(4)]
    xq = [cf32(R2 + 28 * KB + i * 4 * KB, 1024) for i in range(4)]
    yq = [cf32(R2 + 44 * KB + i * 4 * KB, 1024) for i in range(4)]
    ffring = [cbf(R2 + 16 * KB + i * 8 * KB, H) for i in range(3)]
    avring = [cbf(R2 + i * 2 * KB, 1024) for i in range(4)]
    wvd_sb = [cbf(R2, H), cbf(R2 + 8 * KB, H)]
    mx7 = cf32(R2 + 56 * KB, TILE)
    wvu7 = [cbf(R2, H), cbf(R2 + 8 * KB, H)]
    xq7 = [cf32(R2 + 16 * KB + i * 4 * KB, 1024) for i in range(4)]
    evb7 = [cf32(R2 + 32 * KB + i * 4 * KB, 1024) for i in range(4)]
    xq7w = [cf32(R2 + 16 * KB, 2048), cf32(R2 + 24 * KB, 2048)]
    evb7w = [cf32(R2 + 32 * KB, 2048), cf32(R2 + 40 * KB, 2048)]
    evbw = cf32(R2 + 12 * KB, 4096)
    xqw = cf32(R2 + 28 * KB, 4096)
    yqw = cf32(R2 + 44 * KB, 4096)
    gz = cf32(R2 + 48 * KB, TILE)
    gt1 = cf32(R2 + 50 * KB, TILE)
    gt2 = cf32(R2 + 52 * KB, TILE)

    sx = Sem(ctx, nc, "sx")
    sst = Sem(ctx, nc, "sst")
    si = Sem(ctx, nc, "si")
    swq = Sem(ctx, nc, "swq")
    scx = Sem(ctx, nc, "scx")
    sx2 = [Sem(ctx, nc, f"sx2_{i}") for i in range(2)]
    ssl = [Sem(ctx, nc, f"ssl{i}") for i in range(3)]
    swr = [Sem(ctx, nc, f"swr{i}") for i in range(4)]
    sdr = [Sem(ctx, nc, f"sdr{i}") for i in range(6)]
    sfr = [Sem(ctx, nc, f"sfr{i}") for i in range(3)]
    sqy = [Sem(ctx, nc, f"sqy{i}") for i in range(4)]
    swb = [Sem(ctx, nc, f"swb{i}") for i in range(4)]
    sdwb = [Sem(ctx, nc, f"sdwb{i}") for i in range(6)]
    sfwb = [Sem(ctx, nc, f"sfwb{i}") for i in range(3)]
    sw = Sem(ctx, nc, "sw")
    sw2 = Sem(ctx, nc, "sw2")
    sp = Sem(ctx, nc, "sp")
    sv = Sem(ctx, nc, "sv")
    sa = Sem(ctx, nc, "sa")
    sg = Sem(ctx, nc, "sg")
    scc = Sem(ctx, nc, "scc")

    block = ctx.enter_context(nc.Block())
    pst = PsumStack(nc)
    gd = {"pe": 0, "sv": 0, "sa": 0, "st": 0}

    def init_sync(s):
        for name, dst in [("lp", lp_sb), ("bu1", bu1_sb), ("wu2", wu2_sb),
                          ("df", df_sb), ("scal", scal_sb), ("logrel", logrel_sb),
                          ("akt", akt_sb), ("wrt", wrt_sb)]:
            si.inc(s.dma_start(dst[:, :], d[name]), 16)
    block.sync(init_sync)
    n_init_loads = si.n

    def init_g(g):
        sg.inc(g.memset(negbig_sb[:, :], NEGBIG))
        sg.inc(g.memset(ident[:, :], 0.0))
        g.wait_ge(sg.h, sg.n)
        sg.inc(g.affine_select(out=ident[:, :], in_=ident[:, :],
                               compare_op=ALU.not_equal, fill=1.0, base=0,
                               pattern=[[-1, 128]], channel_multiplier=1))
        sg.inc(g.memset(ones_col[:, :], 1.0))
        sg.inc(g.memset(ones_row[:, :], 1.0))
        swq.inc(g.dma_start(wq_sb[:, :], d["wq"]), 16)
    block.gpsimd(init_g)

    def init_v(v):
        v.wait_ge(si.h, n_init_loads)
        sv.inc(v.tensor_scalar(ndf_sb[:, :], df_sb[:, :], -1.0, None, op0=ALU.mult))
    block.vector(init_v)

    psrk_t = pst.enter([128, NSLOT], F32)

    def init_t(t):
        t.wait_ge(si.h, n_init_loads)
        t.matmul(psrk_t[0:RD, 0:512], wrt_sb[:, :], akt_sb[:, 0:512], start=True, stop=True)
        sp.inc(t.matmul(psrk_t[0:RD, 512:1024], wrt_sb[:, :], akt_sb[:, 512:1024],
                        start=True, stop=True))
    block.tensor(init_t)
    pe_rk = sp.n

    def init_v2(v):
        v.wait_ge(sp.h, pe_rk)
        sv.inc(v.tensor_copy(rkt_sb[:, :], psrk_t[0:RD, :]))
    block.vector(init_v2)
    pst.exit(1)
    gd.update(pe=sp.n, sv=sv.n, sa=sa.n, st=sst.n)

    def emit_stats(src, cc_idx):
        slb = [s.n for s in ssl]
        svb = sv.n
        sab = sa.n
        st_guard = sst.n
        g_pe, g_sv, g_sa = gd["pe"], gd["sv"], gd["sa"]
        grp_tok = []

        def fsq(sc):
            sc.wait_ge(sst.h, st_guard)
            sc.wait_ge(sp.h, g_pe)
            sc.wait_ge(sv.h, g_sv)
            for st in range(3):
                ssl[st % 3].inc(sc.dma_start(sxld[st % 3],
                                             src[st * 128:(st + 1) * 128, :]), 16)
            for st in range(NSTG):
                sc.wait_ge(ssl[st % 3].h, slb[st % 3] + 16 * (st // 3 + 1))
                if st >= 1:
                    sc.wait_ge(sv.h, svb + 2 * st)
                for k in range(4):
                    i = sc.activation(scr1k, sxld[st % 3][:, k * 1024:(k + 1) * 1024],
                                      AF.Square, accum_out=sm1[:, k:k + 1])
                sa.inc(i)
                nxt = st + 3
                if nxt < NSTG:
                    sc.wait_ge(sv.h, svb + 2 * (st + 1))
                    ssl[nxt % 3].inc(sc.dma_start(sxld[nxt % 3],
                                                  src[nxt * 128:(nxt + 1) * 128, :]), 16)
        block.scalar(fsq)
        assert sa.n == sab + NSTG

        def fred(v):
            for st in range(NSTG):
                v.wait_ge(ssl[st % 3].h, slb[st % 3] + 16 * (st // 3 + 1))
                v.wait_ge(sv.h, sv.n)
                sv.inc(v.reduce_sum(ssA[:, st:st + 1], sxld[st % 3][:, :], axis=AX.X))
                v.wait_ge(sa.h, sab + st + 1)
                v.wait_ge(sv.h, sv.n)
                sv.inc(v.reduce_sum(ssQ[:, st:st + 1], sm1[:, 0:4], axis=AX.X))
        block.vector(fred)
        assert sv.n == svb + 2 * NSTG

        for g_ in range(4):
            c = slice(4 * g_, 4 * g_ + 4)
            svg = svb + 2 * (4 * g_ + 4)

            def ffin(v, c=c, svg=svg):
                def w():
                    v.wait_ge(sv.h, sv.n)
                v.wait_ge(sv.h, svg)
                w(); sv.inc(v.tensor_scalar(stM[:, c], ssA[:, c], 1.0 / H, None, op0=ALU.mult))
                sv.inc(v.tensor_scalar(ssQ[:, c], ssQ[:, c], 1.0 / H, None, op0=ALU.mult))
                w(); sv.inc(v.tensor_mul(ssA[:, c], stM[:, c], stM[:, c]))
                w(); sv.inc(v.tensor_sub(ssQ[:, c], ssQ[:, c], ssA[:, c]))
                w(); sv.inc(v.tensor_scalar(ssA[:, c], ssQ[:, c], 1e-5, None, op0=ALU.add))
                w(); sv.inc(v.reciprocal(stR[:, c], ssA[:, c]))
                w(); sv.inc(v.tensor_mul(stL[:, c], ssQ[:, c], stR[:, c]))
            block.vector(ffin)
            svn = sv.n

            def ffin_a(sc, c=c, svn=svn):
                sc.wait_ge(sv.h, svn)
                sc.wait_ge(sa.h, sa.n)
                sa.inc(sc.activation(ssQ[:, c], ssA[:, c], AF.Ln))
                sc.wait_ge(sa.h, sa.n)
                sa.inc(sc.activation(stR[:, c], ssQ[:, c], AF.Exp, scale=-0.5))
                sc.wait_ge(sa.h, sa.n)
                sa.inc(sc.activation(stL[:, c], stL[:, c], AF.Ln, bias=1.0))
            block.scalar(ffin_a)
            san = sa.n

            def ffin_v(v, c=c, san=san):
                v.wait_ge(sa.h, san)
                v.wait_ge(sv.h, sv.n)
                sv.inc(v.tensor_scalar(stRh[:, c], stR[:, c], 0.5, None, op0=ALU.mult))
                v.wait_ge(sv.h, sv.n)
                sv.inc(v.tensor_scalar(stA2[:, c], stRh[:, c], 1.0, None, op0=ALU.add))
                sv.inc(v.tensor_mul(stB2[:, c], stM[:, c], stRh[:, c]))
                v.wait_ge(sv.h, sv.n)
                sv.inc(v.tensor_scalar(stB2[:, c], stB2[:, c], -1.0, None, op0=ALU.mult))
            block.vector(ffin_v)
            grp_tok.append((sv.n, sa.n))

        def emit_cc(guard_sv=0):
            pslv_t = pst.enter([128, NSTG], F32)
            psbc = pst.enter([128, 1], F32)
            svn2, san2 = grp_tok[-1]

            def flv(t):
                t.wait_ge(sa.h, san2)
                t.wait_ge(sv.h, max(g_sv, guard_sv))
                sp.inc(t.matmul(pslv_t[0:1, :], ones_col[:, :], stL[:, :],
                                start=True, stop=True))
            block.tensor(flv)
            pen = sp.n

            def flv_v(v):
                v.wait_ge(sp.h, pen)
                v.wait_ge(sv.h, sv.n)
                sv.inc(v.reduce_sum(cc_sb[0:1, 0:1], pslv_t[0:1, :], axis=AX.X))
            block.vector(flv_v)
            svn3 = sv.n

            def fst(s):
                s.wait_ge(sv.h, svn3)
                sst.inc(s.dma_start(cc_in[cc_idx].ap()[0:1, 0:1], cc_sb[0:1, 0:1]), 16)
            block.sync(fst)
            stn = sst.n

            def fcc(g):
                g.wait_ge(sst.h, stn)
                scc.inc(g.collective_compute(
                    "AllReduce", ALU.add, replica_groups=[list(range(NCORES))],
                    ins=[cc_in[cc_idx].ap().opt()],
                    outs=[cc_out[cc_idx].ap().opt()]), 1)
            block.gpsimd(fcc)
            ccn = scc.n

            def fld(s):
                s.wait_ge(scc.h, ccn)
                scx.inc(s.dma_start(cc_sb[0:1, 1:2], cc_out[cc_idx].ap()[0:1, 0:1]), 16)
            block.sync(fld)
            sxn = scx.n

            def fmn(v):
                v.wait_ge(scx.h, sxn)
                v.wait_ge(sv.h, sv.n)
                sv.inc(v.tensor_scalar(cc_sb[0:1, 2:3], cc_sb[0:1, 1:2],
                                       1.0 / 16384.0, 1e-6, op0=ALU.mult, op1=ALU.add))
                v.wait_ge(sv.h, sv.n)
                sv.inc(v.reciprocal(cc_sb[0:1, 3:4], cc_sb[0:1, 2:3]))
            block.vector(fmn)
            svn4 = sv.n

            def fbc(t):
                t.wait_ge(sv.h, svn4)
                sp.inc(t.matmul(psbc[:, :], ones_row[:, :], cc_sb[0:1, 3:4],
                                start=True, stop=True))
            block.tensor(fbc)
            pen2 = sp.n

            def fbc_v(v):
                v.wait_ge(sp.h, pen2)
                v.wait_ge(sv.h, sv.n)
                sv.inc(v.tensor_copy(rm[cc_idx][:, :], psbc[:, :]))
            block.vector(fbc_v)
            pst.exit(2)
            gd["rm_sv"] = sv.n
        return grp_tok, (emit_cc if cc_idx is not None else None)

    def emit_prologue(src, tile, grp=None):
        sxb = [s.n for s in sx2]
        svb = sv.n
        sab = sa.n
        peb = sp.n
        g_pe, g_sv, g_sa = gd["pe"], gd["sv"], gd["sa"]

        def floads(s):
            s.wait_ge(sp.h, g_pe)
            s.wait_ge(sv.h, g_sv)
            s.wait_ge(sa.h, g_sa)
            s.wait_ge(sst.h, gd["st"])
            for st in range(NSTT):
                if st >= 2:
                    s.wait_ge(sv.h, svb + (st - 1))
                r0 = tile * TILE + st * 128
                sx2[st % 2].inc(s.dma_start(xld[st % 2], src[r0:r0 + 128, :]), 16)
        block.sync(floads)

        def fh(v):
            if grp is not None:
                v.wait_ge(sv.h, grp[0])
                v.wait_ge(sa.h, grp[1])
            for st in range(NSTT):
                gix = tile * NSTT + st
                v.wait_ge(sx2[st % 2].h, sxb[st % 2] + 16 * (st // 2 + 1))
                if st > 0:
                    v.wait_ge(sp.h, peb + st * 4)
                v.wait_ge(sv.h, sv.n)
                sv.inc(v.tensor_scalar(h_bf[:, :], xld[st % 2][:, :],
                                       stM[:, gix:gix + 1], stR[:, gix:gix + 1],
                                       op0=ALU.subtract, op1=ALU.mult))
        block.vector(fh)
        assert sv.n == svb + NSTT

        trp = [pst.enter([128, 1024], BF16), pst.enter([128, 1024], BF16)]

        def ftr(t):
            t.wait_ge(sv.h, g_sv)
            t.wait_ge(sa.h, g_sa)
            for st in range(NSTT):
                t.wait_ge(sv.h, svb + st + 1)
                for r in range(4):
                    gidx = st * 4 + r
                    pb = trp[gidx % 2]
                    if gidx >= 2:
                        t.wait_ge(sa.h, sab + gidx - 1)
                    for k in range(8):
                        hc = r * 8 + k
                        i = t.transpose(pb[:, k * 128:(k + 1) * 128],
                                        h_bf[:, hc * 128:(hc + 1) * 128], ident[:, :])
                    sp.inc(i)
        block.tensor(ftr)
        assert sp.n == peb + 16

        def fev(sc):
            sc.wait_ge(sp.h, g_pe)
            for st in range(NSTT):
                for r in range(4):
                    gidx = st * 4 + r
                    sc.wait_ge(sp.h, peb + gidx + 1)
                    pb3 = trp[gidx % 2].ap().rearrange("p (a b) -> p a b", b=128)
                    dst3 = hT3[:, r * 8:r * 8 + 8, st * 128:(st + 1) * 128]
                    sa.inc(sc.copy(dst3, pb3))
        block.scalar(fev)
        assert sa.n == sab + 16
        pst.exit(2)
        return {"hT_sa": sa.n, "xld_sv": svb + NSTT, "pe_end": peb + 16}

    def emit_attn_tile(src, dst, tile, grp=None):
        st_entry = sst.n
        pro = emit_prologue(src, tile, grp)
        hT_ready = pro["hT_sa"]

        psP = [pst.enter([128, 1024], F32), pst.enter([128, 1024], F32)]
        wrb = [s.n for s in swr]
        peb = sp.n
        svb = sv.n
        sab = sa.n
        DV = 17
        AC = 7

        build = (tile == 0)
        wbb = [s.n for s in swb]
        wrb_pre = [s.n for s in swr]

        def m1g_(g):
            g.wait_ge(sv.h, pro["xld_sv"])
            g.wait_ge(sst.h, st_entry)
            if not build:
                for k in range(4):
                    g.wait_ge(swb[k].h, wbb[k])
            for sc_i in range(SCN):
                a2 = 2 * (sc_i % 2)
                if sc_i >= 2:
                    g.wait_ge(sp.h, peb + 2 * (sc_i - 1))
                    if build:
                        g.wait_ge(swb[a2].h, wbb[a2] + 16 * (sc_i // 2))
                        g.wait_ge(swb[a2 + 1].h, wbb[a2 + 1] + 16 * (sc_i // 2))
                su, sg_ = (d["upw"], d["gw"]) if build else (CW["upw"], CW["gw"])
                swr[a2].inc(g.dma_start(wring[a2], su[sc_i]), 16)
                swr[a2 + 1].inc(g.dma_start(wring[a2 + 1], sg_[sc_i]), 16)
        block.gpsimd(m1g_)

        if build:
            def m1wb(s):
                for sc_i in range(SCN):
                    a2 = 2 * (sc_i % 2)
                    s.wait_ge(swr[a2].h, wrb_pre[a2] + 16 * (sc_i // 2 + 1))
                    swb[a2].inc(s.dma_start(CW["upw"][sc_i], wring[a2]), 16)
                    s.wait_ge(swr[a2 + 1].h, wrb_pre[a2 + 1] + 16 * (sc_i // 2 + 1))
                    swb[a2 + 1].inc(s.dma_start(CW["gw"][sc_i], wring[a2 + 1]), 16)
            block.sync(m1wb)

        def m1t(t):
            t.wait_ge(sa.h, hT_ready)
            for sc_i in range(SCN):
                a2 = 2 * (sc_i % 2)
                t.wait_ge(swr[a2].h, wrb[a2] + 16 * (sc_i // 2 + 1))
                t.wait_ge(swr[a2 + 1].h, wrb[a2 + 1] + 16 * (sc_i // 2 + 1))
                if sc_i >= 2:
                    t.wait_ge(sv.h, svb + DV * (sc_i - 2) + 3)
                pA = psP[sc_i % 2][:, 0:512]
                pB = psP[sc_i % 2][:, 512:1024]
                for hc in range(HC):
                    i = t.matmul(pA, wring[2 * (sc_i % 2)][:, hc * 128:(hc + 1) * 128],
                                 hT[:, hc * TILE:hc * TILE + TILE],
                                 start=(hc == 0), stop=(hc == HC - 1))
                sp.inc(i)
                for hc in range(HC):
                    i = t.matmul(pB, wring[2 * (sc_i % 2) + 1][:, hc * 128:(hc + 1) * 128],
                                 hT[:, hc * TILE:hc * TILE + TILE],
                                 start=(hc == 0), stop=(hc == HC - 1))
                sp.inc(i)
        block.tensor(m1t)
        assert sp.n == peb + 2 * SCN

        def m1a(sc):
            for sc_i in range(SCN):
                base_v = svb + DV * sc_i
                sc.wait_ge(sv.h, base_v + 5)
                sc.wait_ge(sa.h, sa.n)
                sa.inc(sc.activation(mx[4], mx[2], AF.Ln))
                for m in range(1, G + 1):
                    dt_slot = mx[5] if m % 2 == 1 else mx[2]
                    if m <= 2:
                        sc.wait_ge(sv.h, base_v + 9)
                    else:
                        tmp_idx = {1: 10, 2: 11, 3: 12, 4: 13}[m - 2]
                        sc.wait_ge(sv.h, base_v + tmp_idx)
                    sc.wait_ge(sa.h, sa.n)
                    sa.inc(sc.activation(dt_slot, mx[4], AF.Exp, scale=float(m),
                                         bias=ndf_sb[:, m - 1:m]))
        block.scalar(m1a)
        assert sa.n == sab + AC * SCN

        def m1v(v):
            for sc_i in range(SCN):
                base_a = sab + AC * sc_i

                def w():
                    v.wait_ge(sv.h, sv.n)
                pA = psP[sc_i % 2][:, 0:512]
                pB = psP[sc_i % 2][:, 512:1024]
                v.wait_ge(sp.h, peb + 2 * sc_i + 1)
                w()
                sv.inc(v.tensor_copy(tmpT[:, sc_i * TILE:sc_i * TILE + TILE], pA))
                sv.inc(v.tensor_copy(mx[0], pA))
                v.wait_ge(sp.h, peb + 2 * sc_i + 2)
                sv.inc(v.tensor_scalar(mx[1], pB, lp_sb[:, sc_i:sc_i + 1], None,
                                       op0=ALU.subtract))
                if sc_i >= 1:
                    v.wait_ge(sa.h, sab + AC * (sc_i - 1) + 6)
                w()
                sv.inc(v.tensor_scalar(mx[5], mx[1], -1.0, None, op0=ALU.mult))
                if sc_i >= 1:
                    v.wait_ge(sa.h, sab + AC * sc_i)
                w()
                sv.inc(v.tensor_max(mx[2], mx[1], mx[5]))
                w()
                sv.inc(v.tensor_scalar(mx[3], mx[1], 0.0, None, op0=ALU.is_gt))
                w()
                sv.inc(v.tensor_scalar(mx[5], mx[3], 2.0, -1.0, op0=ALU.mult,
                                       op1=ALU.add))
                w()
                sv.inc(v.tensor_scalar(mx[3], mx[2], 2.5, None, op0=ALU.is_le))
                w()
                sv.inc(v.tensor_mul(mx[1], mx[0], mx[5]))
                nops = 9
                for m in range(1, G + 1):
                    dt_slot = mx[5] if m % 2 == 1 else mx[2]
                    src_t = mx[1] if m % 2 == 1 else mx[0]
                    slot = tmpT[:, (m * SCN + sc_i) * TILE:(m * SCN + sc_i) * TILE + TILE]
                    v.wait_ge(sa.h, base_a + 1 + m)
                    w()
                    if m <= 4:
                        sv.inc(v.tensor_mul(slot, dt_slot, src_t)); nops += 1
                    else:
                        sv.inc(v.tensor_mul(mx7, dt_slot, src_t))
                        w()
                        sv.inc(v.tensor_mul(slot, mx7, mx[3]))
                        nops += 2
                assert nops == DV
        block.vector(m1v)
        assert sv.n == svb + DV * SCN
        pst.exit(2)
        m1_pe_end = sp.n
        m1_sv_end = sv.n

        psY = pst.enter([128, 4096], F32)
        NTL = 7 * SCN
        sdb = [s.n for s in sdr]
        peb2 = sp.n
        svb2 = sv.n
        stb2 = sst.n

        dwbb = [s.n for s in sdwb]
        sdb_pre = [s.n for s in sdr]

        def _m2src(j, hq, cached):
            term = j // SCN
            sc_i = j % SCN
            base = (CW if cached else d)
            if term == 0:
                return base["lao"][sc_i, :, hq * 1024:(hq + 1) * 1024]
            return base["fuse"][term - 1, sc_i, :, hq * 1024:(hq + 1) * 1024]

        def m2g_(g):
            g.wait_ge(sp.h, m1_pe_end)
            if not build:
                for k in range(6):
                    g.wait_ge(sdwb[k].h, dwbb[k])
            for hq in range(4):
                for j in range(NTL):
                    gi = hq * NTL + j
                    if gi >= 6:
                        g.wait_ge(sp.h, peb2 + gi - 5)
                        if build:
                            g.wait_ge(sdwb[gi % 6].h, dwbb[gi % 6] + 16 * (gi // 6))
                    sdr[gi % 6].inc(g.dma_start(dring[gi % 6],
                                                _m2src(j, hq, not build)), 16)
        block.gpsimd(m2g_)



        def m2t(t):
            t.wait_ge(sv.h, m1_sv_end)
            for hq in range(4):
                if hq > 0:
                    t.wait_ge(sv.h, svb2 + hq)
                for j in range(NTL):
                    gi = hq * NTL + j
                    t.wait_ge(sdr[gi % 6].h, sdb[gi % 6] + 16 * (gi // 6 + 1))
                    term = j // SCN
                    sc_i = j % SCN
                    for st in range(NSTT):
                        lhs = tmpT[:, (term * SCN + sc_i) * TILE + st * 128:
                                   (term * SCN + sc_i) * TILE + st * 128 + 128]
                        for half in range(2):
                            i = t.matmul(psY[:, st * 1024 + half * 512:
                                             st * 1024 + half * 512 + 512],
                                         lhs, dring[gi % 6][:, half * 512:half * 512 + 512],
                                         start=(j == 0), stop=(j == NTL - 1),
                                         skip_group_check=True)
                    sp.inc(i)
        block.tensor(m2t)
        assert sp.n == peb2 + 4 * NTL

        def m2v(v):
            v.wait_ge(sst.h, st_entry)
            for hq in range(4):
                v.wait_ge(sp.h, peb2 + NTL * (hq + 1))
                if hq >= 1:
                    v.wait_ge(sst.h, stb2 + 64 * hq)
                v.wait_ge(sv.h, sv.n)
                sv.inc(v.tensor_copy(evbw, psY[:, :]))
        block.vector(m2v)
        assert sv.n == svb2 + 4

        def m2s(s):
            for hq in range(4):
                if build:
                    for j in range(NTL):
                        gi = hq * NTL + j
                        s.wait_ge(sdr[gi % 6].h, sdb_pre[gi % 6] + 16 * (gi // 6 + 1))
                        sdwb[gi % 6].inc(s.dma_start(_m2src(j, hq, True),
                                                     dring[gi % 6]), 16)
                s.wait_ge(sv.h, svb2 + hq + 1)
                for st in range(NSTT):
                    sst.inc(s.dma_start(
                        ymix[tile % 2][st * 128:(st + 1) * 128,
                                       hq * 1024:(hq + 1) * 1024], evb[st]), 16)
        block.sync(m2s)
        pst.exit(1)
        m2_pe_end = sp.n
        m2_sv_end = sv.n
        m2_st_end = sst.n

        psP2 = [pst.enter([128, 1024], F32), pst.enter([128, 1024], F32)]
        wrb3 = [s.n for s in swr]
        peb3 = sp.n
        svb3 = sv.n
        sab3 = sa.n

        wbb3 = [s.n for s in swb]
        wrb3_pre = [s.n for s in swr]

        def t1g(g):
            g.wait_ge(sp.h, m2_pe_end)
            g.wait_ge(sst.h, m2_st_end)
            if not build:
                for k in range(4):
                    g.wait_ge(swb[k].h, wbb3[k])
            for fc in range(FCN):
                a2 = 2 * (fc % 2)
                if fc >= 2:
                    g.wait_ge(sv.h, svb3 + (fc - 1))
                    if build:
                        g.wait_ge(swb[a2].h, wbb3[a2] + 16 * (fc // 2))
                        g.wait_ge(swb[a2 + 1].h, wbb3[a2 + 1] + 16 * (fc // 2))
                sgt, sut = (d["gt"], d["ut"]) if build else (CW["gt"], CW["ut"])
                swr[a2].inc(g.dma_start(wring[a2], sgt[fc]), 16)
                swr[a2 + 1].inc(g.dma_start(wring[a2 + 1], sut[fc]), 16)
        block.gpsimd(t1g)

        if build:
            def t1wb(s):
                for fc in range(FCN):
                    a2 = 2 * (fc % 2)
                    s.wait_ge(swr[a2].h, wrb3_pre[a2] + 16 * (fc // 2 + 1))
                    swb[a2].inc(s.dma_start(CW["gt"][fc], wring[a2]), 16)
                    s.wait_ge(swr[a2 + 1].h, wrb3_pre[a2 + 1] + 16 * (fc // 2 + 1))
                    swb[a2 + 1].inc(s.dma_start(CW["ut"][fc], wring[a2 + 1]), 16)
            block.sync(t1wb)

        def t1t(t):
            t.wait_ge(sv.h, m2_sv_end)
            for fc in range(FCN):
                a2 = 2 * (fc % 2)
                t.wait_ge(swr[a2].h, wrb3[a2] + 16 * (fc // 2 + 1))
                t.wait_ge(swr[a2 + 1].h, wrb3[a2 + 1] + 16 * (fc // 2 + 1))
                if fc >= 2:
                    t.wait_ge(sv.h, svb3 + (fc - 1))
                pA = psP2[fc % 2][:, 0:512]
                pB = psP2[fc % 2][:, 512:1024]
                for hc in range(HC):
                    i = t.matmul(pA, wring[2 * (fc % 2)][:, hc * 128:(hc + 1) * 128],
                                 hT[:, hc * TILE:hc * TILE + TILE],
                                 start=(hc == 0), stop=(hc == HC - 1))
                sp.inc(i)
                for hc in range(HC):
                    i = t.matmul(pB, wring[2 * (fc % 2) + 1][:, hc * 128:(hc + 1) * 128],
                                 hT[:, hc * TILE:hc * TILE + TILE],
                                 start=(hc == 0), stop=(hc == HC - 1))
                sp.inc(i)
        block.tensor(t1t)
        assert sp.n == peb3 + 2 * FCN

        def t1a(sc):
            for fc in range(FCN):
                sc.wait_ge(sp.h, peb3 + 2 * fc + 1)
                if fc >= 2:
                    sc.wait_ge(sv.h, svb3 + (fc - 1))
                sc.wait_ge(sa.h, sa.n)
                sa.inc(sc.activation(mx[fc % 2], psP2[fc % 2][:, 0:512], AF.Silu))
        block.scalar(t1a)
        assert sa.n == sab3 + FCN

        def t1v(v):
            for fc in range(FCN):
                v.wait_ge(sa.h, sab3 + fc + 1)
                v.wait_ge(sp.h, peb3 + 2 * fc + 2)
                v.wait_ge(sv.h, sv.n)
                sv.inc(v.tensor_mul(aT[:, fc * TILE:(fc + 1) * TILE], mx[fc % 2],
                                    psP2[fc % 2][:, 512:1024]))
        block.vector(t1v)
        assert sv.n == svb3 + FCN
        pst.exit(2)
        t1_pe_end = sp.n
        t1_sv_end = sv.n

        psY2 = pst.enter([128, 4096], F32)
        sdb4 = [s.n for s in sdr]
        peb4 = sp.n
        svb4 = sv.n
        stb4 = sst.n
        sqb4 = [s.n for s in sqy]

        dwbb4 = [s.n for s in sdwb]
        sdb4_pre = [s.n for s in sdr]

        def t2g(g):
            g.wait_ge(sp.h, t1_pe_end)
            if not build:
                for k in range(6):
                    g.wait_ge(sdwb[k].h, dwbb4[k])
            for hq in range(4):
                for fc in range(FCN):
                    gi = hq * FCN + fc
                    if gi >= 6:
                        g.wait_ge(sp.h, peb4 + gi - 5)
                        if build:
                            g.wait_ge(sdwb[gi % 6].h, dwbb4[gi % 6] + 16 * (gi // 6))
                    sdt = d["dtw"] if build else CW["dtw"]
                    sdr[gi % 6].inc(g.dma_start(
                        dring[gi % 6], sdt[fc, :, hq * 1024:(hq + 1) * 1024]), 16)
        block.gpsimd(t2g)



        def t2s(s):
            s.wait_ge(sv.h, t1_sv_end)
            s.wait_ge(sst.h, m2_st_end)
            for hq in range(4):
                if hq >= 1:
                    s.wait_ge(sst.h, stb4 + 64 * hq)
                if hq >= 1:
                    s.wait_ge(sv.h, svb4 + 2 * hq)
                for st in range(NSTT):
                    r0 = tile * TILE + st * 128
                    sqy[st].inc(s.dma_start(xq[st], src[r0:r0 + 128,
                                                        hq * 1024:(hq + 1) * 1024]), 16)
                    sqy[st].inc(s.dma_start(yq[st],
                                            ymix[tile % 2][st * 128:(st + 1) * 128,
                                                           hq * 1024:(hq + 1) * 1024]), 16)
                if build:
                    for fc in range(FCN):
                        gi = hq * FCN + fc
                        s.wait_ge(sdr[gi % 6].h, sdb4_pre[gi % 6] + 16 * (gi // 6 + 1))
                        sdwb[gi % 6].inc(s.dma_start(
                            CW["dtw"][fc, :, hq * 1024:(hq + 1) * 1024],
                            dring[gi % 6]), 16)
                s.wait_ge(sv.h, svb4 + 2 * (hq + 1))
                for st in range(NSTT):
                    r0 = tile * TILE + st * 128
                    sst.inc(s.dma_start(dst[r0:r0 + 128, hq * 1024:(hq + 1) * 1024],
                                        yq[st]), 16)
        block.sync(t2s)

        def t2t(t):
            t.wait_ge(sv.h, t1_sv_end)
            for hq in range(4):
                if hq > 0:
                    t.wait_ge(sv.h, svb4 + 2 * (hq - 1) + 1)
                for fc in range(FCN):
                    gi = hq * FCN + fc
                    t.wait_ge(sdr[gi % 6].h, sdb4[gi % 6] + 16 * (gi // 6 + 1))
                    for st in range(NSTT):
                        lhs = aT[:, fc * TILE + st * 128:fc * TILE + st * 128 + 128]
                        for half in range(2):
                            i = t.matmul(psY2[:, st * 1024 + half * 512:
                                              st * 1024 + half * 512 + 512],
                                         lhs, dring[gi % 6][:, half * 512:half * 512 + 512],
                                         start=(fc == 0), stop=(fc == FCN - 1),
                                         skip_group_check=True)
                    sp.inc(i)
        block.tensor(t2t)
        assert sp.n == peb4 + 4 * FCN

        def t2v(v):
            v.wait_ge(sst.h, m2_st_end)
            for hq in range(4):
                v.wait_ge(sp.h, peb4 + FCN * (hq + 1))
                for st in range(NSTT):
                    v.wait_ge(sqy[st].h, sqb4[st] + 32 * (hq + 1))
                v.wait_ge(sv.h, sv.n)
                sv.inc(v.tensor_add(evbw, psY2[:, :], yqw))
                v.wait_ge(sv.h, sv.n)
                sv.inc(v.tensor_add(yqw, evbw, xqw))
        block.vector(t2v)
        assert sv.n == svb4 + 8
        pst.exit(1)
        gd.update(pe=sp.n, sv=sv.n, sa=sa.n, st=sst.n)

    def emit_ffn_tile(src, dst, tile, rm_idx, grp=None, cc_emit=None):
        st_entry = sst.n
        build = (rm_idx == 0 and tile == 0)
        pro = emit_prologue(src, tile, grp)
        hT_ready = pro["hT_sa"]

        psQ_t = pst.enter([128, TILE], F32)
        psQ = psQ_t[0:BOT, :]
        psRQ_t = pst.enter([128, TILE], F32)
        psRQ = psRQ_t[0:RD, :]

        def f1t(t):
            t.wait_ge(sa.h, hT_ready)
            t.wait_ge(swq.h, 16)
            for hc in range(HC):
                i = t.matmul(psQ[:, :], wq_sb[:, hc * 64:(hc + 1) * 64],
                             hT[:, hc * TILE:hc * TILE + TILE],
                             start=(hc == 0), stop=(hc == HC - 1))
            sp.inc(i)
        block.tensor(f1t)
        pe_q = sp.n

        def f1v(v):
            v.wait_ge(sp.h, pe_q)
            v.wait_ge(sv.h, sv.n)
            sv.inc(v.tensor_copy(q_sb[0:BOT, :], psQ[:, :]))
        block.vector(f1v)
        sv_q = sv.n

        def f1t2(t):
            t.wait_ge(sv.h, sv_q)
            sp.inc(t.matmul(psRQ[:, :], wrt_sb[:, :], q_sb[0:BOT, :],
                            start=True, stop=True))
        block.tensor(f1t2)
        pe_rq = sp.n

        def f1v2(v):
            v.wait_ge(sp.h, pe_rq)
            v.wait_ge(sv.h, sv.n)
            sv.inc(v.tensor_copy(rq_sb[0:RD, :], psRQ[:, :]))
        block.vector(f1v2)
        sv_rq = sv.n
        pst.exit(2)

        psR = pst.enter([128, NSLOT], F32)
        psS = pst.enter([128, NSLOT], F32)
        psM = [pst.enter([128, TILE], F32), pst.enter([128, TILE], F32)]
        trpF = [pst.enter([128, 1024], BF16), pst.enter([128, 1024], BF16)]

        peb_rs = sp.n
        svb_sm = sv.n
        sab_sm = sa.n
        SMV = 8

        def f2t(t):
            t.wait_ge(sv.h, sv_rq)
            for st in range(NSTT):
                if st > 0:
                    t.wait_ge(sv.h, svb_sm + SMV * (st - 1) + 4)
                for half in range(2):
                    t.matmul(psR[:, half * 512:(half + 1) * 512],
                             rq_sb[0:RD, st * 128:(st + 1) * 128],
                             rkt_sb[:, half * 512:(half + 1) * 512],
                             start=True, stop=True)
                for half in range(2):
                    i = t.matmul(psS[:, half * 512:(half + 1) * 512],
                                 q_sb[0:BOT, st * 128:(st + 1) * 128],
                                 akt_sb[:, half * 512:(half + 1) * 512],
                                 start=True, stop=True)
                sp.inc(i)
        block.tensor(f2t)
        assert sp.n == peb_rs + NSTT

        def f2a(sc):
            for st in range(NSTT):
                sc.wait_ge(sv.h, svb_sm + SMV * st + 6)
                sc.wait_ge(sa.h, sa.n)
                sa.inc(sc.activation(ex_sb, Sm_sb, AF.Exp, bias=sm1[:, 0:1],
                                     accum_out=sm1[:, 1:2]))
        block.scalar(f2a)
        assert sa.n == sab_sm + NSTT

        def f2v(v):
            for st in range(NSTT):
                def w():
                    v.wait_ge(sv.h, sv.n)
                v.wait_ge(sp.h, peb_rs + st + 1)
                w()
                sv.inc(v.scalar_tensor_tensor(R_sb, psR[:, :], RSQ48, logrel_sb[:, :],
                                              op0=ALU.mult, op1=ALU.add))
                w()
                sv.inc(v.max(out=t8_sb[:, :], in_=R_sb))
                w()
                sv.inc(v.tensor_scalar(mask_sb, R_sb, t8_sb[:, 7:8], None,
                                       op0=ALU.is_ge))
                sv.inc(v.scalar_tensor_tensor(Sp_sb, psS[:, :], RSQ64, logrel_sb[:, :],
                                              op0=ALU.mult, op1=ALU.add))
                if st >= 1:
                    v.wait_ge(sa.h, sab_sm + st)
                w()
                sv.inc(v.select(Sm_sb, mask_sb, Sp_sb, negbig_sb[:, :]))
                w()
                sv.inc(v.reduce_max(sm1[:, 0:1], Sm_sb, axis=AX.X, negate=True))
                v.wait_ge(sa.h, sab_sm + st + 1)
                w()
                sv.inc(v.reciprocal(sm1[:, 2:3], sm1[:, 1:2]))
                w()
                sv.inc(v.tensor_scalar(w_stage[:, st * 1024:(st + 1) * 1024], ex_sb,
                                       sm1[:, 2:3], None, op0=ALU.mult))
        block.vector(f2v)
        assert sv.n == svb_sm + SMV * NSTT
        sm_sv_end = sv.n

        sfb = [s.n for s in sfr]
        peb_m = sp.n
        svb_g = sv.n
        sab_g = sa.n
        GAV = 2
        GVV = 6

        fwbb = [s.n for s in sfwb]
        sfb_pre = [s.n for s in sfr]

        def f3g(g):
            g.wait_ge(sv.h, pro["xld_sv"])
            g.wait_ge(sp.h, pro["pe_end"])
            g.wait_ge(sst.h, st_entry)
            if not build:
                for k in range(3):
                    g.wait_ge(sfwb[k].h, fwbb[k])
            for uc in range(UCN):
                if uc >= 3:
                    g.wait_ge(sp.h, peb_m + uc - 2)
                    if build:
                        g.wait_ge(sfwb[uc % 3].h, fwbb[uc % 3] + 16 * (uc // 3))
                swu = d["wu1"] if build else CW["wu1"]
                sfr[uc % 3].inc(g.dma_start(ffring[uc % 3], swu[uc]), 16)
        block.gpsimd(f3g)

        if build:
            def f3wb(s):
                for uc in range(UCN):
                    s.wait_ge(sfr[uc % 3].h, sfb_pre[uc % 3] + 16 * (uc // 3 + 1))
                    sfwb[uc % 3].inc(s.dma_start(CW["wu1"][uc], ffring[uc % 3]), 16)
            block.sync(f3wb)

        def f3t(t):
            t.wait_ge(sa.h, hT_ready)
            for uc in range(UCN):
                t.wait_ge(sfr[uc % 3].h, sfb[uc % 3] + 16 * (uc // 3 + 1))
                if uc >= 2:
                    t.wait_ge(sv.h, svb_g + GVV * (uc - 2) + 1)
                pb = psM[uc % 2]
                for hc in range(HC):
                    i = t.matmul(pb[:, :], ffring[uc % 3][:, hc * 128:(hc + 1) * 128],
                                 hT[:, hc * TILE:hc * TILE + TILE],
                                 start=(hc == 0), stop=(hc == HC - 1))
                sp.inc(i)
        block.tensor(f3t)
        assert sp.n == peb_m + UCN

        def f3a(sc):
            sc.wait_ge(sst.h, st_entry)
            for uc in range(UCN):
                sc.wait_ge(sv.h, svb_g + GVV * uc + 1)
                sc.wait_ge(sa.h, sa.n)
                sa.inc(sc.activation(gt1, gz, AF.Square))
                sc.wait_ge(sv.h, svb_g + GVV * uc + 3)
                sc.wait_ge(sa.h, sa.n)
                sa.inc(sc.activation(gt2, gt1, AF.Tanh, scale=GELU_C))
        block.scalar(f3a)
        assert sa.n == sab_g + GAV * UCN

        def f3v(v):
            v.wait_ge(sst.h, st_entry)
            for uc in range(UCN):
                def w():
                    v.wait_ge(sv.h, sv.n)
                v.wait_ge(sp.h, peb_m + uc + 1)
                if uc >= 1:
                    v.wait_ge(sa.h, sab_g + GAV * (uc - 1) + 1)
                w()
                sv.inc(v.tensor_scalar(gz, psM[uc % 2][:, :], bu1_sb[:, uc:uc + 1],
                                       None, op0=ALU.add))
                v.wait_ge(sa.h, sab_g + GAV * uc + 1)
                w()
                sv.inc(v.tensor_mul(gt2, gt1, gz))
                w()
                sv.inc(v.scalar_tensor_tensor(gt1, gt2, GELU_A, gz, op0=ALU.mult,
                                              op1=ALU.add))
                v.wait_ge(sa.h, sab_g + GAV * uc + 2)
                w()
                sv.inc(v.tensor_scalar(gt1, gt2, 1.0, None, op0=ALU.add))
                w()
                sv.inc(v.tensor_mul(gt2, gt1, gz))
                w()
                sv.inc(v.tensor_scalar(m1g[:, uc * TILE:(uc + 1) * TILE], gt2, 0.5,
                                       None, op0=ALU.mult))
        block.vector(f3v)
        assert sv.n == svb_g + GVV * UCN
        gelu_sv_end = sv.n

        peb_tr = sp.n
        sab_tr = sa.n

        def f2bt(t):
            for st in range(NSTT):
                t.wait_ge(sv.h, svb_sm + SMV * (st + 1))
                pb = trpF[st % 2]
                if st >= 2:
                    t.wait_ge(sa.h, sab_tr + st - 1)
                for k in range(8):
                    i = t.transpose(pb[:, k * 128:(k + 1) * 128],
                                    w_stage[:, st * 1024 + k * 128:st * 1024 + (k + 1) * 128],
                                    ident[:, :])
                sp.inc(i)
        block.tensor(f2bt)
        assert sp.n == peb_tr + NSTT

        def f2ba(sc):
            for st in range(NSTT):
                sc.wait_ge(sp.h, peb_tr + st + 1)
                pb3 = trpF[st % 2].ap().rearrange("p (a b) -> p a b", b=128)
                sa.inc(sc.copy(wT3[:, :, st * 128:(st + 1) * 128], pb3))
        block.scalar(f2ba)
        assert sa.n == sab_tr + NSTT
        wT_ready = sa.n
        pst.exit(6)

        psL = pst.enter([128, 8], F32)
        peb_l = sp.n
        svb_l = sv.n

        def f4t(t):
            t.wait_ge(sv.h, gelu_sv_end)
            t.wait_ge(sa.h, wT_ready)
            for st in range(NSTT):
                if st > 0:
                    t.wait_ge(sv.h, svb_l + st)
                for uc in range(UCN):
                    i = t.matmul(psL[:, 0:1],
                                 m1g[:, uc * TILE + st * 128:uc * TILE + st * 128 + 128],
                                 wu2_sb[:, uc:uc + 1],
                                 start=(uc == 0), stop=(uc == UCN - 1))
                sp.inc(i)
        block.tensor(f4t)
        assert sp.n == peb_l + NSTT

        def f4v(v):
            for st in range(NSTT):
                v.wait_ge(sp.h, peb_l + st + 1)
                v.wait_ge(sv.h, sv.n)
                sv.inc(v.tensor_copy(learned_sb[:, st:st + 1], psL[:, 0:1]))
        block.vector(f4v)
        assert sv.n == svb_l + NSTT
        learned_sv = sv.n
        pst.exit(1)


        psA1 = pst.enter([128, 4096], F32)
        sdb5 = [s.n for s in sdr]
        peb5 = sp.n
        sab5 = sa.n

        dwbb5 = [s.n for s in sdwb]
        sdb5_pre = [s.n for s in sdr]

        def f5g(g):
            g.wait_ge(sv.h, pro["xld_sv"])
            if not build:
                for k in range(4):
                    g.wait_ge(sdwb[k].h, dwbb5[k])
            for rnd in range(4):
                for sc_i in range(SCN):
                    gi = rnd * SCN + sc_i
                    if gi >= 4:
                        g.wait_ge(sp.h, peb5 + gi - 3)
                        if build:
                            g.wait_ge(sdwb[gi % 4].h, dwbb5[gi % 4] + 16 * (gi // 4))
                    sav = d["auxv"] if build else CW["auxv"]
                    sdr[gi % 4].inc(g.dma_start(
                        avring[gi % 4], sav[sc_i, :, rnd * 1024:(rnd + 1) * 1024]), 16)
        block.gpsimd(f5g)

        if build:
            def f5wb(s):
                for rnd in range(4):
                    for sc_i in range(SCN):
                        gi = rnd * SCN + sc_i
                        s.wait_ge(sdr[gi % 4].h, sdb5_pre[gi % 4] + 16 * (gi // 4 + 1))
                        sdwb[gi % 4].inc(s.dma_start(
                            CW["auxv"][sc_i, :, rnd * 1024:(rnd + 1) * 1024],
                            avring[gi % 4]), 16)
            block.sync(f5wb)

        def f5t(t):
            t.wait_ge(sa.h, wT_ready)
            t.wait_ge(sv.h, learned_sv)
            for rnd in range(4):
                if rnd > 0:
                    t.wait_ge(sa.h, sab5 + rnd)
                for sc_i in range(SCN):
                    gi = rnd * SCN + sc_i
                    t.wait_ge(sdr[gi % 4].h, sdb5[gi % 4] + 16 * (gi // 4 + 1))
                    for k in range(8):
                        i = t.matmul(psA1[:, k * 512:(k + 1) * 512],
                                     avring[gi % 4][:, k * 128:(k + 1) * 128],
                                     wT[:, sc_i * TILE:(sc_i + 1) * TILE],
                                     start=(sc_i == 0), stop=(sc_i == SCN - 1),
                                     skip_group_check=True)
                    sp.inc(i)
        block.tensor(f5t)
        assert sp.n == peb5 + 32

        def f5a(sc):
            for rnd in range(4):
                sc.wait_ge(sp.h, peb5 + 8 * (rnd + 1))
                sc.wait_ge(sa.h, sa.n)
                sa.inc(sc.copy(aux1T[:, rnd * 8 * TILE:(rnd + 1) * 8 * TILE],
                               psA1[:, :]))
        block.scalar(f5a)
        assert sa.n == sab5 + 4
        aux1_sa = sa.n
        pst.exit(1)

        psV = [pst.enter([128, TILE], F32), pst.enter([128, TILE], F32)]
        wrb6 = [s.n for s in swr]
        peb6 = sp.n
        svb6 = sv.n

        wbb6 = [s.n for s in swb]
        wrb6_pre = [s.n for s in swr]

        def f6g(g):
            g.wait_ge(sp.h, peb5 + 32)
            if not build:
                for k in range(2):
                    g.wait_ge(swb[k].h, wbb6[k])
            for vc in range(2):
                svd = d["wvd"] if build else CW["wvd"]
                swr[vc].inc(g.dma_start(wvd_sb[vc], svd[vc]), 16)
        block.gpsimd(f6g)

        if build:
            def f6wb(s):
                for vc in range(2):
                    s.wait_ge(swr[vc].h, wrb6_pre[vc] + 16)
                    swb[vc].inc(s.dma_start(CW["wvd"][vc], wvd_sb[vc]), 16)
            block.sync(f6wb)

        def f6t(t):
            t.wait_ge(sa.h, aux1_sa)
            for vc in range(2):
                t.wait_ge(swr[vc].h, wrb6[vc] + 16)
                for hc in range(HC):
                    i = t.matmul(psV[vc][:, :], wvd_sb[vc][:, hc * 128:(hc + 1) * 128],
                                 aux1T[:, hc * TILE:hc * TILE + TILE],
                                 start=(hc == 0), stop=(hc == HC - 1))
                sp.inc(i)
        block.tensor(f6t)
        assert sp.n == peb6 + 2

        def f6v(v):
            for vc in range(2):
                v.wait_ge(sp.h, peb6 + vc + 1)
                v.wait_ge(sv.h, sv.n)
                sv.inc(v.tensor_copy(v1T[:, vc * TILE:(vc + 1) * TILE], psV[vc][:, :]))
        block.vector(f6v)
        assert sv.n == svb6 + 2
        v1_sv = sv.n
        pst.exit(2)

        wbb6b = [swb[2].n, swb[3].n]
        wrb6b_pre = [swr[2].n, swr[3].n]

        def f6g2(g):
            g.wait_ge(sp.h, peb6 + 2)
            if not build:
                for k in range(2):
                    g.wait_ge(swb[2 + k].h, wbb6b[k])
            for vc in range(2):
                svu = d["wvu"] if build else CW["wvu"]
                swr[2 + vc].inc(g.dma_start(wvu7[vc], svu[vc]), 16)
        block.gpsimd(f6g2)
        wvu_tok = [swr[2].n, swr[3].n]

        if cc_emit is not None:
            cc_emit(learned_sv)
        g_idx = tile * NSTT
        sab_gt = sa.n

        rm_tok = gd.get("rm_sv", 0)

        def fgates_a(sc):
            sc.wait_ge(sv.h, learned_sv)
            sc.wait_ge(sa.h, sa.n)
            sa.inc(sc.activation(sm1[:, 0:4], learned_sb[:, 0:4], AF.Sigmoid,
                                 bias=scal_sb[:, 2:3]))
            sc.wait_ge(sv.h, svb_gt_pred[0])
            sc.wait_ge(sa.h, sa.n)
            sa.inc(sc.activation(sm2[:, 0:4], sm2[:, 4:8], AF.Sigmoid))
        svb_gt_pred = [None]

        def fgates_v(v):
            def w():
                v.wait_ge(sv.h, sv.n)
            v.wait_ge(sv.h, rm_tok)
            v.wait_ge(sa.h, sab_gt + 1)
            w()
            sv.inc(v.tensor_scalar(sm1[:, 4:8], stL[:, g_idx:g_idx + 4],
                                   rm[rm_idx][:, 0:1], None, op0=ALU.mult))
            sv.inc(v.tensor_scalar(sm2[:, 0:4], sm1[:, 0:4], 2.5, None,
                                   op0=ALU.mult))
            w()
            sv.inc(v.scalar_tensor_tensor(sm1[:, 0:4], sm1[:, 4:8], 0.5,
                                          sm2[:, 0:4], op0=ALU.mult,
                                          op1=ALU.add))
            w()
            sv.inc(v.tensor_scalar(sm1[:, 4:8], sm1[:, 0:4], 0.0, 5.0,
                                   op0=ALU.max, op1=ALU.min))
            w()
            sv.inc(v.tensor_scalar(sm1[:, 0:4], sm1[:, 4:8], 0.5, 1.0 / 1.5,
                                   op0=ALU.subtract, op1=ALU.mult))
            w()
            sv.inc(v.tensor_scalar(sm1[:, 4:8], sm1[:, 0:4], 0.0, 1.0,
                                   op0=ALU.max, op1=ALU.min))
            w()
            sv.inc(v.tensor_scalar(sm2[:, 4:8], sm1[:, 4:8], scal_sb[:, 0:1],
                                   scal_sb[:, 1:2], op0=ALU.mult, op1=ALU.add))
            svb_gt_pred[0] = sv.n
            v.wait_ge(sa.h, sab_gt + 2)
            w()
            sv.inc(v.tensor_scalar(sm3[:, 4:8], sm2[:, 0:4], 0.05, None,
                                   op0=ALU.is_ge))
            w()
            sv.inc(v.scalar_tensor_tensor(sm3[:, 0:4], sm2[:, 0:4], 0.5,
                                          sm3[:, 4:8], op0=ALU.mult,
                                          op1=ALU.mult))
        block.vector(fgates_v)
        block.scalar(fgates_a)
        gates_sv = sv.n

        if build:
            def f6wb2(s):
                for vc in range(2):
                    s.wait_ge(swr[2 + vc].h, wrb6b_pre[vc] + 16)
                    swb[2 + vc].inc(s.dma_start(CW["wvu"][vc], wvu7[vc]), 16)
            block.sync(f6wb2)

        psYh = [pst.enter([128, 2048], F32), pst.enter([128, 2048], F32)]
        peb7 = sp.n
        svb7 = sv.n
        sqb7 = [s.n for s in sqy]
        stb7 = sst.n
        FEV = 4

        def f7s(s):
            s.wait_ge(sp.h, peb7)
            for st in range(NSTT):
                if st >= 1:
                    s.wait_ge(sv.h, svb7 + FEV * st)
                    s.wait_ge(sst.h, stb7 + 16 * 4 * st)
                r0 = tile * TILE + st * 128
                for hq in range(4):
                    sqy[hq].inc(s.dma_start(xq7[hq], src[r0:r0 + 128,
                                                         hq * 1024:(hq + 1) * 1024]), 16)
                for hq in range(4):
                    s.wait_ge(sv.h, svb7 + FEV * st + 2 * (hq // 2 + 1))
                    sst.inc(s.dma_start(dst[r0:r0 + 128, hq * 1024:(hq + 1) * 1024],
                                        xq7[hq]), 16)
        block.sync(f7s)

        def f7t(t):
            t.wait_ge(sv.h, v1_sv)
            t.wait_ge(swr[2].h, wvu_tok[0])
            t.wait_ge(swr[3].h, wvu_tok[1])
            for st in range(NSTT):
                for half in range(2):
                    if st > 0:
                        t.wait_ge(sv.h, svb7 + FEV * (st - 1) + 2 * (half + 1))
                    for vc in range(2):
                        for k in range(4):
                            kk = half * 4 + k
                            i = t.matmul(psYh[half][:, k * 512:(k + 1) * 512],
                                         v1T[:, vc * TILE + st * 128:vc * TILE + st * 128 + 128],
                                         wvu7[vc][:, kk * 512:(kk + 1) * 512],
                                         start=(vc == 0), stop=(vc == 1),
                                         skip_group_check=True)
                    sp.inc(i)
        block.tensor(f7t)
        assert sp.n == peb7 + 2 * NSTT


        def f7v(v):
            v.wait_ge(sst.h, st_entry)
            v.wait_ge(sv.h, gates_sv)
            for st in range(NSTT):
                def w():
                    v.wait_ge(sv.h, sv.n)
                gst = g_idx + st
                for half in range(2):
                    v.wait_ge(sp.h, peb7 + 2 * st + half + 1)
                    v.wait_ge(sqy[2 * half].h, sqb7[2 * half] + 16 * (st + 1))
                    v.wait_ge(sqy[2 * half + 1].h, sqb7[2 * half + 1] + 16 * (st + 1))
                    w()
                    sv.inc(v.tensor_scalar(evb7w[half], xq7w[half],
                                           stA2[:, gst:gst + 1], stB2[:, gst:gst + 1],
                                           op0=ALU.mult, op1=ALU.add))
                    w()
                    sv.inc(v.scalar_tensor_tensor(
                        xq7w[half], psYh[half][:, :],
                        sm3[:, st:st + 1], evb7w[half], op0=ALU.mult,
                        op1=ALU.add))
        block.vector(f7v)
        assert sv.n == svb7 + FEV * NSTT
        pst.exit(2)
        gd.update(pe=sp.n, sv=sv.n, sa=sa.n, st=sst.n)

    gt0, cc0 = emit_stats(d["x0"], 0)
    last = "x0"
    if upto >= 1:
        for tile in range(NT):
            emit_ffn_tile(d["x0"], d["x1"], tile, 0, grp=gt0[tile],
                          cc_emit=(cc0 if tile == 0 else None))
        last = "x1"
    if upto >= 2:
        gt1_, _ = emit_stats(d["x1"], None)
        for tile in range(NT):
            emit_attn_tile(d["x1"], d["x2"], tile, grp=gt1_[tile])
        last = "x2"
    if upto >= 3:
        gt2_, cc2 = emit_stats(d["x2"], 1)
        for tile in range(NT):
            emit_ffn_tile(d["x2"], d["y"], tile, 1, grp=gt2_[tile],
                          cc_emit=(cc2 if tile == 0 else None))
        last = "y"

    if last != "y":
        stn0 = sst.n
        g_pe, g_sv, g_sa = gd["pe"], gd["sv"], gd["sa"]

        def fcp(s):
            s.wait_ge(sp.h, g_pe)
            s.wait_ge(sv.h, g_sv)
            s.wait_ge(sa.h, g_sa)
            s.wait_ge(sst.h, stn0)
            for st in range(NSTG):
                if st >= 2:
                    s.wait_ge(sst.h, stn0 + 16 * (st - 1))
                sx2[st % 2].inc(s.dma_start(xld[st % 2],
                                            d[last][st * 128:(st + 1) * 128, :]), 16)
                s.wait_ge(sx2[st % 2].h, sx2[st % 2].n)
                sst.inc(s.dma_start(d["y"][st * 128:(st + 1) * 128, :], xld[st % 2]), 16)
        block.sync(fcp)

    fins = [(sst, sst.n)] + [(s, s.n) for s in
                              ([si, swq, scx] + sx2 + swr + sdr + sfr + sqy
                               + swb + sdwb + sfwb)]

    def fend(s):
        for sem, n in fins:
            s.wait_ge(sem.h, n)
    block.sync(fend)

    ctx.close()
    return nc


def prep_inputs(inp):
    f32 = np.float32

    def A(x):
        return np.ascontiguousarray(np.asarray(x, dtype=f32))

    w = {}
    w["gt"] = _blk_lhsT(A(inp["L0_attn_ffn_gate_w"]))
    w["ut"] = _blk_lhsT(A(inp["L0_attn_ffn_up_w"]))
    w["dtw"] = A(np.asarray(inp["L0_attn_ffn_down_w"], f32).T).reshape(FCN, 128, H)
    w["upw"] = _blk_lhsT(A(inp["L0_attn_up_w"]))
    w["gw"] = _blk_lhsT(A(inp["L0_attn_gate_w"]))
    w["lao"] = A(np.asarray(inp["L0_attn_local_approx_output"], f32).T).reshape(SCN, 128, H)
    w["fuse"] = A(np.transpose(np.asarray(inp["L0_attn_fuse_weight"], f32),
                               (2, 1, 0))).reshape(G, SCN, 128, H)
    w["wq"] = _blk_lhsT(A(inp["L0_ffn_Wq"]), oc=64).reshape(128, HC * 64)
    w["wu1"] = _blk_lhsT(A(inp["L0_ffn_Wu1"]))
    w["wvd"] = _blk_lhsT(A(inp["L0_ffn_Wvd"]))
    w["wvu"] = A(np.asarray(inp["L0_ffn_Wvu"], f32).T).reshape(2, 128, H)
    w["auxv"] = A(inp["L0_ffn_aux_values"]).reshape(SCN, 128, H)
    w["akt"] = A(np.asarray(inp["L0_ffn_aux_keys"], f32).T)
    w["wrt"] = A(np.asarray(inp["L0_ffn_Wr"], f32).T)
    w["lp"] = A(np.asarray(inp["L0_attn_local_point"], f32).reshape(SCN, 128).T)
    w["bu1"] = A(np.asarray(inp["L0_ffn_bu1"], f32).reshape(UCN, 128).T)
    w["wu2"] = A(np.asarray(inp["L0_ffn_Wu2"], f32).reshape(UCN, 128).T)
    w["df"] = A(np.broadcast_to(np.asarray(inp["L0_attn_discount_factor"], f32),
                                (128, G)))
    w["scal"] = A(np.broadcast_to(np.stack([
        np.float32(inp["L0_ffn_gate_w1"]), np.float32(inp["L0_ffn_gate_bias"]),
        np.asarray(inp["L0_ffn_bu2"], f32).reshape(())]).astype(f32), (128, 3)))
    w["logrel"] = A(np.broadcast_to(np.asarray(inp["L0_ffn_log_rel"], f32),
                                    (128, NSLOT)))
    x = np.asarray(inp["L0_attn_hidden_states"], f32).reshape(TCORE * NCORES, H)
    return w, x


def kernel(upto=3, debug_xs=False, **inputs):
    w, x = prep_inputs(inputs)
    nc = build_program(upto=upto, debug_xs=debug_xs)
    in_maps = []
    for c in range(NCORES):
        m = dict(w)
        m["x0"] = np.ascontiguousarray(x[c * TCORE:(c + 1) * TCORE])
        in_maps.append(m)
    res = run_bass_kernel_spmd(nc, in_maps, core_ids=list(range(NCORES)))
    out = np.concatenate([res.results[c]["y"] for c in range(NCORES)], axis=0)
    extras = None
    if debug_xs:
        extras = {k: np.concatenate([res.results[c][k] for c in range(NCORES)], axis=0)
                  for k in ("x1", "x2")}
    out = out.reshape(2, TCORE * NCORES // 2, H)
    return (out, extras) if debug_xs else out


# version 12 (final)
# speedup vs baseline: 1.0176x; 1.0053x over previous
import numpy as np
from contextlib import ExitStack
import concourse.bass as bass
import concourse.mybir as mybir
from concourse.bass_utils import run_bass_kernel_spmd

F32 = mybir.dt.float32
BF16 = mybir.dt.bfloat16
U8 = mybir.dt.uint8
AF = mybir.ActivationFunctionType
ALU = mybir.AluOpType
AX = mybir.AxisListType

H = 4096; HC = 32
S = 1024; SCN = 8
FF = 9984; FCN = 78
G = 6
BOT = 64; RD = 48; VB = 256; UCN = 8
NSLOT = 1024
TCORE = 2048; TILE = 512; NT = 4; NSTT = 4
NSTG = 16
RSQ48 = 0.14433756729740643
RSQ64 = 0.125
NEGBIG = -30000.0
GELU_C = 0.7978845608028654
GELU_A = 0.044715
KB = 1024
NCORES = 8


def _blk_lhsT(w, oc=128):
    O, I = w.shape
    nO, nI = O // oc, I // 128
    b = w.reshape(nO, oc, nI, 128).transpose(0, 3, 2, 1)
    return np.ascontiguousarray(b).reshape(nO, 128, nI * oc)


class Sem:
    __slots__ = ("h", "n")

    def __init__(self, ctx, nc, name):
        self.h = ctx.enter_context(nc.semaphore(name))
        self.n = 0

    def inc(self, instr, k=1):
        instr.then_inc(self.h, k)
        self.n += k
        return self.n


class PsumStack:
    def __init__(self, nc):
        self.nc = nc
        self.stack = []
        self.k = 0

    def enter(self, shape, dt):
        self.k += 1
        cm = self.nc.psum_tensor(f"ps{self.k}", shape, dt)
        t = cm.__enter__()
        self.stack.append(cm)
        return t

    def exit(self, k=1):
        for _ in range(k):
            self.stack.pop().__exit__(None, None, None)


def build_program(upto=3, debug_xs=False):
    ctx = ExitStack()
    nc = bass.Bass("TRN2", target_bir_lowering=False)
    d = {}

    def din(name, shape, dt=F32):
        d[name] = nc.dram_tensor(name, list(shape), dt, kind="ExternalInput").ap()

    din("x0", (TCORE, H))
    din("gt", (FCN, 128, H)); din("ut", (FCN, 128, H)); din("dtw", (FCN, 128, H))
    din("upw", (SCN, 128, H)); din("gw", (SCN, 128, H))
    din("lao", (SCN, 128, H)); din("fuse", (G, SCN, 128, H))
    din("wq", (128, HC * 64)); din("wu1", (UCN, 128, H))
    din("wvd", (2, 128, H)); din("wvu", (2, 128, H)); din("auxv", (SCN, 128, H))
    din("akt", (BOT, NSLOT)); din("wrt", (BOT, RD))
    din("lp", (128, SCN)); din("bu1", (128, UCN)); din("wu2", (128, UCN))
    din("df", (128, G)); din("scal", (128, 3)); din("logrel", (128, NSLOT))

    okind = "ExternalOutput" if debug_xs else "Internal"
    d["x1"] = nc.dram_tensor("x1", [TCORE, H], F32, kind=okind).ap()
    d["x2"] = nc.dram_tensor("x2", [TCORE, H], F32, kind=okind).ap()
    d["y"] = nc.dram_tensor("y", [TCORE, H], F32, kind="ExternalOutput").ap()
    ymix = [nc.dram_tensor(f"ymix{i}", [TILE, H], F32, kind="Internal").ap()
            for i in range(2)]
    CW = {}
    for name, shape in [("gt", (FCN, 128, H)), ("ut", (FCN, 128, H)),
                        ("dtw", (FCN, 128, H)), ("upw", (SCN, 128, H)),
                        ("gw", (SCN, 128, H)), ("lao", (SCN, 128, H)),
                        ("fuse", (G, SCN, 128, H)), ("wu1", (UCN, 128, H)),
                        ("auxv", (SCN, 128, H)), ("wvd", (2, 128, H)),
                        ("wvu", (2, 128, H))]:
        CW[name] = nc.dram_tensor(name + "_c", list(shape), BF16, kind="Internal").ap()
    cc_in = [nc.dram_tensor(f"cc{i}_in", [1, 8], F32, kind="Internal") for i in (1, 2)]
    cc_out = [nc.dram_tensor(f"cc{i}_out", [1, 8], F32, kind="Internal") for i in (1, 2)]

    def sbt(name, shape, dt):
        return ctx.enter_context(nc.sbuf_tensor(name, list(shape), dt))

    ident = sbt("ident", [128, 128], BF16)
    ones_col = sbt("ones_col", [128, 1], F32)
    ones_row = sbt("ones_row", [1, 128], F32)
    lp_sb = sbt("lp_sb", [128, SCN], F32)
    bu1_sb = sbt("bu1_sb", [128, UCN], F32)
    wu2_sb = sbt("wu2_sb", [128, UCN], F32)
    df_sb = sbt("df_sb", [128, G], F32)
    ndf_sb = sbt("ndf_sb", [128, G], F32)
    scal_sb = sbt("scal_sb", [128, 3], F32)
    logrel_sb = sbt("logrel_sb", [128, NSLOT], F32)
    negbig_sb = sbt("negbig_sb", [128, NSLOT], F32)
    akt_sb = sbt("akt_sb", [BOT, NSLOT], F32)
    wrt_sb = sbt("wrt_sb", [BOT, RD], F32)
    rkt_sb = sbt("rkt_sb", [RD, NSLOT], F32)
    wq_sb = sbt("wq_sb", [128, HC * 64], BF16)
    stM = sbt("stM", [128, NSTG], F32)
    stR = sbt("stR", [128, NSTG], F32)
    stRh = sbt("stRh", [128, NSTG], F32)
    stA2 = sbt("stA2", [128, NSTG], F32)
    stB2 = sbt("stB2", [128, NSTG], F32)
    stL = sbt("stL", [128, NSTG], F32)
    ssA = sbt("ssA", [128, NSTG], F32)
    ssQ = sbt("ssQ", [128, NSTG], F32)
    rm = [sbt("rm1", [128, 1], F32), sbt("rm2", [128, 1], F32)]
    t8_sb = sbt("t8_sb", [128, 8], F32)
    sm1 = sbt("sm1", [128, 8], F32)
    sm2 = sbt("sm2", [128, 8], F32)
    sm3 = sbt("sm3", [128, 8], F32)
    learned_sb = sbt("learned_sb", [128, NSTT], F32)
    cc_sb = sbt("cc_sb", [1, 8], F32)
    w_stage = sbt("w_stage", [128, 4096], BF16)

    AB = 172 * KB
    arena = ctx.enter_context(nc.sbuf_tensor("arena", [128, AB], U8))
    a_f32 = arena.bitcast(F32).ap()
    a_bf = arena.bitcast(BF16).ap()
    a_u8 = arena.ap()

    def cf32(off_b, n):
        assert off_b % 4 == 0 and off_b + 4 * n <= AB, (off_b, n)
        return a_f32[:, off_b // 4: off_b // 4 + n]

    def cbf(off_b, n):
        assert off_b % 2 == 0 and off_b + 2 * n <= AB, (off_b, n)
        return a_bf[:, off_b // 2: off_b // 2 + n]

    def cu8(off_b, n):
        assert off_b + n <= AB
        return a_u8[:, off_b: off_b + n]

    hT = cbf(0, HC * TILE)
    hT3 = a_bf[:, 0:HC * TILE].rearrange("p (hc t) -> p hc t", t=TILE)
    R1 = 32 * KB
    tmpT = cbf(R1, 7 * SCN * TILE)
    aT = cbf(R1, FCN * TILE)
    aux1T = cbf(R1, HC * TILE)
    wT = cbf(R1 + 32 * KB, SCN * TILE)
    wT3 = a_bf[:, (R1 + 32 * KB) // 2:(R1 + 32 * KB) // 2 + SCN * TILE] \
        .rearrange("p (sc t) -> p sc t", t=TILE)
    v1T = cbf(R1 + 40 * KB, 2 * TILE)
    q_sb = cf32(R1 + 42 * KB, TILE)
    rq_sb = cf32(R1 + 44 * KB, TILE)
    m1g = cf32(R1 + 46 * KB, UCN * TILE)
    R_sb = cf32(R1 + 62 * KB, NSLOT)
    Sp_sb = cf32(R1 + 66 * KB, NSLOT)
    Sm_sb = cf32(R1 + 70 * KB, NSLOT)
    ex_sb = cf32(R1 + 74 * KB, NSLOT)
    mask_sb = cu8(R1 + 78 * KB, NSLOT)
    scr1k = cf32(R1 + 62 * KB, NSLOT)
    sxld = [cf32(R1, H), cf32(R1 + 16 * KB, H), cf32(R1 + 32 * KB, H)]

    R2 = 112 * KB
    xld = [cf32(R2, H), cf32(R2 + 16 * KB, H)]
    h_bf = cbf(R2 + 32 * KB, H)
    wring = [cbf(R2 + i * 8 * KB, H) for i in range(4)]
    wring6 = wring + [cbf(R2 + 44 * KB, H), cbf(R2 + 52 * KB, H)]
    mx = [cf32(R2 + 32 * KB + i * 2 * KB, TILE) for i in range(6)]
    dring = [cbf(R2 + i * 2 * KB, 1024) for i in range(6)]
    evb = [cf32(R2 + 12 * KB + i * 4 * KB, 1024) for i in range(4)]
    xq = [cf32(R2 + 28 * KB + i * 4 * KB, 1024) for i in range(4)]
    yq = [cf32(R2 + 44 * KB + i * 4 * KB, 1024) for i in range(4)]
    ffring = [cbf(R2 + 16 * KB + i * 8 * KB, H) for i in range(3)]
    avring = [cbf(R2 + i * 2 * KB, 1024) for i in range(4)]
    wvd_sb = [cbf(R2, H), cbf(R2 + 8 * KB, H)]
    mx7 = cf32(R2 + 56 * KB, TILE)
    wvu7 = [cbf(R2, H), cbf(R2 + 8 * KB, H)]
    xq7 = [cf32(R2 + 16 * KB + i * 4 * KB, 1024) for i in range(4)]
    evb7 = [cf32(R2 + 32 * KB + i * 4 * KB, 1024) for i in range(4)]
    xq7w = [cf32(R2 + 16 * KB, 2048), cf32(R2 + 24 * KB, 2048)]
    evb7w = [cf32(R2 + 32 * KB, 2048), cf32(R2 + 40 * KB, 2048)]
    evbw = cf32(R2 + 12 * KB, 4096)
    xqw = cf32(R2 + 28 * KB, 4096)
    yqw = cf32(R2 + 44 * KB, 4096)
    gz = cf32(R2 + 48 * KB, TILE)
    gt1 = cf32(R2 + 50 * KB, TILE)
    gt2 = cf32(R2 + 52 * KB, TILE)

    sx = Sem(ctx, nc, "sx")
    sst = Sem(ctx, nc, "sst")
    si = Sem(ctx, nc, "si")
    swq = Sem(ctx, nc, "swq")
    scx = Sem(ctx, nc, "scx")
    sx2 = [Sem(ctx, nc, f"sx2_{i}") for i in range(2)]
    ssl = [Sem(ctx, nc, f"ssl{i}") for i in range(3)]
    swr = [Sem(ctx, nc, f"swr{i}") for i in range(4)]
    sdr = [Sem(ctx, nc, f"sdr{i}") for i in range(6)]
    sfr = [Sem(ctx, nc, f"sfr{i}") for i in range(3)]
    sqy = [Sem(ctx, nc, f"sqy{i}") for i in range(4)]
    swb = [Sem(ctx, nc, f"swb{i}") for i in range(4)]
    sdwb = [Sem(ctx, nc, f"sdwb{i}") for i in range(6)]
    sfwb = [Sem(ctx, nc, f"sfwb{i}") for i in range(3)]
    sw = Sem(ctx, nc, "sw")
    sw2 = Sem(ctx, nc, "sw2")
    sp = Sem(ctx, nc, "sp")
    sv = Sem(ctx, nc, "sv")
    sa = Sem(ctx, nc, "sa")
    sg = Sem(ctx, nc, "sg")
    scc = Sem(ctx, nc, "scc")

    block = ctx.enter_context(nc.Block())
    pst = PsumStack(nc)
    gd = {"pe": 0, "sv": 0, "sa": 0, "st": 0}

    def init_sync(s):
        for name, dst in [("lp", lp_sb), ("bu1", bu1_sb), ("wu2", wu2_sb),
                          ("df", df_sb), ("scal", scal_sb), ("logrel", logrel_sb),
                          ("akt", akt_sb), ("wrt", wrt_sb)]:
            si.inc(s.dma_start(dst[:, :], d[name]), 16)
    block.sync(init_sync)
    n_init_loads = si.n

    def init_g(g):
        sg.inc(g.memset(negbig_sb[:, :], NEGBIG))
        sg.inc(g.memset(ident[:, :], 0.0))
        g.wait_ge(sg.h, sg.n)
        sg.inc(g.affine_select(out=ident[:, :], in_=ident[:, :],
                               compare_op=ALU.not_equal, fill=1.0, base=0,
                               pattern=[[-1, 128]], channel_multiplier=1))
        sg.inc(g.memset(ones_col[:, :], 1.0))
        sg.inc(g.memset(ones_row[:, :], 1.0))
        swq.inc(g.dma_start(wq_sb[:, :], d["wq"]), 16)
    block.gpsimd(init_g)

    def init_v(v):
        v.wait_ge(si.h, n_init_loads)
        sv.inc(v.tensor_scalar(ndf_sb[:, :], df_sb[:, :], -1.0, None, op0=ALU.mult))
    block.vector(init_v)

    psrk_t = pst.enter([128, NSLOT], F32)

    def init_t(t):
        t.wait_ge(si.h, n_init_loads)
        t.matmul(psrk_t[0:RD, 0:512], wrt_sb[:, :], akt_sb[:, 0:512], start=True, stop=True)
        sp.inc(t.matmul(psrk_t[0:RD, 512:1024], wrt_sb[:, :], akt_sb[:, 512:1024],
                        start=True, stop=True))
    block.tensor(init_t)
    pe_rk = sp.n

    def init_v2(v):
        v.wait_ge(sp.h, pe_rk)
        sv.inc(v.tensor_copy(rkt_sb[:, :], psrk_t[0:RD, :]))
    block.vector(init_v2)
    pst.exit(1)
    gd.update(pe=sp.n, sv=sv.n, sa=sa.n, st=sst.n)

    def emit_stats(src, cc_idx):
        slb = [s.n for s in ssl]
        svb = sv.n
        sab = sa.n
        st_guard = sst.n
        g_pe, g_sv, g_sa = gd["pe"], gd["sv"], gd["sa"]
        grp_tok = []

        def fsq(sc):
            sc.wait_ge(sst.h, st_guard)
            sc.wait_ge(sp.h, g_pe)
            sc.wait_ge(sv.h, g_sv)
            for st in range(3):
                ssl[st % 3].inc(sc.dma_start(sxld[st % 3],
                                             src[st * 128:(st + 1) * 128, :]), 16)
            for st in range(NSTG):
                sc.wait_ge(ssl[st % 3].h, slb[st % 3] + 16 * (st // 3 + 1))
                if st >= 1:
                    sc.wait_ge(sv.h, svb + 2 * st)
                for k in range(4):
                    i = sc.activation(scr1k, sxld[st % 3][:, k * 1024:(k + 1) * 1024],
                                      AF.Square, accum_out=sm1[:, k:k + 1])
                sa.inc(i)
                nxt = st + 3
                if nxt < NSTG:
                    sc.wait_ge(sv.h, svb + 2 * (st + 1))
                    ssl[nxt % 3].inc(sc.dma_start(sxld[nxt % 3],
                                                  src[nxt * 128:(nxt + 1) * 128, :]), 16)
        block.scalar(fsq)
        assert sa.n == sab + NSTG

        def fred(v):
            for st in range(NSTG):
                v.wait_ge(ssl[st % 3].h, slb[st % 3] + 16 * (st // 3 + 1))
                v.wait_ge(sv.h, sv.n)
                sv.inc(v.reduce_sum(ssA[:, st:st + 1], sxld[st % 3][:, :], axis=AX.X))
                v.wait_ge(sa.h, sab + st + 1)
                v.wait_ge(sv.h, sv.n)
                sv.inc(v.reduce_sum(ssQ[:, st:st + 1], sm1[:, 0:4], axis=AX.X))
        block.vector(fred)
        assert sv.n == svb + 2 * NSTG

        for g_ in range(4):
            c = slice(4 * g_, 4 * g_ + 4)
            svg = svb + 2 * (4 * g_ + 4)

            def ffin(v, c=c, svg=svg):
                def w():
                    v.wait_ge(sv.h, sv.n)
                v.wait_ge(sv.h, svg)
                w(); sv.inc(v.tensor_scalar(stM[:, c], ssA[:, c], 1.0 / H, None, op0=ALU.mult))
                sv.inc(v.tensor_scalar(ssQ[:, c], ssQ[:, c], 1.0 / H, None, op0=ALU.mult))
                w(); sv.inc(v.tensor_mul(ssA[:, c], stM[:, c], stM[:, c]))
                w(); sv.inc(v.tensor_sub(ssQ[:, c], ssQ[:, c], ssA[:, c]))
                w(); sv.inc(v.tensor_scalar(ssA[:, c], ssQ[:, c], 1e-5, None, op0=ALU.add))
                w(); sv.inc(v.reciprocal(stR[:, c], ssA[:, c]))
                w(); sv.inc(v.tensor_mul(stL[:, c], ssQ[:, c], stR[:, c]))
            block.vector(ffin)
            svn = sv.n

            def ffin_a(sc, c=c, svn=svn):
                sc.wait_ge(sv.h, svn)
                sc.wait_ge(sa.h, sa.n)
                sa.inc(sc.activation(ssQ[:, c], ssA[:, c], AF.Ln))
                sc.wait_ge(sa.h, sa.n)
                sa.inc(sc.activation(stR[:, c], ssQ[:, c], AF.Exp, scale=-0.5))
                sc.wait_ge(sa.h, sa.n)
                sa.inc(sc.activation(stL[:, c], stL[:, c], AF.Ln, bias=1.0))
            block.scalar(ffin_a)
            san = sa.n

            def ffin_v(v, c=c, san=san):
                v.wait_ge(sa.h, san)
                v.wait_ge(sv.h, sv.n)
                sv.inc(v.tensor_scalar(stRh[:, c], stR[:, c], 0.5, None, op0=ALU.mult))
                v.wait_ge(sv.h, sv.n)
                sv.inc(v.tensor_scalar(stA2[:, c], stRh[:, c], 1.0, None, op0=ALU.add))
                sv.inc(v.tensor_mul(stB2[:, c], stM[:, c], stRh[:, c]))
                v.wait_ge(sv.h, sv.n)
                sv.inc(v.tensor_scalar(stB2[:, c], stB2[:, c], -1.0, None, op0=ALU.mult))
            block.vector(ffin_v)
            grp_tok.append((sv.n, sa.n))

        def emit_cc_a(guard_sa=0, guard_sv=0):
            pslv_t = pst.enter([128, NSTG], F32)
            svn2, san2 = grp_tok[-1]

            def flv(t):
                t.wait_ge(sa.h, max(san2, guard_sa))
                t.wait_ge(sv.h, guard_sv)
                sp.inc(t.matmul(pslv_t[0:1, :], ones_col[:, :], stL[:, :],
                                start=True, stop=True))
            block.tensor(flv)
            pen = sp.n

            def flv_v(v):
                v.wait_ge(sp.h, pen)
                v.wait_ge(sv.h, sv.n)
                sv.inc(v.reduce_sum(cc_sb[0:1, 0:1], pslv_t[0:1, :], axis=AX.X))
            block.vector(flv_v)
            svn3 = sv.n

            def fst(s):
                s.wait_ge(sv.h, svn3)
                sst.inc(s.dma_start(cc_in[cc_idx].ap()[0:1, 0:1], cc_sb[0:1, 0:1]), 16)
            block.sync(fst)
            stn = sst.n

            def fcc(g):
                g.wait_ge(sst.h, stn)
                scc.inc(g.collective_compute(
                    "AllReduce", ALU.add, replica_groups=[list(range(NCORES))],
                    ins=[cc_in[cc_idx].ap().opt()],
                    outs=[cc_out[cc_idx].ap().opt()]), 1)
            block.gpsimd(fcc)
            pst.exit(1)
            return svn3, scc.n

        def emit_cc(a_tok):
            ccn = a_tok[1]
            psbc = pst.enter([128, 1], F32)

            def fld(s):
                s.wait_ge(scc.h, ccn)
                scx.inc(s.dma_start(cc_sb[0:1, 1:2], cc_out[cc_idx].ap()[0:1, 0:1]), 16)
            block.sync(fld)
            sxn = scx.n

            def fmn(v):
                v.wait_ge(scx.h, sxn)
                v.wait_ge(sv.h, sv.n)
                sv.inc(v.tensor_scalar(cc_sb[0:1, 2:3], cc_sb[0:1, 1:2],
                                       1.0 / 16384.0, 1e-6, op0=ALU.mult, op1=ALU.add))
                v.wait_ge(sv.h, sv.n)
                sv.inc(v.reciprocal(cc_sb[0:1, 3:4], cc_sb[0:1, 2:3]))
            block.vector(fmn)
            svn4 = sv.n

            def fbc(t):
                t.wait_ge(sv.h, svn4)
                sp.inc(t.matmul(psbc[:, :], ones_row[:, :], cc_sb[0:1, 3:4],
                                start=True, stop=True))
            block.tensor(fbc)
            pen2 = sp.n

            def fbc_v(v):
                v.wait_ge(sp.h, pen2)
                v.wait_ge(sv.h, sv.n)
                sv.inc(v.tensor_copy(rm[cc_idx][:, :], psbc[:, :]))
            block.vector(fbc_v)
            pst.exit(1)
            gd["rm_sv"] = sv.n
        if cc_idx is None:
            return grp_tok, None, None
        return grp_tok, emit_cc_a, emit_cc

    def emit_prologue(src, tile, grp=None):
        sxb = [s.n for s in sx2]
        svb = sv.n
        sab = sa.n
        peb = sp.n
        g_pe, g_sv, g_sa = gd["pe"], gd["sv"], gd["sa"]

        def floads(s):
            s.wait_ge(sp.h, g_pe)
            s.wait_ge(sv.h, g_sv)
            s.wait_ge(sa.h, g_sa)
            s.wait_ge(sst.h, gd["st"])
            for st in range(NSTT):
                if st >= 2:
                    s.wait_ge(sv.h, svb + (st - 1))
                r0 = tile * TILE + st * 128
                sx2[st % 2].inc(s.dma_start(xld[st % 2], src[r0:r0 + 128, :]), 16)
        block.sync(floads)

        def fh(v):
            if grp is not None:
                v.wait_ge(sv.h, grp[0])
                v.wait_ge(sa.h, grp[1])
            for st in range(NSTT):
                gix = tile * NSTT + st
                v.wait_ge(sx2[st % 2].h, sxb[st % 2] + 16 * (st // 2 + 1))
                if st > 0:
                    v.wait_ge(sp.h, peb + st * 4)
                v.wait_ge(sv.h, sv.n)
                sv.inc(v.tensor_scalar(h_bf[:, :], xld[st % 2][:, :],
                                       stM[:, gix:gix + 1], stR[:, gix:gix + 1],
                                       op0=ALU.subtract, op1=ALU.mult))
        block.vector(fh)
        assert sv.n == svb + NSTT

        trp = [pst.enter([128, 1024], BF16), pst.enter([128, 1024], BF16)]

        def ftr(t):
            t.wait_ge(sv.h, g_sv)
            t.wait_ge(sa.h, g_sa)
            for st in range(NSTT):
                t.wait_ge(sv.h, svb + st + 1)
                for r in range(4):
                    gidx = st * 4 + r
                    pb = trp[gidx % 2]
                    if gidx >= 2:
                        t.wait_ge(sa.h, sab + gidx - 1)
                    for k in range(8):
                        hc = r * 8 + k
                        i = t.transpose(pb[:, k * 128:(k + 1) * 128],
                                        h_bf[:, hc * 128:(hc + 1) * 128], ident[:, :])
                    sp.inc(i)
        block.tensor(ftr)
        assert sp.n == peb + 16

        def fev(sc):
            sc.wait_ge(sp.h, g_pe)
            for st in range(NSTT):
                for r in range(4):
                    gidx = st * 4 + r
                    sc.wait_ge(sp.h, peb + gidx + 1)
                    pb3 = trp[gidx % 2].ap().rearrange("p (a b) -> p a b", b=128)
                    dst3 = hT3[:, r * 8:r * 8 + 8, st * 128:(st + 1) * 128]
                    sa.inc(sc.copy(dst3, pb3))
        block.scalar(fev)
        assert sa.n == sab + 16
        pst.exit(2)
        return {"hT_sa": sa.n, "xld_sv": svb + NSTT, "pe_end": peb + 16}

    def emit_attn_tile(src, dst, tile, grp=None):
        st_entry = sst.n
        pro = emit_prologue(src, tile, grp)
        hT_ready = pro["hT_sa"]

        psP = [pst.enter([128, 1024], F32), pst.enter([128, 1024], F32)]
        wrb = [s.n for s in swr]
        peb = sp.n
        svb = sv.n
        sab = sa.n
        DV = 17
        AC = 7

        build = (tile == 0)
        wbb = [s.n for s in swb]
        wrb_pre = [s.n for s in swr]

        def m1g_(g):
            g.wait_ge(sv.h, pro["xld_sv"])
            g.wait_ge(sst.h, st_entry)
            if not build:
                for k in range(4):
                    g.wait_ge(swb[k].h, wbb[k])
            for sc_i in range(SCN):
                a2 = 2 * (sc_i % 2)
                if sc_i >= 2:
                    g.wait_ge(sp.h, peb + 2 * (sc_i - 1))
                    if build:
                        g.wait_ge(swb[a2].h, wbb[a2] + 16 * (sc_i // 2))
                        g.wait_ge(swb[a2 + 1].h, wbb[a2 + 1] + 16 * (sc_i // 2))
                su, sg_ = (d["upw"], d["gw"]) if build else (CW["upw"], CW["gw"])
                swr[a2].inc(g.dma_start(wring[a2], su[sc_i]), 16)
                swr[a2 + 1].inc(g.dma_start(wring[a2 + 1], sg_[sc_i]), 16)
        block.gpsimd(m1g_)

        if build:
            def m1wb(s):
                for sc_i in range(SCN):
                    a2 = 2 * (sc_i % 2)
                    s.wait_ge(swr[a2].h, wrb_pre[a2] + 16 * (sc_i // 2 + 1))
                    swb[a2].inc(s.dma_start(CW["upw"][sc_i], wring[a2]), 16)
                    s.wait_ge(swr[a2 + 1].h, wrb_pre[a2 + 1] + 16 * (sc_i // 2 + 1))
                    swb[a2 + 1].inc(s.dma_start(CW["gw"][sc_i], wring[a2 + 1]), 16)
            block.sync(m1wb)

        def m1t(t):
            t.wait_ge(sa.h, hT_ready)
            for sc_i in range(SCN):
                a2 = 2 * (sc_i % 2)
                t.wait_ge(swr[a2].h, wrb[a2] + 16 * (sc_i // 2 + 1))
                t.wait_ge(swr[a2 + 1].h, wrb[a2 + 1] + 16 * (sc_i // 2 + 1))
                if sc_i >= 2:
                    t.wait_ge(sv.h, svb + DV * (sc_i - 2) + 3)
                pA = psP[sc_i % 2][:, 0:512]
                pB = psP[sc_i % 2][:, 512:1024]
                for hc in range(HC):
                    i = t.matmul(pA, wring[2 * (sc_i % 2)][:, hc * 128:(hc + 1) * 128],
                                 hT[:, hc * TILE:hc * TILE + TILE],
                                 start=(hc == 0), stop=(hc == HC - 1))
                sp.inc(i)
                for hc in range(HC):
                    i = t.matmul(pB, wring[2 * (sc_i % 2) + 1][:, hc * 128:(hc + 1) * 128],
                                 hT[:, hc * TILE:hc * TILE + TILE],
                                 start=(hc == 0), stop=(hc == HC - 1))
                sp.inc(i)
        block.tensor(m1t)
        assert sp.n == peb + 2 * SCN

        def m1a(sc):
            for sc_i in range(SCN):
                base_v = svb + DV * sc_i
                sc.wait_ge(sv.h, base_v + 5)
                sc.wait_ge(sa.h, sa.n)
                sa.inc(sc.activation(mx[4], mx[2], AF.Ln))
                for m in range(1, G + 1):
                    dt_slot = mx[5] if m % 2 == 1 else mx[2]
                    if m <= 2:
                        sc.wait_ge(sv.h, base_v + 9)
                    else:
                        tmp_idx = {1: 10, 2: 11, 3: 12, 4: 13}[m - 2]
                        sc.wait_ge(sv.h, base_v + tmp_idx)
                    sc.wait_ge(sa.h, sa.n)
                    sa.inc(sc.activation(dt_slot, mx[4], AF.Exp, scale=float(m),
                                         bias=ndf_sb[:, m - 1:m]))
        block.scalar(m1a)
        assert sa.n == sab + AC * SCN

        def m1v(v):
            for sc_i in range(SCN):
                base_a = sab + AC * sc_i

                def w():
                    v.wait_ge(sv.h, sv.n)
                pA = psP[sc_i % 2][:, 0:512]
                pB = psP[sc_i % 2][:, 512:1024]
                v.wait_ge(sp.h, peb + 2 * sc_i + 1)
                w()
                sv.inc(v.tensor_copy(tmpT[:, sc_i * TILE:sc_i * TILE + TILE], pA))
                sv.inc(v.tensor_copy(mx[0], pA))
                v.wait_ge(sp.h, peb + 2 * sc_i + 2)
                sv.inc(v.tensor_scalar(mx[1], pB, lp_sb[:, sc_i:sc_i + 1], None,
                                       op0=ALU.subtract))
                if sc_i >= 1:
                    v.wait_ge(sa.h, sab + AC * (sc_i - 1) + 6)
                w()
                sv.inc(v.tensor_scalar(mx[5], mx[1], -1.0, None, op0=ALU.mult))
                if sc_i >= 1:
                    v.wait_ge(sa.h, sab + AC * sc_i)
                w()
                sv.inc(v.tensor_max(mx[2], mx[1], mx[5]))
                w()
                sv.inc(v.tensor_scalar(mx[3], mx[1], 0.0, None, op0=ALU.is_gt))
                w()
                sv.inc(v.tensor_scalar(mx[5], mx[3], 2.0, -1.0, op0=ALU.mult,
                                       op1=ALU.add))
                w()
                sv.inc(v.tensor_scalar(mx[3], mx[2], 2.5, None, op0=ALU.is_le))
                w()
                sv.inc(v.tensor_mul(mx[1], mx[0], mx[5]))
                nops = 9
                for m in range(1, G + 1):
                    dt_slot = mx[5] if m % 2 == 1 else mx[2]
                    src_t = mx[1] if m % 2 == 1 else mx[0]
                    slot = tmpT[:, (m * SCN + sc_i) * TILE:(m * SCN + sc_i) * TILE + TILE]
                    v.wait_ge(sa.h, base_a + 1 + m)
                    w()
                    if m <= 4:
                        sv.inc(v.tensor_mul(slot, dt_slot, src_t)); nops += 1
                    else:
                        sv.inc(v.tensor_mul(mx7, dt_slot, src_t))
                        w()
                        sv.inc(v.tensor_mul(slot, mx7, mx[3]))
                        nops += 2
                assert nops == DV
        block.vector(m1v)
        assert sv.n == svb + DV * SCN
        pst.exit(2)
        m1_pe_end = sp.n
        m1_sv_end = sv.n

        psY = pst.enter([128, 4096], F32)
        NTL = 7 * SCN
        sdb = [s.n for s in sdr]
        peb2 = sp.n
        svb2 = sv.n
        stb2 = sst.n

        dwbb = [s.n for s in sdwb]
        sdb_pre = [s.n for s in sdr]

        def _m2src(j, hq, cached):
            term = j // SCN
            sc_i = j % SCN
            base = (CW if cached else d)
            if term == 0:
                return base["lao"][sc_i, :, hq * 1024:(hq + 1) * 1024]
            return base["fuse"][term - 1, sc_i, :, hq * 1024:(hq + 1) * 1024]

        def m2g_(g):
            g.wait_ge(sp.h, m1_pe_end)
            if not build:
                for k in range(6):
                    g.wait_ge(sdwb[k].h, dwbb[k])
            for hq in range(4):
                for j in range(NTL):
                    gi = hq * NTL + j
                    if gi >= 6:
                        g.wait_ge(sp.h, peb2 + gi - 5)
                        if build:
                            g.wait_ge(sdwb[gi % 6].h, dwbb[gi % 6] + 16 * (gi // 6))
                    sdr[gi % 6].inc(g.dma_start(dring[gi % 6],
                                                _m2src(j, hq, not build)), 16)
        block.gpsimd(m2g_)



        def m2t(t):
            t.wait_ge(sv.h, m1_sv_end)
            for hq in range(4):
                if hq > 0:
                    t.wait_ge(sv.h, svb2 + hq)
                for j in range(NTL):
                    gi = hq * NTL + j
                    t.wait_ge(sdr[gi % 6].h, sdb[gi % 6] + 16 * (gi // 6 + 1))
                    term = j // SCN
                    sc_i = j % SCN
                    for st in range(NSTT):
                        lhs = tmpT[:, (term * SCN + sc_i) * TILE + st * 128:
                                   (term * SCN + sc_i) * TILE + st * 128 + 128]
                        for half in range(2):
                            i = t.matmul(psY[:, st * 1024 + half * 512:
                                             st * 1024 + half * 512 + 512],
                                         lhs, dring[gi % 6][:, half * 512:half * 512 + 512],
                                         start=(j == 0), stop=(j == NTL - 1),
                                         skip_group_check=True)
                    sp.inc(i)
        block.tensor(m2t)
        assert sp.n == peb2 + 4 * NTL

        def m2v(v):
            v.wait_ge(sst.h, st_entry)
            for hq in range(4):
                v.wait_ge(sp.h, peb2 + NTL * (hq + 1))
                if hq >= 1:
                    v.wait_ge(sst.h, stb2 + 64 * hq)
                v.wait_ge(sv.h, sv.n)
                sv.inc(v.tensor_copy(evbw, psY[:, :]))
        block.vector(m2v)
        assert sv.n == svb2 + 4

        def m2s(s):
            for hq in range(4):
                if build:
                    for j in range(NTL):
                        gi = hq * NTL + j
                        s.wait_ge(sdr[gi % 6].h, sdb_pre[gi % 6] + 16 * (gi // 6 + 1))
                        sdwb[gi % 6].inc(s.dma_start(_m2src(j, hq, True),
                                                     dring[gi % 6]), 16)
                s.wait_ge(sv.h, svb2 + hq + 1)
                for st in range(NSTT):
                    sst.inc(s.dma_start(
                        ymix[tile % 2][st * 128:(st + 1) * 128,
                                       hq * 1024:(hq + 1) * 1024], evb[st]), 16)
        block.sync(m2s)
        pst.exit(1)
        m2_pe_end = sp.n
        m2_sv_end = sv.n
        m2_st_end = sst.n

        psP2 = [pst.enter([128, 1024], F32), pst.enter([128, 1024], F32)]
        wq6 = list(swr) + [sfr[0], sfr[1]]
        P3 = 2 if build else 3
        wrb3 = [s.n for s in wq6]
        peb3 = sp.n
        svb3 = sv.n
        sab3 = sa.n

        wbb3 = [s.n for s in swb]
        wrb3_pre = [s.n for s in swr]

        def t1g(g):
            g.wait_ge(sp.h, m2_pe_end)
            g.wait_ge(sst.h, m2_st_end)
            if not build:
                for k in range(4):
                    g.wait_ge(swb[k].h, wbb3[k])
            for fc in range(FCN):
                a2 = 2 * (fc % P3)
                if fc >= P3:
                    g.wait_ge(sv.h, svb3 + (fc - P3 + 1))
                    if build:
                        g.wait_ge(swb[a2].h, wbb3[a2] + 16 * (fc // P3))
                        g.wait_ge(swb[a2 + 1].h, wbb3[a2 + 1] + 16 * (fc // P3))
                sgt, sut = (d["gt"], d["ut"]) if build else (CW["gt"], CW["ut"])
                wq6[a2].inc(g.dma_start(wring6[a2], sgt[fc]), 16)
                wq6[a2 + 1].inc(g.dma_start(wring6[a2 + 1], sut[fc]), 16)
        block.gpsimd(t1g)

        if build:
            def t1wb(s):
                for fc in range(FCN):
                    a2 = 2 * (fc % 2)
                    s.wait_ge(swr[a2].h, wrb3_pre[a2] + 16 * (fc // 2 + 1))
                    swb[a2].inc(s.dma_start(CW["gt"][fc], wring[a2]), 16)
                    s.wait_ge(swr[a2 + 1].h, wrb3_pre[a2 + 1] + 16 * (fc // 2 + 1))
                    swb[a2 + 1].inc(s.dma_start(CW["ut"][fc], wring[a2 + 1]), 16)
            block.sync(t1wb)

        def t1t(t):
            t.wait_ge(sv.h, m2_sv_end)
            for fc in range(FCN):
                a2 = 2 * (fc % P3)
                t.wait_ge(wq6[a2].h, wrb3[a2] + 16 * (fc // P3 + 1))
                t.wait_ge(wq6[a2 + 1].h, wrb3[a2 + 1] + 16 * (fc // P3 + 1))
                if fc >= 2:
                    t.wait_ge(sv.h, svb3 + (fc - 1))
                pA = psP2[fc % 2][:, 0:512]
                pB = psP2[fc % 2][:, 512:1024]
                for hc in range(HC):
                    i = t.matmul(pA, wring6[a2][:, hc * 128:(hc + 1) * 128],
                                 hT[:, hc * TILE:hc * TILE + TILE],
                                 start=(hc == 0), stop=(hc == HC - 1))
                sp.inc(i)
                for hc in range(HC):
                    i = t.matmul(pB, wring6[a2 + 1][:, hc * 128:(hc + 1) * 128],
                                 hT[:, hc * TILE:hc * TILE + TILE],
                                 start=(hc == 0), stop=(hc == HC - 1))
                sp.inc(i)
        block.tensor(t1t)
        assert sp.n == peb3 + 2 * FCN

        def t1a(sc):
            for fc in range(FCN):
                sc.wait_ge(sp.h, peb3 + 2 * fc + 1)
                if fc >= 2:
                    sc.wait_ge(sv.h, svb3 + (fc - 1))
                sc.wait_ge(sa.h, sa.n)
                sa.inc(sc.activation(mx[fc % 2], psP2[fc % 2][:, 0:512], AF.Silu))
        block.scalar(t1a)
        assert sa.n == sab3 + FCN

        def t1v(v):
            for fc in range(FCN):
                v.wait_ge(sa.h, sab3 + fc + 1)
                v.wait_ge(sp.h, peb3 + 2 * fc + 2)
                v.wait_ge(sv.h, sv.n)
                sv.inc(v.tensor_mul(aT[:, fc * TILE:(fc + 1) * TILE], mx[fc % 2],
                                    psP2[fc % 2][:, 512:1024]))
        block.vector(t1v)
        assert sv.n == svb3 + FCN
        pst.exit(2)
        t1_pe_end = sp.n
        t1_sv_end = sv.n

        psY2 = pst.enter([128, 4096], F32)
        sdb4 = [s.n for s in sdr]
        peb4 = sp.n
        svb4 = sv.n
        stb4 = sst.n
        sqb4 = [s.n for s in sqy]

        dwbb4 = [s.n for s in sdwb]
        sdb4_pre = [s.n for s in sdr]

        def t2g(g):
            g.wait_ge(sp.h, t1_pe_end)
            if not build:
                for k in range(6):
                    g.wait_ge(sdwb[k].h, dwbb4[k])
            for hq in range(4):
                for fc in range(FCN):
                    gi = hq * FCN + fc
                    if gi >= 6:
                        g.wait_ge(sp.h, peb4 + gi - 5)
                        if build:
                            g.wait_ge(sdwb[gi % 6].h, dwbb4[gi % 6] + 16 * (gi // 6))
                    sdt = d["dtw"] if build else CW["dtw"]
                    sdr[gi % 6].inc(g.dma_start(
                        dring[gi % 6], sdt[fc, :, hq * 1024:(hq + 1) * 1024]), 16)
        block.gpsimd(t2g)



        def t2s(s):
            s.wait_ge(sv.h, t1_sv_end)
            s.wait_ge(sst.h, m2_st_end)
            for hq in range(4):
                if hq >= 1:
                    s.wait_ge(sst.h, stb4 + 64 * hq)
                if hq >= 1:
                    s.wait_ge(sv.h, svb4 + 2 * hq)
                for st in range(NSTT):
                    r0 = tile * TILE + st * 128
                    sqy[st].inc(s.dma_start(xq[st], src[r0:r0 + 128,
                                                        hq * 1024:(hq + 1) * 1024]), 16)
                    sqy[st].inc(s.dma_start(yq[st],
                                            ymix[tile % 2][st * 128:(st + 1) * 128,
                                                           hq * 1024:(hq + 1) * 1024]), 16)
                if build:
                    for fc in range(FCN):
                        gi = hq * FCN + fc
                        s.wait_ge(sdr[gi % 6].h, sdb4_pre[gi % 6] + 16 * (gi // 6 + 1))
                        sdwb[gi % 6].inc(s.dma_start(
                            CW["dtw"][fc, :, hq * 1024:(hq + 1) * 1024],
                            dring[gi % 6]), 16)
                s.wait_ge(sv.h, svb4 + 2 * (hq + 1))
                for st in range(NSTT):
                    r0 = tile * TILE + st * 128
                    sst.inc(s.dma_start(dst[r0:r0 + 128, hq * 1024:(hq + 1) * 1024],
                                        yq[st]), 16)
        block.sync(t2s)

        def t2t(t):
            t.wait_ge(sv.h, t1_sv_end)
            for hq in range(4):
                if hq > 0:
                    t.wait_ge(sv.h, svb4 + 2 * (hq - 1) + 1)
                for fc in range(FCN):
                    gi = hq * FCN + fc
                    t.wait_ge(sdr[gi % 6].h, sdb4[gi % 6] + 16 * (gi // 6 + 1))
                    for st in range(NSTT):
                        lhs = aT[:, fc * TILE + st * 128:fc * TILE + st * 128 + 128]
                        for half in range(2):
                            i = t.matmul(psY2[:, st * 1024 + half * 512:
                                              st * 1024 + half * 512 + 512],
                                         lhs, dring[gi % 6][:, half * 512:half * 512 + 512],
                                         start=(fc == 0), stop=(fc == FCN - 1),
                                         skip_group_check=True)
                    sp.inc(i)
        block.tensor(t2t)
        assert sp.n == peb4 + 4 * FCN

        def t2v(v):
            v.wait_ge(sst.h, m2_st_end)
            for hq in range(4):
                v.wait_ge(sp.h, peb4 + FCN * (hq + 1))
                for st in range(NSTT):
                    v.wait_ge(sqy[st].h, sqb4[st] + 32 * (hq + 1))
                v.wait_ge(sv.h, sv.n)
                sv.inc(v.tensor_add(evbw, psY2[:, :], yqw))
                v.wait_ge(sv.h, sv.n)
                sv.inc(v.tensor_add(yqw, evbw, xqw))
        block.vector(t2v)
        assert sv.n == svb4 + 8
        pst.exit(1)
        gd.update(pe=sp.n, sv=sv.n, sa=sa.n, st=sst.n)

    def emit_ffn_tile(src, dst, tile, rm_idx, grp=None, cc_a=None, cc_emit=None):
        st_entry = sst.n
        build = (rm_idx == 0 and tile == 0)
        pro = emit_prologue(src, tile, grp)
        hT_ready = pro["hT_sa"]
        a_tok = None

        psQ_t = pst.enter([128, TILE], F32)
        psQ = psQ_t[0:BOT, :]
        psRQ_t = pst.enter([128, TILE], F32)
        psRQ = psRQ_t[0:RD, :]

        def f1t(t):
            t.wait_ge(sa.h, hT_ready)
            t.wait_ge(swq.h, 16)
            for hc in range(HC):
                i = t.matmul(psQ[:, :], wq_sb[:, hc * 64:(hc + 1) * 64],
                             hT[:, hc * TILE:hc * TILE + TILE],
                             start=(hc == 0), stop=(hc == HC - 1))
            sp.inc(i)
        block.tensor(f1t)
        pe_q = sp.n

        def f1v(v):
            v.wait_ge(sp.h, pe_q)
            v.wait_ge(sv.h, sv.n)
            sv.inc(v.tensor_copy(q_sb[0:BOT, :], psQ[:, :]))
        block.vector(f1v)
        sv_q = sv.n

        def f1t2(t):
            t.wait_ge(sv.h, sv_q)
            sp.inc(t.matmul(psRQ[:, :], wrt_sb[:, :], q_sb[0:BOT, :],
                            start=True, stop=True))
        block.tensor(f1t2)
        pe_rq = sp.n

        def f1v2(v):
            v.wait_ge(sp.h, pe_rq)
            v.wait_ge(sv.h, sv.n)
            sv.inc(v.tensor_copy(rq_sb[0:RD, :], psRQ[:, :]))
        block.vector(f1v2)
        sv_rq = sv.n
        pst.exit(2)

        psR = pst.enter([128, NSLOT], F32)
        psS = pst.enter([128, NSLOT], F32)
        psM = [pst.enter([128, TILE], F32), pst.enter([128, TILE], F32)]
        trpF = [pst.enter([128, 1024], BF16), pst.enter([128, 1024], BF16)]

        peb_rs = sp.n
        svb_sm = sv.n
        sab_sm = sa.n
        SMV = 8

        def f2t(t):
            t.wait_ge(sv.h, sv_rq)
            for st in range(NSTT):
                if st > 0:
                    t.wait_ge(sv.h, svb_sm + SMV * (st - 1) + 4)
                for half in range(2):
                    t.matmul(psR[:, half * 512:(half + 1) * 512],
                             rq_sb[0:RD, st * 128:(st + 1) * 128],
                             rkt_sb[:, half * 512:(half + 1) * 512],
                             start=True, stop=True)
                for half in range(2):
                    i = t.matmul(psS[:, half * 512:(half + 1) * 512],
                                 q_sb[0:BOT, st * 128:(st + 1) * 128],
                                 akt_sb[:, half * 512:(half + 1) * 512],
                                 start=True, stop=True)
                sp.inc(i)
        block.tensor(f2t)
        assert sp.n == peb_rs + NSTT

        def f2a(sc):
            for st in range(NSTT):
                sc.wait_ge(sv.h, svb_sm + SMV * st + 6)
                sc.wait_ge(sa.h, sa.n)
                sa.inc(sc.activation(ex_sb, Sm_sb, AF.Exp, bias=sm1[:, 0:1],
                                     accum_out=sm1[:, 1:2]))
        block.scalar(f2a)
        assert sa.n == sab_sm + NSTT

        def f2v(v):
            for st in range(NSTT):
                def w():
                    v.wait_ge(sv.h, sv.n)
                v.wait_ge(sp.h, peb_rs + st + 1)
                w()
                sv.inc(v.scalar_tensor_tensor(R_sb, psR[:, :], RSQ48, logrel_sb[:, :],
                                              op0=ALU.mult, op1=ALU.add))
                w()
                sv.inc(v.max(out=t8_sb[:, :], in_=R_sb))
                w()
                sv.inc(v.tensor_scalar(mask_sb, R_sb, t8_sb[:, 7:8], None,
                                       op0=ALU.is_ge))
                sv.inc(v.scalar_tensor_tensor(Sp_sb, psS[:, :], RSQ64, logrel_sb[:, :],
                                              op0=ALU.mult, op1=ALU.add))
                if st >= 1:
                    v.wait_ge(sa.h, sab_sm + st)
                w()
                sv.inc(v.select(Sm_sb, mask_sb, Sp_sb, negbig_sb[:, :]))
                w()
                sv.inc(v.reduce_max(sm1[:, 0:1], Sm_sb, axis=AX.X, negate=True))
                v.wait_ge(sa.h, sab_sm + st + 1)
                w()
                sv.inc(v.reciprocal(sm1[:, 2:3], sm1[:, 1:2]))
                w()
                sv.inc(v.tensor_scalar(w_stage[:, st * 1024:(st + 1) * 1024], ex_sb,
                                       sm1[:, 2:3], None, op0=ALU.mult))
        block.vector(f2v)
        assert sv.n == svb_sm + SMV * NSTT
        sm_sv_end = sv.n

        sfb = [s.n for s in sfr]
        peb_m = sp.n
        svb_g = sv.n
        sab_g = sa.n
        GAV = 2
        GVV = 6

        fwbb = [s.n for s in sfwb]
        sfb_pre = [s.n for s in sfr]

        def f3g(g):
            g.wait_ge(sv.h, pro["xld_sv"])
            g.wait_ge(sp.h, pro["pe_end"])
            g.wait_ge(sst.h, st_entry)
            if not build:
                for k in range(3):
                    g.wait_ge(sfwb[k].h, fwbb[k])
            for uc in range(UCN):
                if uc >= 3:
                    g.wait_ge(sp.h, peb_m + uc - 2)
                    if build:
                        g.wait_ge(sfwb[uc % 3].h, fwbb[uc % 3] + 16 * (uc // 3))
                swu = d["wu1"] if build else CW["wu1"]
                sfr[uc % 3].inc(g.dma_start(ffring[uc % 3], swu[uc]), 16)
        block.gpsimd(f3g)

        if build:
            def f3wb(s):
                for uc in range(UCN):
                    s.wait_ge(sfr[uc % 3].h, sfb_pre[uc % 3] + 16 * (uc // 3 + 1))
                    sfwb[uc % 3].inc(s.dma_start(CW["wu1"][uc], ffring[uc % 3]), 16)
            block.sync(f3wb)

        def f3t(t):
            t.wait_ge(sa.h, hT_ready)
            for uc in range(UCN):
                t.wait_ge(sfr[uc % 3].h, sfb[uc % 3] + 16 * (uc // 3 + 1))
                if uc >= 2:
                    t.wait_ge(sv.h, svb_g + GVV * (uc - 2) + 1)
                pb = psM[uc % 2]
                for hc in range(HC):
                    i = t.matmul(pb[:, :], ffring[uc % 3][:, hc * 128:(hc + 1) * 128],
                                 hT[:, hc * TILE:hc * TILE + TILE],
                                 start=(hc == 0), stop=(hc == HC - 1))
                sp.inc(i)
        block.tensor(f3t)
        assert sp.n == peb_m + UCN

        def f3a(sc):
            sc.wait_ge(sst.h, st_entry)
            for uc in range(UCN):
                sc.wait_ge(sv.h, svb_g + GVV * uc + 1)
                sc.wait_ge(sa.h, sa.n)
                sa.inc(sc.activation(gt1, gz, AF.Square))
                sc.wait_ge(sv.h, svb_g + GVV * uc + 3)
                sc.wait_ge(sa.h, sa.n)
                sa.inc(sc.activation(gt2, gt1, AF.Tanh, scale=GELU_C))
        block.scalar(f3a)
        assert sa.n == sab_g + GAV * UCN

        def f3v(v):
            v.wait_ge(sst.h, st_entry)
            for uc in range(UCN):
                def w():
                    v.wait_ge(sv.h, sv.n)
                v.wait_ge(sp.h, peb_m + uc + 1)
                if uc >= 1:
                    v.wait_ge(sa.h, sab_g + GAV * (uc - 1) + 1)
                w()
                sv.inc(v.tensor_scalar(gz, psM[uc % 2][:, :], bu1_sb[:, uc:uc + 1],
                                       None, op0=ALU.add))
                v.wait_ge(sa.h, sab_g + GAV * uc + 1)
                w()
                sv.inc(v.tensor_mul(gt2, gt1, gz))
                w()
                sv.inc(v.scalar_tensor_tensor(gt1, gt2, GELU_A, gz, op0=ALU.mult,
                                              op1=ALU.add))
                v.wait_ge(sa.h, sab_g + GAV * uc + 2)
                w()
                sv.inc(v.tensor_scalar(gt1, gt2, 1.0, None, op0=ALU.add))
                w()
                sv.inc(v.tensor_mul(gt2, gt1, gz))
                w()
                sv.inc(v.tensor_scalar(m1g[:, uc * TILE:(uc + 1) * TILE], gt2, 0.5,
                                       None, op0=ALU.mult))
        block.vector(f3v)
        assert sv.n == svb_g + GVV * UCN
        gelu_sv_end = sv.n

        peb_tr = sp.n
        sab_tr = sa.n

        def f2bt(t):
            for st in range(NSTT):
                t.wait_ge(sv.h, svb_sm + SMV * (st + 1))
                pb = trpF[st % 2]
                if st >= 2:
                    t.wait_ge(sa.h, sab_tr + st - 1)
                for k in range(8):
                    i = t.transpose(pb[:, k * 128:(k + 1) * 128],
                                    w_stage[:, st * 1024 + k * 128:st * 1024 + (k + 1) * 128],
                                    ident[:, :])
                sp.inc(i)
        block.tensor(f2bt)
        assert sp.n == peb_tr + NSTT

        def f2ba(sc):
            for st in range(NSTT):
                sc.wait_ge(sp.h, peb_tr + st + 1)
                pb3 = trpF[st % 2].ap().rearrange("p (a b) -> p a b", b=128)
                sa.inc(sc.copy(wT3[:, :, st * 128:(st + 1) * 128], pb3))
        block.scalar(f2ba)
        assert sa.n == sab_tr + NSTT
        wT_ready = sa.n
        pst.exit(6)

        ccA_sv = 0
        if cc_a is not None:
            a_tok = cc_a(wT_ready, gelu_sv_end)
            ccA_sv = a_tok[0]

        psL = pst.enter([128, 8], F32)
        peb_l = sp.n
        svb_l = sv.n

        def f4t(t):
            t.wait_ge(sv.h, max(gelu_sv_end, ccA_sv))
            t.wait_ge(sa.h, wT_ready)
            for st in range(NSTT):
                if st > 0:
                    t.wait_ge(sv.h, svb_l + st)
                for uc in range(UCN):
                    i = t.matmul(psL[:, 0:1],
                                 m1g[:, uc * TILE + st * 128:uc * TILE + st * 128 + 128],
                                 wu2_sb[:, uc:uc + 1],
                                 start=(uc == 0), stop=(uc == UCN - 1))
                sp.inc(i)
        block.tensor(f4t)
        assert sp.n == peb_l + NSTT

        def f4v(v):
            for st in range(NSTT):
                v.wait_ge(sp.h, peb_l + st + 1)
                v.wait_ge(sv.h, sv.n)
                sv.inc(v.tensor_copy(learned_sb[:, st:st + 1], psL[:, 0:1]))
        block.vector(f4v)
        assert sv.n == svb_l + NSTT
        learned_sv = sv.n
        pst.exit(1)


        psA1 = pst.enter([128, 4096], F32)
        sdb5 = [s.n for s in sdr]
        peb5 = sp.n
        sab5 = sa.n

        dwbb5 = [s.n for s in sdwb]
        sdb5_pre = [s.n for s in sdr]

        def f5g(g):
            g.wait_ge(sv.h, pro["xld_sv"])
            if not build:
                for k in range(4):
                    g.wait_ge(sdwb[k].h, dwbb5[k])
            for rnd in range(4):
                for sc_i in range(SCN):
                    gi = rnd * SCN + sc_i
                    if gi >= 4:
                        g.wait_ge(sp.h, peb5 + gi - 3)
                        if build:
                            g.wait_ge(sdwb[gi % 4].h, dwbb5[gi % 4] + 16 * (gi // 4))
                    sav = d["auxv"] if build else CW["auxv"]
                    sdr[gi % 4].inc(g.dma_start(
                        avring[gi % 4], sav[sc_i, :, rnd * 1024:(rnd + 1) * 1024]), 16)
        block.gpsimd(f5g)

        if build:
            def f5wb(s):
                for rnd in range(4):
                    for sc_i in range(SCN):
                        gi = rnd * SCN + sc_i
                        s.wait_ge(sdr[gi % 4].h, sdb5_pre[gi % 4] + 16 * (gi // 4 + 1))
                        sdwb[gi % 4].inc(s.dma_start(
                            CW["auxv"][sc_i, :, rnd * 1024:(rnd + 1) * 1024],
                            avring[gi % 4]), 16)
            block.sync(f5wb)

        def f5t(t):
            t.wait_ge(sa.h, wT_ready)
            t.wait_ge(sv.h, learned_sv)
            for rnd in range(4):
                if rnd > 0:
                    t.wait_ge(sa.h, sab5 + rnd)
                for sc_i in range(SCN):
                    gi = rnd * SCN + sc_i
                    t.wait_ge(sdr[gi % 4].h, sdb5[gi % 4] + 16 * (gi // 4 + 1))
                    for k in range(8):
                        i = t.matmul(psA1[:, k * 512:(k + 1) * 512],
                                     avring[gi % 4][:, k * 128:(k + 1) * 128],
                                     wT[:, sc_i * TILE:(sc_i + 1) * TILE],
                                     start=(sc_i == 0), stop=(sc_i == SCN - 1),
                                     skip_group_check=True)
                    sp.inc(i)
        block.tensor(f5t)
        assert sp.n == peb5 + 32

        def f5a(sc):
            for rnd in range(4):
                sc.wait_ge(sp.h, peb5 + 8 * (rnd + 1))
                sc.wait_ge(sa.h, sa.n)
                sa.inc(sc.copy(aux1T[:, rnd * 8 * TILE:(rnd + 1) * 8 * TILE],
                               psA1[:, :]))
        block.scalar(f5a)
        assert sa.n == sab5 + 4
        aux1_sa = sa.n
        pst.exit(1)

        psV = [pst.enter([128, TILE], F32), pst.enter([128, TILE], F32)]
        wrb6 = [s.n for s in swr]
        peb6 = sp.n
        svb6 = sv.n

        wbb6 = [s.n for s in swb]
        wrb6_pre = [s.n for s in swr]

        def f6g(g):
            g.wait_ge(sp.h, peb5 + 32)
            if not build:
                for k in range(2):
                    g.wait_ge(swb[k].h, wbb6[k])
            for vc in range(2):
                svd = d["wvd"] if build else CW["wvd"]
                swr[vc].inc(g.dma_start(wvd_sb[vc], svd[vc]), 16)
        block.gpsimd(f6g)

        if build:
            def f6wb(s):
                for vc in range(2):
                    s.wait_ge(swr[vc].h, wrb6_pre[vc] + 16)
                    swb[vc].inc(s.dma_start(CW["wvd"][vc], wvd_sb[vc]), 16)
            block.sync(f6wb)

        def f6t(t):
            t.wait_ge(sa.h, aux1_sa)
            for vc in range(2):
                t.wait_ge(swr[vc].h, wrb6[vc] + 16)
                for hc in range(HC):
                    i = t.matmul(psV[vc][:, :], wvd_sb[vc][:, hc * 128:(hc + 1) * 128],
                                 aux1T[:, hc * TILE:hc * TILE + TILE],
                                 start=(hc == 0), stop=(hc == HC - 1))
                sp.inc(i)
        block.tensor(f6t)
        assert sp.n == peb6 + 2

        def f6v(v):
            for vc in range(2):
                v.wait_ge(sp.h, peb6 + vc + 1)
                v.wait_ge(sv.h, sv.n)
                sv.inc(v.tensor_copy(v1T[:, vc * TILE:(vc + 1) * TILE], psV[vc][:, :]))
        block.vector(f6v)
        assert sv.n == svb6 + 2
        v1_sv = sv.n
        pst.exit(2)

        wbb6b = [swb[2].n, swb[3].n]
        wrb6b_pre = [swr[2].n, swr[3].n]

        def f6g2(g):
            g.wait_ge(sp.h, peb6 + 2)
            if not build:
                for k in range(2):
                    g.wait_ge(swb[2 + k].h, wbb6b[k])
            for vc in range(2):
                svu = d["wvu"] if build else CW["wvu"]
                swr[2 + vc].inc(g.dma_start(wvu7[vc], svu[vc]), 16)
        block.gpsimd(f6g2)
        wvu_tok = [swr[2].n, swr[3].n]

        if cc_emit is not None:
            cc_emit(a_tok)
        g_idx = tile * NSTT
        sab_gt = sa.n

        rm_tok = gd.get("rm_sv", 0)

        def fgates_a(sc):
            sc.wait_ge(sv.h, learned_sv)
            sc.wait_ge(sa.h, sa.n)
            sa.inc(sc.activation(sm1[:, 0:4], learned_sb[:, 0:4], AF.Sigmoid,
                                 bias=scal_sb[:, 2:3]))
            sc.wait_ge(sv.h, svb_gt_pred[0])
            sc.wait_ge(sa.h, sa.n)
            sa.inc(sc.activation(sm2[:, 0:4], sm2[:, 4:8], AF.Sigmoid))
        svb_gt_pred = [None]

        def fgates_v(v):
            def w():
                v.wait_ge(sv.h, sv.n)
            v.wait_ge(sv.h, rm_tok)
            v.wait_ge(sa.h, sab_gt + 1)
            w()
            sv.inc(v.tensor_scalar(sm1[:, 4:8], stL[:, g_idx:g_idx + 4],
                                   rm[rm_idx][:, 0:1], None, op0=ALU.mult))
            sv.inc(v.tensor_scalar(sm2[:, 0:4], sm1[:, 0:4], 2.5, None,
                                   op0=ALU.mult))
            w()
            sv.inc(v.scalar_tensor_tensor(sm1[:, 0:4], sm1[:, 4:8], 0.5,
                                          sm2[:, 0:4], op0=ALU.mult,
                                          op1=ALU.add))
            w()
            sv.inc(v.tensor_scalar(sm1[:, 4:8], sm1[:, 0:4], 0.0, 5.0,
                                   op0=ALU.max, op1=ALU.min))
            w()
            sv.inc(v.tensor_scalar(sm1[:, 0:4], sm1[:, 4:8], 0.5, 1.0 / 1.5,
                                   op0=ALU.subtract, op1=ALU.mult))
            w()
            sv.inc(v.tensor_scalar(sm1[:, 4:8], sm1[:, 0:4], 0.0, 1.0,
                                   op0=ALU.max, op1=ALU.min))
            w()
            sv.inc(v.tensor_scalar(sm2[:, 4:8], sm1[:, 4:8], scal_sb[:, 0:1],
                                   scal_sb[:, 1:2], op0=ALU.mult, op1=ALU.add))
            svb_gt_pred[0] = sv.n
            v.wait_ge(sa.h, sab_gt + 2)
            w()
            sv.inc(v.tensor_scalar(sm3[:, 4:8], sm2[:, 0:4], 0.05, None,
                                   op0=ALU.is_ge))
            w()
            sv.inc(v.scalar_tensor_tensor(sm3[:, 0:4], sm2[:, 0:4], 0.5,
                                          sm3[:, 4:8], op0=ALU.mult,
                                          op1=ALU.mult))
        block.vector(fgates_v)
        block.scalar(fgates_a)
        gates_sv = sv.n

        if build:
            def f6wb2(s):
                for vc in range(2):
                    s.wait_ge(swr[2 + vc].h, wrb6b_pre[vc] + 16)
                    swb[2 + vc].inc(s.dma_start(CW["wvu"][vc], wvu7[vc]), 16)
            block.sync(f6wb2)

        psYh = [pst.enter([128, 2048], F32), pst.enter([128, 2048], F32)]
        peb7 = sp.n
        svb7 = sv.n
        sqb7 = [s.n for s in sqy]
        stb7 = sst.n
        FEV = 4

        def f7s(s):
            s.wait_ge(sp.h, peb7)
            for st in range(NSTT):
                if st >= 1:
                    s.wait_ge(sv.h, svb7 + FEV * st)
                    s.wait_ge(sst.h, stb7 + 16 * 4 * st)
                r0 = tile * TILE + st * 128
                for hq in range(4):
                    sqy[hq].inc(s.dma_start(xq7[hq], src[r0:r0 + 128,
                                                         hq * 1024:(hq + 1) * 1024]), 16)
                for hq in range(4):
                    s.wait_ge(sv.h, svb7 + FEV * st + 2 * (hq // 2 + 1))
                    sst.inc(s.dma_start(dst[r0:r0 + 128, hq * 1024:(hq + 1) * 1024],
                                        xq7[hq]), 16)
        block.sync(f7s)

        def f7t(t):
            t.wait_ge(sv.h, max(v1_sv, rm_tok))
            t.wait_ge(swr[2].h, wvu_tok[0])
            t.wait_ge(swr[3].h, wvu_tok[1])
            for st in range(NSTT):
                for half in range(2):
                    if st > 0:
                        t.wait_ge(sv.h, svb7 + FEV * (st - 1) + 2 * (half + 1))
                    for vc in range(2):
                        for k in range(4):
                            kk = half * 4 + k
                            i = t.matmul(psYh[half][:, k * 512:(k + 1) * 512],
                                         v1T[:, vc * TILE + st * 128:vc * TILE + st * 128 + 128],
                                         wvu7[vc][:, kk * 512:(kk + 1) * 512],
                                         start=(vc == 0), stop=(vc == 1),
                                         skip_group_check=True)
                    sp.inc(i)
        block.tensor(f7t)
        assert sp.n == peb7 + 2 * NSTT


        def f7v(v):
            v.wait_ge(sst.h, st_entry)
            v.wait_ge(sv.h, gates_sv)
            for st in range(NSTT):
                def w():
                    v.wait_ge(sv.h, sv.n)
                gst = g_idx + st
                for half in range(2):
                    v.wait_ge(sp.h, peb7 + 2 * st + half + 1)
                    v.wait_ge(sqy[2 * half].h, sqb7[2 * half] + 16 * (st + 1))
                    v.wait_ge(sqy[2 * half + 1].h, sqb7[2 * half + 1] + 16 * (st + 1))
                    w()
                    sv.inc(v.tensor_scalar(evb7w[half], xq7w[half],
                                           stA2[:, gst:gst + 1], stB2[:, gst:gst + 1],
                                           op0=ALU.mult, op1=ALU.add))
                    w()
                    sv.inc(v.scalar_tensor_tensor(
                        xq7w[half], psYh[half][:, :],
                        sm3[:, st:st + 1], evb7w[half], op0=ALU.mult,
                        op1=ALU.add))
        block.vector(f7v)
        assert sv.n == svb7 + FEV * NSTT
        pst.exit(2)
        gd.update(pe=sp.n, sv=sv.n, sa=sa.n, st=sst.n)

    gt0, cc0a, cc0 = emit_stats(d["x0"], 0)
    last = "x0"
    if upto >= 1:
        for tile in range(NT):
            emit_ffn_tile(d["x0"], d["x1"], tile, 0, grp=gt0[tile],
                          cc_a=(cc0a if tile == 0 else None),
                          cc_emit=(cc0 if tile == 0 else None))
        last = "x1"
    if upto >= 2:
        gt1_, _, _ = emit_stats(d["x1"], None)
        for tile in range(NT):
            emit_attn_tile(d["x1"], d["x2"], tile, grp=gt1_[tile])
        last = "x2"
    if upto >= 3:
        gt2_, cc2a, cc2 = emit_stats(d["x2"], 1)
        for tile in range(NT):
            emit_ffn_tile(d["x2"], d["y"], tile, 1, grp=gt2_[tile],
                          cc_a=(cc2a if tile == 0 else None),
                          cc_emit=(cc2 if tile == 0 else None))
        last = "y"

    if last != "y":
        stn0 = sst.n
        g_pe, g_sv, g_sa = gd["pe"], gd["sv"], gd["sa"]

        def fcp(s):
            s.wait_ge(sp.h, g_pe)
            s.wait_ge(sv.h, g_sv)
            s.wait_ge(sa.h, g_sa)
            s.wait_ge(sst.h, stn0)
            for st in range(NSTG):
                if st >= 2:
                    s.wait_ge(sst.h, stn0 + 16 * (st - 1))
                sx2[st % 2].inc(s.dma_start(xld[st % 2],
                                            d[last][st * 128:(st + 1) * 128, :]), 16)
                s.wait_ge(sx2[st % 2].h, sx2[st % 2].n)
                sst.inc(s.dma_start(d["y"][st * 128:(st + 1) * 128, :], xld[st % 2]), 16)
        block.sync(fcp)

    fins = [(sst, sst.n)] + [(s, s.n) for s in
                              ([si, swq, scx] + sx2 + swr + sdr + sfr + sqy
                               + swb + sdwb + sfwb)]

    def fend(s):
        for sem, n in fins:
            s.wait_ge(sem.h, n)
    block.sync(fend)

    ctx.close()
    return nc


def prep_inputs(inp):
    f32 = np.float32

    def A(x):
        return np.ascontiguousarray(np.asarray(x, dtype=f32))

    w = {}
    w["gt"] = _blk_lhsT(A(inp["L0_attn_ffn_gate_w"]))
    w["ut"] = _blk_lhsT(A(inp["L0_attn_ffn_up_w"]))
    w["dtw"] = A(np.asarray(inp["L0_attn_ffn_down_w"], f32).T).reshape(FCN, 128, H)
    w["upw"] = _blk_lhsT(A(inp["L0_attn_up_w"]))
    w["gw"] = _blk_lhsT(A(inp["L0_attn_gate_w"]))
    w["lao"] = A(np.asarray(inp["L0_attn_local_approx_output"], f32).T).reshape(SCN, 128, H)
    w["fuse"] = A(np.transpose(np.asarray(inp["L0_attn_fuse_weight"], f32),
                               (2, 1, 0))).reshape(G, SCN, 128, H)
    w["wq"] = _blk_lhsT(A(inp["L0_ffn_Wq"]), oc=64).reshape(128, HC * 64)
    w["wu1"] = _blk_lhsT(A(inp["L0_ffn_Wu1"]))
    w["wvd"] = _blk_lhsT(A(inp["L0_ffn_Wvd"]))
    w["wvu"] = A(np.asarray(inp["L0_ffn_Wvu"], f32).T).reshape(2, 128, H)
    w["auxv"] = A(inp["L0_ffn_aux_values"]).reshape(SCN, 128, H)
    w["akt"] = A(np.asarray(inp["L0_ffn_aux_keys"], f32).T)
    w["wrt"] = A(np.asarray(inp["L0_ffn_Wr"], f32).T)
    w["lp"] = A(np.asarray(inp["L0_attn_local_point"], f32).reshape(SCN, 128).T)
    w["bu1"] = A(np.asarray(inp["L0_ffn_bu1"], f32).reshape(UCN, 128).T)
    w["wu2"] = A(np.asarray(inp["L0_ffn_Wu2"], f32).reshape(UCN, 128).T)
    w["df"] = A(np.broadcast_to(np.asarray(inp["L0_attn_discount_factor"], f32),
                                (128, G)))
    w["scal"] = A(np.broadcast_to(np.stack([
        np.float32(inp["L0_ffn_gate_w1"]), np.float32(inp["L0_ffn_gate_bias"]),
        np.asarray(inp["L0_ffn_bu2"], f32).reshape(())]).astype(f32), (128, 3)))
    w["logrel"] = A(np.broadcast_to(np.asarray(inp["L0_ffn_log_rel"], f32),
                                    (128, NSLOT)))
    x = np.asarray(inp["L0_attn_hidden_states"], f32).reshape(TCORE * NCORES, H)
    return w, x


def kernel(upto=3, debug_xs=False, **inputs):
    w, x = prep_inputs(inputs)
    nc = build_program(upto=upto, debug_xs=debug_xs)
    in_maps = []
    for c in range(NCORES):
        m = dict(w)
        m["x0"] = np.ascontiguousarray(x[c * TCORE:(c + 1) * TCORE])
        in_maps.append(m)
    res = run_bass_kernel_spmd(nc, in_maps, core_ids=list(range(NCORES)))
    out = np.concatenate([res.results[c]["y"] for c in range(NCORES)], axis=0)
    extras = None
    if debug_xs:
        extras = {k: np.concatenate([res.results[c][k] for c in range(NCORES)], axis=0)
                  for k in ("x1", "x2")}
    out = out.reshape(2, TCORE * NCORES // 2, H)
    return (out, extras) if debug_xs else out
